# version 19
# baseline (speedup 1.0000x reference)
import math
import numpy as np
from contextlib import ExitStack

import concourse.bass as bass
import concourse.mybir as mybir
from concourse.bass_utils import run_bass_kernel_spmd

F32 = mybir.dt.float32
BF16 = mybir.dt.bfloat16
AF = mybir.ActivationFunctionType
ALU = mybir.AluOpType
AX = mybir.AxisListType

T = 2048
D = 1024
CH = 8
NCORES = 8
HEADS = 8
DH = 64
INNER = 512
HID = 2730
NJ = 22
NSLOT = 44
WIN = 512

TB = 256
NB = T // TB
TS = 128
NS = TB // TS

FT = 512
NF = T // FT

NEG = -1.0e30

BUILD_STAGE = 3
DBG_STOP = None
DBG_NB = None


def _host_consts():
    c32 = {}
    segs32 = []

    def add32(name, arr):
        arr = np.asarray(arr, np.float32)
        assert arr.shape[0] == 128 or arr.ndim == 2
        c32[name] = arr
        segs32.append(name)

    add32("ones", np.ones((128, 128), np.float32))
    add32("ident", np.eye(128, dtype=np.float32))
    add32("iota", np.broadcast_to(np.arange(128, dtype=np.float32)[None, :],
                                  (128, 128)).copy())
    add32("iota1", np.broadcast_to(
        np.arange(1, 129, dtype=np.float32)[None, :], (128, 128)).copy())
    hbc = np.zeros((128, 4, 128), np.float32)
    for c in range(4):
        hbc[2 * c, c, 0:64] = 1.0
        hbc[2 * c + 1, c, 64:128] = 1.0
    add32("hbc", hbc.reshape(128, 512))

    i = np.arange(32, dtype=np.float64)
    inv_freq = 1.0 / (10000.0 ** (2.0 * i / 64.0))
    base = (2.0 * i + 0.4 * 64.0) / (1.4 * 64.0)
    t = np.arange(T, dtype=np.float64)
    ang = t[:, None] * inv_freq[None, :]
    power = (t - T // 2) / float(WIN // 2)
    scl = base[None, :] ** power[:, None]
    cos = np.cos(ang)
    sin = np.sin(ang)
    def mk(tabc, tabs):
        cs = np.concatenate([tabc, tabc], 1)
        sn = np.concatenate([-tabs, tabs], 1)
        return cs, sn
    csq, snq = mk(cos * scl * 8.0, sin * scl * 8.0)
    csk, snk = mk(cos / scl, sin / scl)
    rot = np.stack([csq, snq, csk, snk], 0)
    rot = np.broadcast_to(rot.transpose(0, 2, 1)[:, None, :, :],
                          (4, 2, 64, T)).reshape(4, 128, T)
    rot = np.ascontiguousarray(rot, np.float32)

    cbf = {}
    segsbf = []

    def addbf(name, arr):
        cbf[name] = np.asarray(arr, np.float32)
        segsbf.append(name)

    addbf("ident", np.eye(128, dtype=np.float32))
    addbf("ones1", np.ones((128, 1), np.float32))
    hsum = np.zeros((128, 4, 8), np.float32)
    for c in range(4):
        hsum[0:64, c, 2 * c] = 1.0
        hsum[64:128, c, 2 * c + 1] = 1.0
    addbf("hsum", hsum.reshape(128, 32))
    addbf("hbc", hbc.reshape(128, 512))
    a = np.arange(128)[:, None]
    j = np.arange(512)[None, :]
    maskA = np.where(j >= a, 0.0, NEG)
    addbf("maskA", maskA)
    jb = np.arange(128)[None, :]
    maskB = np.where(jb <= a, 0.0, NEG)
    addbf("maskB", maskB)
    mA0 = np.empty((128, 4, 512), np.float64)
    for qc in range(4):
        lim = 512 - 128 * qc
        mA0[:, qc, :] = np.where((j >= a) & (j >= lim), 0.0, NEG)
    addbf("maskA0", mA0.reshape(128, 2048))
    resident = ["ident", "ones1", "hsum", "hbc"]

    return c32, segs32, cbf, segsbf, rot, resident


def _pack(cdict, order):
    offs = {}
    cols = 0
    for nm in order:
        a = cdict[nm].reshape(128, -1)
        offs[nm] = (cols, a.shape[1])
        cols += a.shape[1]
    buf = np.zeros((128, cols), np.float32)
    for nm in order:
        a = cdict[nm].reshape(128, -1)
        o, w = offs[nm]
        buf[:, o:o + w] = a
    return buf, offs


class PB:
    ENGS = ("pe", "v", "s", "g", "y")

    def __init__(self, nc):
        self.nc = nc
        self.items = {e: [] for e in self.ENGS}
        self.cnt = {e: 0 for e in self.ENGS}
        self.dma_cnt = {}
        self.sem = {}
        self.dsem = {}
        self._es = None

    def init_sems(self, es):
        self._es = es
        for e in self.ENGS:
            self.sem[e] = es.enter_context(self.nc.semaphore(f"sem_{e}"))

    CHAINED = ("v", "s", "g")

    @staticmethod
    def _ckw(waits):
        for w in waits:
            assert (isinstance(w, tuple) and len(w) == 2
                    and isinstance(w[1], int)), f"bad wait token: {w!r}"

    def add(self, eng, fn, inc=False, waits=()):
        self._ckw(waits)
        if eng in self.CHAINED:
            prev = self.cnt[eng]
            w = list(waits) + [(self.sem[eng], prev)]
            self.items[eng].append((fn, True, w))
            self.cnt[eng] += 1
        else:
            self.items[eng].append((fn, bool(inc), list(waits)))
            if inc:
                self.cnt[eng] += 1
        return self.cnt[eng]

    def dma(self, q, name, out, in_, waits=()):
        if name not in self.dsem:
            self.dsem[name] = self._es.enter_context(
                self.nc.semaphore(f"dma_{name}"))
            self.dma_cnt[name] = 0

        self._ckw(waits)

        def fn(e, out=out, in_=in_):
            return e.dma_start(out=out, in_=in_)
        self.items[q].append((fn, ("dma", name), list(waits)))
        self.dma_cnt[name] += 16
        return (self.dsem[name], self.dma_cnt[name])

    def tok(self, eng):
        return (self.sem[eng], self.cnt[eng])

    def all_dma_toks(self):
        return [(self.dsem[n], self.dma_cnt[n]) for n in self.dsem]

    def emit(self, block):
        def mk_body(eng_name):
            def body(e):
                for fn, inc, waits in self.items[eng_name]:
                    for sem, val in waits:
                        e.wait_ge(sem, val)
                    instr = fn(e)
                    if inc is True:
                        instr.then_inc(self.sem[eng_name], 1)
                    elif isinstance(inc, tuple) and inc[0] == "dma":
                        instr.then_inc(self.dsem[inc[1]], 16)
            return body
        block.tensor(mk_body("pe"))
        block.vector(mk_body("v"))
        block.scalar(mk_body("s"))
        block.gpsimd(mk_body("g"))
        block.sync(mk_body("y"))


def build(nc, stage):
    dt = lambda nm, shp, k=None: (nc.dram_tensor(nm, shp, F32, kind=k)
                                  if k else nc.dram_tensor(nm, shp, F32)).ap()
    x_h = dt("x", [T, D], "ExternalInput")
    out_h = dt("out", [T, D], "ExternalOutput")
    c32_np, segs32, cbf_np, segsbf, rot_np, cbf_res = _host_consts()
    c32_buf, c32_off = _pack(c32_np, segs32)
    cbf_buf, cbf_off = _pack(cbf_np, segsbf)
    smallw = dt("smallw", [128, SMALL_COLS], "ExternalInput")
    cst32_h = dt("cst32", list(c32_buf.shape), "ExternalInput")
    cstbf_h = dt("cstbf", list(cbf_buf.shape), "ExternalInput")
    rot_h = dt("rot", [4, 128, T], "ExternalInput")
    wkT_h = dt("wkT", [D, D], "ExternalInput")
    wvT_h = dt("wvT", [D, D], "ExternalInput")
    wrT_h = dt("wrT", [D, D], "ExternalInput")
    woT_h = dt("woT", [D, D], "ExternalInput")
    wqkv_h = dt("wqkv", [2, D, 1536], "ExternalInput")
    wqsw_h = dt("wqsw", [2, D, 1024], "ExternalInput")
    wof_h = dt("wof", [2, INNER, D], "ExternalInput")
    w1i_h = dt("w1i", [2, D, NSLOT * 128], "ExternalInput")
    w2p_h = dt("w2p", [2, NJ * 128, D], "ExternalInput")
    xsp_h = nc.dram_tensor("xsp", [128, CH, T], BF16).ap()
    ysp_h = nc.dram_tensor("ysp", [128, CH, T], BF16).ap()
    import os
    dbg_dump = bool(os.environ.get("K_DUMP"))
    if dbg_dump:
        dby_h = {}
        for tag in ("0", "a0", "b0", "a1", "b1"):
            dby_h[tag] = dt(f"dby{tag}", [128, CH, T], "ExternalOutput")
        dbhb_h = dt("dbhb", [128, CH, FT], "ExternalOutput")
        dbha_h = dt("dbha", [128, NJ, FT], "ExternalOutput")
        dbw1_h = dt("dbw1", [128, CH, NSLOT * 128], "ExternalOutput")

    p = PB(nc)
    es = ExitStack()
    with es:
        p.init_sems(es)
        sb = lambda nm, shp, dty: es.enter_context(nc.sbuf_tensor(nm, shp, dty))
        front_es = ExitStack()
        fsb = lambda nm, shp, dty: front_es.enter_context(
            nc.sbuf_tensor(nm, shp, dty))
        C32 = sb("C32", list(c32_buf.shape), F32)
        ncres = sum(cbf_off[nm][1] for nm in cbf_res)
        CBF = sb("CBF", [128, ncres], BF16)
        SMW = sb("SMW", [128, SMALL_COLS], F32)
        RRF = sb("RRF", [128, 2048], F32)
        XF = fsb("XF", [128, CH, T], BF16)

        def c32(nm, r0=0, r1=128):
            o, w = c32_off[nm]
            return C32[r0:r1, o:o + w]

        def cbf(nm, r0=0, r1=128):
            o, w = cbf_off[nm]
            assert nm in cbf_res and o + w <= ncres
            return CBF[r0:r1, o:o + w]

        def cbf_h(nm):
            o, w = cbf_off[nm]
            return cstbf_h[:, o:o + w]

        def smw(nm):
            o, w = SMALL_OFF[nm]
            return SMW[:, o:o + w]

        t_c32 = p.dma("y", "cst", C32[:], cst32_h)
        t_smw = p.dma("y", "smw", SMW[:], smallw)
        t_cbf = p.dma("g", "cbf", CBF[:], cstbf_h[:, 0:ncres])

        s0_ev = [None] * 16
        with nc.sbuf_tensor("XTOK", [128, 16, D], BF16) as XTOK, \
             nc.psum_tensor("PT0", [128, 512], BF16) as PT0, \
             nc.psum_tensor("PT1", [128, 512], BF16) as PT1:
            t_x = p.dma("g", "x", XTOK[:],
                        x_h.rearrange("(tc p) c -> p tc c", p=128))
            wx = [t_x, t_cbf]
            pev = 0
            for tc in range(16):
                for half, PT in ((0, PT0), (1, PT1)):
                    for cc in range(4):
                        c = half * 4 + cc
                        def tr(e, tc=tc, c=c, cc=cc, PT=PT):
                            return e.transpose(
                                PT[:, cc * 128:(cc + 1) * 128],
                                XTOK[:, tc, c * 128:(c + 1) * 128],
                                cbf("ident"))
                        w = list(wx) if (tc == 0 and half == 0 and cc == 0) else []
                        if tc > 0 and cc == 0:
                            w.append((p.sem["v"], pev - 1))
                        p.add("pe", tr, inc=(cc == 3), waits=w)
                    tpe = p.tok("pe")
                    def ev(e, tc=tc, half=half, PT=PT):
                        return e.tensor_copy(
                            XF[:, half * 4:half * 4 + 4, tc * 128:tc * 128 + 128],
                            PT[:].rearrange("p (c n) -> p c n", c=4))
                    pev = p.add("v", ev, inc=True, waits=[tpe])
                s0_ev[tc] = p.tok("v")

        t_s0pe = p.tok("pe")
        WK = fsb("WK", [128, CH, D], BF16)
        WV = fsb("WV", [128, CH, D], BF16)
        WR = fsb("WR", [128, CH, D], BF16)
        WO = fsb("WO", [128, CH, D], BF16)
        p.dma("g", "wr4", WK[:], wkT_h.rearrange("(c p) o -> p c o", p=128),
              waits=[t_s0pe])
        p.dma("g", "wr4", WV[:], wvT_h.rearrange("(c p) o -> p c o", p=128))
        p.dma("g", "wr4", WR[:], wrT_h.rearrange("(c p) o -> p c o", p=128))
        t_wr4 = p.dma("g", "wr4", WO[:],
                      woT_h.rearrange("(c p) o -> p c o", p=128))

        rwkv_scope = ExitStack()
        with rwkv_scope as rs:
            rsb = lambda nm, shp, dty: rs.enter_context(
                nc.sbuf_tensor(nm, shp, dty))
            H0 = rsb("H0", [128, CH, TB + 1], BF16)
            XKk = rsb("XKk", [128, CH, TB], BF16)
            XKv = rsb("XKv", [128, CH, TB], BF16)
            WKVR = rsb("WKVR", [128, 2, CH, TB], BF16)
            RG2 = rsb("RG2", [128, 2, CH, TB], BF16)
            KF = rsb("KF", [128, CH, TB], F32)
            XC = rsb("XC", [128, CH, TB], F32)
            VFB = rsb("VFB", [128, CH, TB], BF16)
            M2 = rsb("M2", [128, 2, CH, TS + 1], F32)
            A2 = rsb("A2", [128, 2, CH, TS + 1], BF16)
            B2 = rsb("B2", [128, 2, CH, TS + 1], BF16)
            S1B = rsb("S1B", [128, CH, TS], F32)
            SAB = rsb("SAB", [128, 2, CH, TS], F32)
            E12 = rsb("E12", [128, 2, CH, TS], BF16)
            N1B = rsb("N1B", [128, CH, TS], BF16)
            DEN = rsb("DEN", [128, CH, TS], BF16)
            RCB = rsb("RCB", [128, CH, TS], BF16)
            TWU = rsb("TWU", [128, CH, TS], F32)
            TWZ = rsb("TWZ", [128, CH, TS], F32)
            SMX = rsb("SMX", [128, CH], F32)
            PBS = [rs.enter_context(nc.psum_tensor(f"pb{i}", [128, 512], F32))
                   for i in range(8)]

            murow = RRF[0:1, 0:TB]
            rstdrow = RRF[0:1, 256:256 + TB]
            scrap = RRF[0:1, 512:512 + TB]
            sdrow = RRF[0:1, 768:768 + TB]

            wsm = [t_smw, t_c32]
            def f0(e):
                return e.tensor_tensor(SMX[:], smw("w"), smw("u"), ALU.add)
            p.add("v", f0, waits=wsm)
            for c in range(CH):
                def f1(e, c=c):
                    return e.tensor_scalar(TWU[:, c, :], c32("iota"),
                                           smw("w")[:, c:c + 1], None, ALU.mult)
                p.add("v", f1)
            for c in range(CH):
                def f2(e, c=c):
                    return e.tensor_scalar(TWU[:, c, :], TWU[:, c, :],
                                           smw("u")[:, c:c + 1], None,
                                           ALU.subtract)
                p.add("v", f2)
            for c in range(CH):
                def f3z(e, c=c):
                    return e.tensor_scalar(TWZ[:, c, :], TWU[:, c, :],
                                           SMX[:, c:c + 1], None, ALU.add)
                p.add("v", f3z)

            def col2d4(buf, par, i):
                return buf[:, par, :, i:i + 1].rearrange("p c n -> p (c n)")

            def fill(dst, val):
                def f(e, dst=dst, val=val):
                    return e.tensor_scalar(dst, smw("w"), 0.0, float(val),
                                           ALU.mult, ALU.add)
                return f
            p.add("v", fill(col2d4(M2, 0, 0), -1.0e38))
            p.add("v", fill(col2d4(A2, 0, 0), 0.0))
            p.add("v", fill(col2d4(B2, 0, 0), 0.0))
            p.add("v", fill(H0[:, :, 0:1].rearrange("p c n -> p (c n)"), 0.0),
                  inc=True)

            ST = {"pend": None, "sg": 0, "t_exps": None, "t_vnum": None,
                  "t_gb": None, "t_vnum2": None,
                  "t_vwk": None, "t_vwk_s1": None, "t_wkvr_done": None,
                  "t_ln2v": p.tok("v"), "t_ln2pe": None,
                  "t_kw": None, "t_vw": None, "t_rw": None,
                  "t_kev": None, "t_vev": None, "t_revs": None,
                  "t_aev": None, "t_aevs": None}

            def emit_wkv(pd):
                def wkv(e):
                    return e.tensor_tensor(N1B[:], N1B[:], RCB[:], ALU.mult)
                w = [pd["t_sdn"]]
                if ST["t_ln2pe"] is not None:
                    w.append(ST["t_ln2pe"])
                p.add("v", wkv, waits=w)
                bp = pd["b"] & 1
                def wkr(e, bp=bp, wsl=pd["wsl"]):
                    return e.tensor_tensor(WKVR[:, bp, :, wsl], N1B[:],
                                           RG2[:, bp, :, wsl], ALU.mult)
                p.add("v", wkr, inc=True)
                ST["t_vwk"] = p.tok("v")
                if pd["sblk"] == 1:
                    ST["t_vwk_s1"] = ST["t_vwk"]
                    ST["t_wkvr_done"] = ST["t_vwk"]

            def emit_slot(b, sblk):
                sg = ST["sg"]
                par = sg & 1
                op = 1 - par
                wsl = slice(sblk * TS, sblk * TS + TS)
                pend = ST["pend"]
                if sg > 0:
                    def mcar(e, par=par, op=op):
                        return e.scalar_tensor_tensor(
                            col2d4(M2, par, 0), smw("w"), float(TS),
                            col2d4(M2, op, TS), ALU.mult, ALU.add)
                    p.add("v", mcar)
                    def acar(e, par=par, op=op):
                        return e.tensor_copy(col2d4(A2, par, 0),
                                             col2d4(A2, op, TS))
                    p.add("v", acar)
                    def bcar(e, par=par, op=op):
                        return e.tensor_copy(col2d4(B2, par, 0),
                                             col2d4(B2, op, TS))
                    p.add("v", bcar)
                wz = [pend["t_sdn"]] if pend is not None else []
                if sblk == 0:
                    wz.append(ST["t_kev"])
                def zf(e, wsl=wsl):
                    return e.tensor_tensor(S1B[:], KF[:, :, wsl], TWZ[:],
                                           ALU.subtract)
                p.add("v", zf, waits=wz)
                for c in range(CH):
                    def cm(e, c=c, par=par):
                        return e.tensor_tensor_scan(
                            M2[:, par, c, 1:TS + 1], S1B[:, c, :],
                            S1B[:, c, :], M2[:, par, c, 0:1],
                            ALU.max, ALU.max)
                    p.add("v", cm)
                wa = []
                if ST["t_exps"] is not None:
                    wa.append(ST["t_exps"])
                if ST["t_gb"] is not None:
                    wa.append(ST["t_gb"])
                def asub(e, par=par):
                    return e.tensor_tensor(SAB[:, 0], M2[:, par, :, 0:TS],
                                           M2[:, par, :, 1:TS + 1],
                                           ALU.subtract)
                p.add("v", asub, waits=wa)
                def bsub(e, par=par):
                    return e.tensor_tensor(SAB[:, 1], S1B[:],
                                           M2[:, par, :, 1:TS + 1],
                                           ALU.subtract)
                p.add("v", bsub, inc=True)
                t_subs = p.tok("v")
                def dd1(e, wsl=wsl, par=par):
                    return e.tensor_tensor(S1B[:], KF[:, :, wsl],
                                           M2[:, par, :, 0:TS], ALU.subtract)
                p.add("v", dd1)
                def dd2(e):
                    return e.tensor_tensor(S1B[:], S1B[:], TWU[:],
                                           ALU.subtract)
                p.add("v", dd2, inc=True)
                t_vD = p.tok("v")
                if pend is not None:
                    emit_wkv(pend)
                def exps(e):
                    return e.activation(SAB[:], SAB[:], AF.Exp)
                p.add("s", exps, inc=True, waits=[t_subs])
                ST["t_exps"] = p.tok("s")
                wr1 = [t_vD]
                if ST["t_vnum"] is not None:
                    wr1.append(ST["t_vnum"])
                def rl1(e):
                    return e.activation(E12[:, 0], S1B[:], AF.Relu)
                p.add("s", rl1, waits=wr1)
                def rl2(e):
                    return e.activation(E12[:, 1], S1B[:], AF.Relu, scale=-1.0)
                p.add("s", rl2)
                def eexp(e):
                    return e.activation(E12[:], E12[:], AF.Exp, scale=-1.0)
                p.add("s", eexp, inc=True)
                t_sE = p.tok("s")
                for c in range(CH):
                    def sb_(e, c=c, par=par):
                        return e.tensor_tensor_scan(
                            B2[:, par, c, 1:TS + 1], SAB[:, 0, c, :],
                            SAB[:, 1, c, :], B2[:, par, c, 0:1],
                            ALU.mult, ALU.add)
                    p.add("v", sb_,
                          waits=([ST["t_exps"]] if c == 0 else ()))
                def bv(e, wsl=wsl):
                    return e.tensor_tensor(SAB[:, 1], SAB[:, 1],
                                           VFB[:, :, wsl], ALU.mult)
                p.add("v", bv)
                for c in range(CH):
                    def sa_(e, c=c, par=par):
                        return e.tensor_tensor_scan(
                            A2[:, par, c, 1:TS + 1], SAB[:, 0, c, :],
                            SAB[:, 1, c, :], A2[:, par, c, 0:1],
                            ALU.mult, ALU.add)
                    p.add("v", sa_)
                def d1(e, par=par):
                    return e.tensor_tensor(DEN[:], E12[:, 0],
                                           B2[:, par, :, 0:TS], ALU.mult)
                p.add("v", d1, waits=[t_sE])
                def d2(e):
                    return e.tensor_tensor(DEN[:], DEN[:], E12[:, 1], ALU.add)
                p.add("v", d2, inc=True)
                t_vden = p.tok("v")
                def t2(e, par=par):
                    return e.tensor_tensor(N1B[:], E12[:, 0],
                                           A2[:, par, :, 0:TS], ALU.mult)
                p.add("v", t2)
                def t3(e, wsl=wsl):
                    return e.tensor_tensor(E12[:, 1], E12[:, 1],
                                           VFB[:, :, wsl], ALU.mult)
                p.add("v", t3)
                def t4(e):
                    return e.tensor_tensor(N1B[:], N1B[:], E12[:, 1], ALU.add)
                p.add("v", t4, inc=True)
                ST["t_vnum"] = p.tok("v")
                ST["t_vnum2"] = ST["t_vnum"]
                wln = [t_vden]
                if ST["t_vwk"] is not None:
                    wln.append(ST["t_vwk"])
                def dln(e):
                    return e.activation(S1B[:], DEN[:], AF.Ln,
                                        bias=smw("e9")[:, 0:1])
                p.add("s", dln, waits=wln)
                def dex(e):
                    return e.activation(RCB[:], S1B[:], AF.Exp, scale=-1.0)
                p.add("s", dex, inc=True)
                t_sdn = p.tok("s")
                ST["pend"] = {"b": b, "sblk": sblk, "wsl": wsl, "par": par,
                              "t_sE": t_sE, "t_sdn": t_sdn}
                ST["sg"] = sg + 1

            def emit_ln1_mixk_kwave(b):
                col = b * TB
                xfb = XF[:, :, col:col + TB]
                if b > 0:
                    def hcar(e):
                        return e.tensor_copy(H0[:, :, 0:1], H0[:, :, TB:TB + 1])
                    p.add("v", hcar)
                ws = [s0_ev[2 * b + 1], ST["t_ln2v"]]
                def sqr(e, xfb=xfb):
                    return e.activation(XKk[:], xfb, AF.Square)
                p.add("s", sqr, inc=True, waits=ws)
                t_sqr = p.tok("s")
                for c in range(CH):
                    def m0(e, c=c, xfb=xfb):
                        return e.matmul(PBS[6][0:1, 0:TB], cbf("ones1"),
                                        xfb[:, c, :], start=(c == 0),
                                        stop=(c == CH - 1))
                    w = []
                    if c == 0:
                        w = [s0_ev[2 * b + 1], ST["t_ln2v"], t_cbf]
                        if ST["t_revs"] is not None:
                            w.append(ST["t_revs"])
                        if ST["t_aevs"] is not None:
                            w.append(ST["t_aevs"])
                    p.add("pe", m0, waits=w)
                for c in range(CH):
                    def m1(e, c=c):
                        return e.matmul(PBS[7][0:1, 0:TB], cbf("ones1"),
                                        XKk[:, c, :], start=(c == 0),
                                        stop=(c == CH - 1))
                    p.add("pe", m1, inc=(c == CH - 1),
                          waits=([t_sqr] if c == 0 else ()))
                t_stats = p.tok("pe")
                def r1(e):
                    return e.tensor_scalar(murow, PBS[6][0:1, 0:TB],
                                           1.0 / D, None, ALU.mult)
                p.add("v", r1, waits=[t_stats])
                def r2(e):
                    return e.tensor_tensor(scrap, murow, murow, ALU.mult)
                p.add("v", r2)
                def r3(e):
                    return e.scalar_tensor_tensor(sdrow, PBS[7][0:1, 0:TB],
                                                  1.0 / D, scrap, ALU.mult,
                                                  ALU.subtract)
                p.add("v", r3, inc=True)
                t_var = p.tok("v")
                def r4(e):
                    return e.activation(sdrow, sdrow, AF.Ln,
                                        bias=smw("eps")[0:1, :])
                p.add("s", r4, waits=[t_var])
                def r6(e):
                    return e.activation(rstdrow, sdrow, AF.Exp, scale=-0.5)
                p.add("s", r6, inc=True)
                t_rows = p.tok("s")
                def b1(e):
                    return e.matmul(PBS[6][:, 0:TB], c32("ones", 0, 1), murow,
                                    start=True, stop=True)
                p.add("pe", b1, waits=[t_rows])
                def b2(e):
                    return e.matmul(PBS[7][:, 0:TB], c32("ones", 0, 1),
                                    rstdrow, start=True, stop=True)
                p.add("pe", b2, inc=True)
                t_bc = p.tok("pe")
                for c in range(CH):
                    def a1(e, c=c, xfb=xfb):
                        return e.tensor_tensor(XC[:, c, :], xfb[:, c, :],
                                               PBS[6][:, 0:TB], ALU.subtract)
                    p.add("v", a1, waits=([t_bc] if c == 0 else ()))
                for c in range(CH):
                    def a2(e, c=c):
                        return e.tensor_tensor(H0[:, c, 1:TB + 1], XC[:, c, :],
                                               PBS[7][:, 0:TB], ALU.mult)
                    p.add("v", a2)
                def dxf(e):
                    return e.tensor_tensor(XC[:], H0[:, :, 1:TB + 1],
                                           H0[:, :, 0:TB], ALU.subtract)
                p.add("v", dxf)
                def pslot(bank0, oc):
                    return PBS[bank0 + oc // 2][:, (oc % 2) * TB:
                                                (oc % 2) * TB + TB]

                def mixwave(W, mixnm, dst, bank0, t_gate, mx_wait=()):
                    for c in range(CH):
                        def mx(e, c=c, dst=dst, mixnm=mixnm):
                            return e.scalar_tensor_tensor(
                                dst[:, c, :], XC[:, c, :],
                                smw(mixnm)[:, c:c + 1], H0[:, c, 0:TB],
                                ALU.mult, ALU.add)
                        p.add("v", mx, inc=(c == CH - 1),
                              waits=(mx_wait if c == 0 else ()))
                    t_mix = p.tok("v")
                    for oc in range(CH):
                        for kc in range(CH):
                            def mm(e, oc=oc, kc=kc, W=W, dst=dst, bank0=bank0):
                                return e.matmul(
                                    pslot(bank0, oc),
                                    W[:, kc, oc * 128:oc * 128 + 128],
                                    dst[:, kc, :], start=(kc == 0),
                                    stop=(kc == CH - 1))
                            w = []
                            if oc == 0 and kc == 0:
                                w = [t_mix, t_wr4]
                                if t_gate is not None:
                                    w.append(t_gate)
                            p.add("pe", mm,
                                  inc=(oc == CH - 1 and kc == CH - 1),
                                  waits=w)
                    return p.tok("pe")
                ST["mixwave"] = mixwave
                ST["pslot"] = pslot
                ST["t_kw"] = mixwave(WK, "tmk", XKk, 0, None)
                ST["t_vw"] = mixwave(WV, "tmv", XKv, 4, None)

            def emit_evkv(b):
                pslot = ST["pslot"]
                for oc in range(CH):
                    def evv(e, oc=oc):
                        return e.activation(VFB[:, oc, :], pslot(4, oc),
                                            AF.Copy, bias=0.0)
                    w = []
                    if oc == 0:
                        w = [ST["t_vw"]]
                        if ST["t_vnum"] is not None:
                            w.append(ST["t_vnum"])
                    p.add("s", evv, inc=(oc == CH - 1), waits=w)
                ST["t_vev"] = p.tok("s")
                for oc in range(CH):
                    def evk(e, oc=oc):
                        return e.activation(KF[:, oc, :], pslot(0, oc),
                                            AF.Copy, bias=0.0)
                    p.add("s", evk, inc=(oc == CH - 1),
                          waits=([ST["t_kw"]] if oc == 0 else ()))
                ST["t_kev"] = p.tok("s")

            def emit_rwave(b):
                pslot = ST["pslot"]
                ST["t_rw"] = ST["mixwave"](WR, "tmr", XKk, 0, ST["t_kev"],
                                           mx_wait=[ST["t_kw"]])
                bp = b & 1
                for oc in range(CH):
                    def evr(e, oc=oc, bp=bp):
                        return e.activation(RG2[:, bp, oc, :], pslot(0, oc),
                                            AF.Tanh, scale=0.5)
                    w = []
                    if oc == 0:
                        w = [ST["t_rw"]]
                        if ST["t_vwk_s1"] is not None:
                            w.append(ST["t_vwk_s1"])
                    p.add("s", evr, inc=(oc == CH - 1), waits=w)
                ST["t_revs"] = p.tok("s")
                def rresc(e, bp=bp):
                    return e.tensor_scalar(RG2[:, bp], RG2[:, bp], 0.5, 0.5,
                                           ALU.mult, ALU.add)
                p.add("v", rresc, inc=True, waits=[ST["t_revs"]])

            def emit_wo_ln2(bb):
                bp = bb & 1
                col = bb * TB
                xfb = XF[:, :, col:col + TB]
                pslot = ST["pslot"]
                for oc in range(CH):
                    for kc in range(CH):
                        def mo(e, oc=oc, kc=kc, bp=bp):
                            return e.matmul(
                                pslot(4, oc),
                                WO[:, kc, oc * 128:oc * 128 + 128],
                                WKVR[:, bp, kc, :], start=(kc == 0),
                                stop=(kc == CH - 1))
                        w = []
                        if oc == 0 and kc == 0:
                            w = [ST["t_wkvr_done"], ST["t_vev"]]
                            if ST["t_aev"] is not None:
                                w += [ST["t_aev"], ST["t_aevs"]]
                        p.add("pe", mo, inc=(oc == CH - 1 and kc == CH - 1),
                              waits=w)
                t_ow = p.tok("pe")
                for bk in range(4):
                    tv_bank = None
                    for half in range(2):
                        oc = bk * 2 + half
                        def ea1(e, oc=oc):
                            return e.tensor_copy(XC[:, oc, :], pslot(4, oc))
                        p.add("v", ea1,
                              waits=([t_ow] if oc == 0 else ()))
                        def ea2(e, oc=oc, bp=bp):
                            return e.tensor_copy(WKVR[:, bp, oc, :],
                                                 pslot(4, oc))
                        tv_bank = p.add("v", ea2, inc=(oc == CH - 1))
                    for half in range(2):
                        oc = bk * 2 + half
                        def ea3(e, oc=oc):
                            return e.activation(XKk[:, oc, :], pslot(4, oc),
                                                AF.Square)
                        p.add("s", ea3, inc=(oc == CH - 1),
                              waits=([(p.sem["v"], tv_bank)]
                                     if half == 0 else ()))
                ST["t_aev"] = p.tok("v")
                ST["t_aevs"] = p.tok("s")
                for c in range(CH):
                    def m0(e, c=c, bp=bp):
                        return e.matmul(PBS[6][0:1, 0:TB], cbf("ones1"),
                                        WKVR[:, bp, c, :], start=(c == 0),
                                        stop=(c == CH - 1))
                    p.add("pe", m0,
                          waits=([ST["t_aev"], ST["t_aevs"]] if c == 0 else ()))
                for c in range(CH):
                    def m1(e, c=c):
                        return e.matmul(PBS[7][0:1, 0:TB], cbf("ones1"),
                                        XKk[:, c, :], start=(c == 0),
                                        stop=(c == CH - 1))
                    p.add("pe", m1, inc=(c == CH - 1))
                t_st2 = p.tok("pe")
                def q1(e):
                    return e.tensor_scalar(murow, PBS[6][0:1, 0:TB], 1.0 / D,
                                           None, ALU.mult)
                p.add("v", q1, waits=[t_st2])
                def q2(e):
                    return e.tensor_tensor(scrap, murow, murow, ALU.mult)
                p.add("v", q2)
                def q3(e):
                    return e.scalar_tensor_tensor(sdrow, PBS[7][0:1, 0:TB],
                                                  1.0 / D, scrap, ALU.mult,
                                                  ALU.subtract)
                p.add("v", q3, inc=True)
                t_v2 = p.tok("v")
                def q4(e):
                    return e.activation(sdrow, sdrow, AF.Ln,
                                        bias=smw("eps")[0:1, :])
                p.add("s", q4, waits=[t_v2])
                def q6(e):
                    return e.activation(rstdrow, sdrow, AF.Exp, scale=-0.5)
                p.add("s", q6, inc=True)
                t_rws = p.tok("s")
                def bb1(e):
                    return e.matmul(PBS[6][:, 0:TB], c32("ones", 0, 1), murow,
                                    start=True, stop=True)
                p.add("pe", bb1, waits=[t_rws])
                def bb2(e):
                    return e.matmul(PBS[7][:, 0:TB], c32("ones", 0, 1),
                                    rstdrow, start=True, stop=True)
                p.add("pe", bb2, inc=True)
                ST["t_ln2pe"] = p.tok("pe")
                for c in range(CH):
                    def g1(e, c=c):
                        return e.tensor_tensor(XC[:, c, :], XC[:, c, :],
                                               PBS[6][:, 0:TB], ALU.subtract)
                    p.add("v", g1,
                          waits=([ST["t_ln2pe"]] if c == 0 else ()))
                for c in range(CH):
                    def g2(e, c=c):
                        return e.tensor_tensor(XKk[:, c, :], XC[:, c, :],
                                               PBS[7][:, 0:TB], ALU.mult)
                    p.add("v", g2)
                for c in range(CH):
                    def g3(e, c=c, xfb=xfb):
                        return e.tensor_tensor(xfb[:, c, :], xfb[:, c, :],
                                               XKk[:, c, :], ALU.add)
                    p.add("v", g3, inc=(c == CH - 1))
                ST["t_ln2v"] = p.tok("v")

            nb_run = DBG_NB if DBG_NB is not None else NB
            for it in range(nb_run + 2):
                if it < nb_run:
                    emit_ln1_mixk_kwave(it)
                if 1 <= it <= nb_run:
                    emit_slot(it - 1, 0)
                    emit_slot(it - 1, 1)
                if it == nb_run + 1 and ST["pend"] is not None:
                    emit_wkv(ST["pend"])
                    ST["pend"] = None
                if it < nb_run:
                    emit_evkv(it)
                    emit_rwave(it)
                if it >= 2 and it - 2 < nb_run:
                    emit_wo_ln2(it - 2)
            t_rwkv_done = ST["t_ln2v"]

            t_xsp = p.dma("y", "xsp", xsp_h, XF[:], waits=[t_rwkv_done])
            t_y0 = None
            v_apply = ST["t_ln2v"]
            t_st3 = None
            t_ap3 = None
            if stage >= 2:
                for b in range(NB):
                    col = b * TB
                    xfb = XF[:, :, col:col + TB]
                    ws3 = [v_apply] if b == 0 else [t_st3]
                    def sqr3(e, xfb=xfb):
                        return e.activation(XKk[:], xfb, AF.Square)
                    p.add("s", sqr3, inc=True, waits=ws3)
                    t_x2 = p.tok("s")
                    for c in range(CH):
                        def m0(e, c=c, xfb=xfb):
                            return e.matmul(PBS[6][0:1, 0:TB], cbf("ones1"),
                                            xfb[:, c, :], start=(c == 0),
                                            stop=(c == CH - 1))
                        w = []
                        if b == 0 and c == 0:
                            w = [v_apply, ST["t_aevs"]]
                        elif c == 0:
                            w = [t_ap3]
                        p.add("pe", m0, waits=w)
                    for c in range(CH):
                        def m1(e, c=c):
                            return e.matmul(PBS[7][0:1, 0:TB], cbf("ones1"),
                                            XKk[:, c, :], start=(c == 0),
                                            stop=(c == CH - 1))
                        p.add("pe", m1, inc=(c == CH - 1),
                              waits=([t_x2] if c == 0 else ()))
                    t_st3 = p.tok("pe")
                    def r1(e):
                        return e.tensor_scalar(murow, PBS[6][0:1, 0:TB],
                                               1.0 / D, None, ALU.mult)
                    p.add("v", r1, waits=[t_st3])
                    def r2(e):
                        return e.tensor_tensor(scrap, murow, murow, ALU.mult)
                    p.add("v", r2)
                    def r3(e):
                        return e.scalar_tensor_tensor(sdrow, PBS[7][0:1, 0:TB],
                                                      1.0 / D, scrap, ALU.mult,
                                                      ALU.subtract)
                    p.add("v", r3, inc=True)
                    t_var = p.tok("v")
                    def r4(e):
                        return e.activation(sdrow, sdrow, AF.Ln,
                                            bias=smw("eps")[0:1, :])
                    p.add("s", r4, waits=[t_var])
                    def r6(e):
                        return e.activation(rstdrow, sdrow, AF.Exp, scale=-0.5)
                    p.add("s", r6, inc=True)
                    t_rows3 = p.tok("s")
                    def b1(e):
                        return e.matmul(PBS[6][:, 0:TB], c32("ones", 0, 1),
                                        murow, start=True, stop=True)
                    p.add("pe", b1, waits=[t_rows3])
                    def b2(e):
                        return e.matmul(PBS[7][:, 0:TB], c32("ones", 0, 1),
                                        rstdrow, start=True, stop=True)
                    p.add("pe", b2, inc=True)
                    t_bc3 = p.tok("pe")
                    for c in range(CH):
                        def a1(e, c=c, xfb=xfb):
                            return e.tensor_tensor(XC[:, c, :], xfb[:, c, :],
                                                   PBS[6][:, 0:TB],
                                                   ALU.subtract)
                        p.add("v", a1, waits=([t_bc3] if c == 0 else ()))
                    for c in range(CH):
                        def a2(e, c=c):
                            return e.tensor_tensor(XKv[:, c, :], XC[:, c, :],
                                                   PBS[7][:, 0:TB], ALU.mult)
                        w = []
                        if c == 0 and t_y0 is not None:
                            w.append(t_y0)
                        p.add("v", a2, waits=w)
                    t_ap3 = p.tok("v")
                    t_y0 = p.dma("y", "ysp0",
                                 ysp_h[:, :, col:col + TB], XKv[:],
                                 waits=[t_ap3])
            t_front = [t_xsp, p.tok("v"), p.tok("s"), p.tok("pe")]
            if t_y0 is not None:
                t_front.append(t_y0)
            if dbg_dump and t_y0 is not None:
                t_front.append(p.dma("y", "dby", dby_h["0"], ysp_h,
                                     waits=[t_y0]))

        front_es.close()


        nlayers = 0 if stage < 2 else (1 if stage == 2 else 2)
        t_prev_writes = (p.dsem.get("ysp0"), p.dma_cnt.get("ysp0")) \
            if nlayers else None
        YB2 = sb("YB2", [128, 2, CH, FT], BF16)
        YLC = sb("YLC", [128, CH, 2], BF16)

        def col2d2(buf, i):
            return buf[:, :, i:i + 1].rearrange("p c n -> p (c n)")

        def emit_layer(l, t_prev_writes, t_front, t_gate_prev):
            pa = ExitStack()
            asb = lambda nm, shp, dty: pa.enter_context(
                nc.sbuf_tensor(nm + f"_l{l}", shp, dty))
            WQKV = asb("WQKV", [128, CH, 1536], BF16)
            WQSW = asb("WQSW", [128, CH, 1024], BF16)
            WOF = asb("WOF", [64, CH, 1024], BF16)
            MSK = asb("MSK", [128, 512 + 128 + 2048], BF16)
            HQ = asb("HQ", [128, CH, FT], BF16)
            KR = asb("KR", [128, 4, 1024], BF16)
            VR = asb("VR", [128, 8, 512], BF16)
            QH = asb("QH", [128, 4, FT], BF16)
            QA = asb("QA", [128, 4, FT], F32)
            QB2 = asb("QB2", [128, 4, FT], F32)
            RT = asb("RT", [128, 4, FT], F32)
            SQB = asb("SQB", [128, 4, FT], BF16)
            PRB = asb("PRB", [128, 2, 640], BF16)
            PTB = asb("PTB", [128, 2, 640], BF16)
            RRB = asb("RRB", [8, 1024], BF16)
            OH = asb("OH", [64, CH, FT], BF16)
            PA = [pa.enter_context(
                nc.psum_tensor(f"pa{i}_l{l}", [128, 512], F32))
                for i in range(4)]
            BROW = pa.enter_context(
                nc.psum_tensor(f"brow_l{l}", [128, 512], F32))
            BSA = pa.enter_context(
                nc.psum_tensor(f"bsa_l{l}", [128, 512], F32))
            PBF2 = [pa.enter_context(
                nc.psum_tensor(f"pbf{i}_l{l}", [128, 640], BF16))
                for i in range(2)]

            gate = (list(t_front) if l == 0 else list(t_gate_prev)) \
                + p.all_dma_toks()
            t_wfa = p.dma("g", f"wfa{l}", WQKV[:],
                          wqkv_h[l].rearrange("(c p) o -> p c o", p=128),
                          waits=gate)
            p.dma("g", f"wfa{l}", WQSW[:],
                  wqsw_h[l].rearrange("(c p) o -> p c o", p=128))
            p.dma("g", f"wfa{l}", WOF[:],
                  wof_h[l].rearrange("(kc p) o -> p kc o", p=64))
            m_w = 512 + 128 + 2048
            def mskload():
                o0, _ = cbf_off["maskA"]
                return cstbf_h[:, o0:o0 + m_w]
            t_wfa = p.dma("g", f"wfa{l}", MSK[:], mskload())
            mA_off, mB_off, mA0_off = 0, 512, 640

            def zKR(e):
                return e.memset(KR[:, :, 0:512], 0.0)
            p.add("g", zKR, waits=gate)
            def zVR(e):
                return e.memset(VR[:, 0:4, :], 0.0)
            p.add("g", zVR)
            t_ring0 = p.tok("g")
            def zYL(e):
                return e.tensor_scalar(col2d2(YLC, 0), smw("w"), 0.0, 0.0,
                                       ALU.mult, ALU.add)
            p.add("v", zYL, waits=gate)

            t_store_par = [None, None]
            t_ringcp = t_ring0
            for fb in range(NF):
                col = fb * FT
                ypar = fb % 2
                YBF = YB2[:, ypar]
                w_l = (list(t_prev_writes)
                       if isinstance(t_prev_writes, list)
                       else [t_prev_writes])
                if t_store_par[ypar] is not None:
                    w_l.append(t_store_par[ypar])
                if fb == 0:
                    w_l.extend(gate)
                t_yl = p.dma("y", f"ya{l}", YBF[:],
                             ysp_h[:, :, col:col + FT], waits=w_l)
                t_rt = p.dma("y", f"rt{l}{fb % 2}", RT[:],
                             rot_h[:, :, col:col + FT].rearrange(
                                 "r p t -> p r t"),
                             waits=[(p.sem["v"], p.cnt["v"])])
                def hq1(e, YBF=YBF):
                    return e.tensor_copy(HQ[:, 0:4, :], YBF[:, 0:4, :])
                p.add("v", hq1, waits=[t_yl])
                def hq2(e, YBF=YBF):
                    return e.tensor_copy(HQ[:, 4:8, 1:FT],
                                         YBF[:, 4:8, 0:FT - 1])
                p.add("v", hq2)
                def hq3(e):
                    return e.tensor_copy(HQ[:, 4:8, 0:1], YLC[:, 4:8, 0:1])
                p.add("v", hq3)
                def ylup(e, YBF=YBF):
                    return e.tensor_copy(col2d2(YLC, 0),
                                         YBF[:, :, FT - 1:FT]
                                         .rearrange("p c n -> p (c n)"))
                p.add("v", ylup)
                t_hq = p.tok("v")

                for tc in range(4):
                    for kc in range(CH):
                        def mv(e, tc=tc, kc=kc):
                            return e.matmul(
                                PA[tc][:], HQ[:, kc, tc * 128:tc * 128 + 128],
                                WQKV[:, kc, 1024:1536],
                                start=(kc == 0), stop=(kc == CH - 1))
                        w = []
                        if tc == 0 and kc == 0:
                            w = [t_hq, t_wfa, t_ring0]
                        p.add("pe", mv, inc=(tc == 3 and kc == CH - 1),
                              waits=w)
                t_vw = p.tok("pe")
                for tc in range(4):
                    def ev(e, tc=tc):
                        return e.tensor_copy(VR[:, 4 + tc, :], PA[tc][:])
                    w = []
                    if tc == 0:
                        w = [t_vw, t_ringcp]
                    p.add("v", ev, waits=w)
                t_vev = p.tok("v")

                def wave(dst_off, swapped, waits0):
                    W_, cols = (WQSW, (dst_off, dst_off + 512)) if swapped \
                        else (WQKV, (dst_off, dst_off + 512))
                    for oc in range(4):
                        for kc in range(CH):
                            def mm(e, oc=oc, kc=kc, W_=W_, cols=cols):
                                return e.matmul(
                                    PA[oc][:],
                                    W_[:, kc, cols[0] + oc * 128:
                                       cols[0] + oc * 128 + 128],
                                    HQ[:, kc, :],
                                    start=(kc == 0), stop=(kc == CH - 1))
                            w = list(waits0) if (oc == 0 and kc == 0) else []
                            p.add("pe", mm,
                                  inc=(oc == 3 and kc == CH - 1), waits=w)
                    return p.tok("pe")

                def qk_pipeline(dst_off, qsnm, qswnm, rows_cols, rt_cs, rt_sn,
                                out_to_kr):
                    t_w = wave(dst_off, False, [p.tok("v"), p.tok("s")])
                    for c in range(4):
                        def sq(e, c=c):
                            return e.activation(SQB[:, c, :], PA[c][:],
                                                AF.Square)
                        p.add("s", sq, waits=([t_w] if c == 0 else ()))
                    t_sq = p.tok("s")
                    for c in range(4):
                        def qa(e, c=c, rt_cs=rt_cs, qsnm=qsnm):
                            return e.scalar_tensor_tensor(
                                QA[:, c, :], PA[c][:], smw(qsnm),
                                RT[:, rt_cs, :], ALU.mult, ALU.mult)
                        p.add("v", qa, waits=([t_sq, t_rt] if c == 0 else ()))
                    t_qa = p.tok("v")
                    for c in range(4):
                        def sm(e, c=c):
                            return e.matmul(
                                BROW[0:8, 0:512],
                                cbf("hsum")[:, c * 8:c * 8 + 8],
                                SQB[:, c, :], start=(c == 0), stop=(c == 3))
                        p.add("pe", sm, inc=(c == 3),
                              waits=([t_sq] if c == 0 else ()))
                    t_sums = p.tok("pe")
                    t_w2 = wave(dst_off, True, [t_qa])
                    def rc(e, rows_cols=rows_cols):
                        return e.tensor_copy(
                            RRF[0:8, rows_cols:rows_cols + 512],
                            BROW[0:8, 0:512])
                    p.add("v", rc, waits=[t_sums])
                    def rm(e, rows_cols=rows_cols):
                        return e.tensor_scalar_max(
                            RRF[0:8, rows_cols:rows_cols + 512],
                            RRF[0:8, rows_cols:rows_cols + 512], 1e-24)
                    p.add("v", rm)
                    t_rv = p.tok("v")
                    def rl(e, rows_cols=rows_cols):
                        return e.activation(
                            RRF[0:8, rows_cols:rows_cols + 512],
                            RRF[0:8, rows_cols:rows_cols + 512], AF.Ln)
                    p.add("s", rl, waits=[t_rv])
                    def re_(e, rows_cols=rows_cols):
                        return e.activation(
                            RRB[0:8, rows_cols:rows_cols + 512],
                            RRF[0:8, rows_cols:rows_cols + 512], AF.Exp,
                            scale=-0.5)
                    p.add("s", re_)
                    t_rr = p.tok("s")
                    for c in range(4):
                        def qb(e, c=c, rt_sn=rt_sn, qswnm=qswnm):
                            return e.scalar_tensor_tensor(
                                QB2[:, c, :], PA[c][:], smw(qswnm),
                                RT[:, rt_sn, :], ALU.mult, ALU.mult)
                        p.add("v", qb, waits=([t_w2] if c == 0 else ()))
                    t_qb = p.tok("v")
                    def gadd(e):
                        return e.tensor_tensor(QA[:], QA[:], QB2[:], ALU.add)
                    p.add("v", gadd, waits=[t_qb])
                    t_g = p.tok("v")
                    for c in range(4):
                        def bc(e, c=c, rows_cols=rows_cols):
                            return e.matmul(
                                PA[c][:, 0:512],
                                cbf("hbc", 0, 8)[:, c * 128:c * 128 + 128],
                                RRB[0:8, rows_cols:rows_cols + 512],
                                start=True, stop=True)
                        w = [t_rr, t_g, t_qb] if c == 0 else []
                        p.add("pe", bc, inc=(c == 3), waits=w)
                    t_bcq = p.tok("pe")
                    for c in range(4):
                        if out_to_kr:
                            def fm(e, c=c):
                                return e.tensor_tensor(
                                    KR[:, c, 512:1024], QA[:, c, :],
                                    PA[c][:, 0:512], ALU.mult)
                        else:
                            def fm(e, c=c):
                                return e.tensor_tensor(
                                    QH[:, c, :], QA[:, c, :],
                                    PA[c][:, 0:512], ALU.mult)
                        p.add("v", fm, waits=([t_bcq] if c == 0 else ()))
                    return p.tok("v")

                t_qh = qk_pipeline(0, f"qs{l}", f"qsw{l}", 0, 0, 1, False)
                t_kh = qk_pipeline(512, f"ks{l}", f"ksw{l}", 512, 2, 3, True)

                its = [(h, qc) for h in range(HEADS) for qc in range(4)]
                t_ex_par = [None, None]
                t_pv_par = [None, None]
                t_ptv_par = [None, None]
                t_oev = None
                st_pn = {}
                st_tp = {}

                def emit_scores(it):
                    h, qc = its[it]
                    par = it % 2
                    w0 = 128 * qc
                    pq = (h % 2) * 64
                    tcp = h // 2
                    sA = PA[par]
                    sB = PA[2 + par]
                    def s1m(e, pq=pq, tcp=tcp, w0=w0, sA=sA):
                        return e.matmul(
                            sA[:, 0:512],
                            QH[pq:pq + 64, tcp, w0:w0 + 128],
                            KR[pq:pq + 64, tcp, w0:w0 + 512],
                            start=True, stop=False)
                    w = [t_qh, t_kh] if it == 0 else []
                    if t_ex_par[par] is not None:
                        w = w + [t_ex_par[par]]
                    p.add("pe", s1m, waits=w)
                    if fb == 0:
                        mA = MSK[:, mA0_off + 512 * qc:mA0_off + 512 * qc + 512]
                    else:
                        mA = MSK[:, mA_off:mA_off + 512]
                    def s1k(e, mA=mA, sA=sA):
                        return e.matmul(sA[:, 0:512], cbf("ident"), mA,
                                        start=False, stop=True)
                    p.add("pe", s1k)
                    def s2m(e, pq=pq, tcp=tcp, w0=w0, sB=sB):
                        return e.matmul(
                            sB[:, 0:128],
                            QH[pq:pq + 64, tcp, w0:w0 + 128],
                            KR[pq:pq + 64, tcp, w0 + 512:w0 + 640],
                            start=True, stop=False)
                    p.add("pe", s2m)
                    def s2k(e, sB=sB):
                        return e.matmul(sB[:, 0:128], cbf("ident"),
                                        MSK[:, mB_off:mB_off + 128],
                                        start=False, stop=True)
                    p.add("pe", s2k, inc=True)
                    return p.tok("pe")

                st_ex = {}

                def emit_vmex(it, t_sc):
                    par = it % 2
                    sA = PA[par]
                    sB = PA[2 + par]
                    so = 1536 + 6 * par
                    nmA = RRF[:, so:so + 1]
                    nmB = RRF[:, so + 1:so + 2]
                    nm = RRF[:, so + 2:so + 3]
                    s1c = RRF[:, so + 3:so + 4]
                    s2c = RRF[:, so + 4:so + 5]
                    def vm1(e, sA=sA, nmA=nmA):
                        return e.reduce_max(nmA, sA[:, 0:512], axis=AX.X,
                                            negate=True)
                    p.add("v", vm1, waits=[t_sc])
                    def vm2(e, sB=sB, nmB=nmB):
                        return e.reduce_max(nmB, sB[:, 0:128], axis=AX.X,
                                            negate=True)
                    p.add("v", vm2)
                    def vm3(e, nm=nm, nmA=nmA, nmB=nmB):
                        return e.tensor_tensor(nm, nmA, nmB, ALU.min)
                    p.add("v", vm3)
                    t_nm = p.tok("v")
                    def ex1(e, sA=sA, nm=nm, s1c=s1c, par=par):
                        return e.activation(PRB[:, par, 0:512], sA[:, 0:512],
                                            AF.Exp, bias=nm, accum_out=s1c)
                    p.add("s", ex1, waits=[t_nm])
                    def ex2(e, sB=sB, nm=nm, s2c=s2c, par=par):
                        return e.activation(PRB[:, par, 512:640],
                                            sB[:, 0:128], AF.Exp,
                                            bias=nm, accum_out=s2c)
                    p.add("s", ex2, inc=True)
                    st_ex[it] = p.tok("s")
                    t_ex_par[par] = st_ex[it]

                def emit_vnorm(it):
                    par = it % 2
                    so = 1536 + 6 * par
                    s1c = RRF[:, so + 3:so + 4]
                    s2c = RRF[:, so + 4:so + 5]
                    def vn1(e, s1c=s1c, s2c=s2c):
                        return e.tensor_tensor(s1c, s1c, s2c, ALU.add)
                    p.add("g", vn1, inc=True, waits=[st_ex[it]])
                    t_g1 = p.tok("g")
                    def vr1(e, s1c=s1c):
                        return e.activation(s1c, s1c, AF.Ln)
                    p.add("s", vr1, waits=[t_g1])
                    def vr2(e, s1c=s1c):
                        return e.activation(s1c, s1c, AF.Exp, scale=-1.0)
                    p.add("s", vr2, inc=True)
                    t_rc = p.tok("s")
                    def vn3(e, s1c=s1c, par=par):
                        return e.tensor_scalar(PRB[:, par, :], PRB[:, par, :],
                                               s1c, None, ALU.mult)
                    p.add("v", vn3, waits=[t_rc])
                    st_pn[it] = p.tok("v")

                def emit_tp(it):
                    par = it % 2
                    for kk in range(5):
                        def tp(e, kk=kk, par=par):
                            return e.transpose(
                                PBF2[par][:, kk * 128:kk * 128 + 128],
                                PRB[:, par, kk * 128:kk * 128 + 128],
                                cbf("ident"))
                        w = [st_pn[it]] if kk == 0 else []
                        if kk == 0 and t_ptv_par[par] is not None:
                            w.append(t_ptv_par[par])
                        p.add("pe", tp, inc=(kk == 4), waits=w)
                    st_tp[it] = p.tok("pe")

                def emit_ptv(it):
                    par = it % 2
                    def ptv(e, par=par):
                        return e.activation(PTB[:, par, :], PBF2[par][:],
                                            AF.Copy)
                    w = [st_tp[it]]
                    if t_pv_par[par] is not None:
                        w.append(t_pv_par[par])
                    t = (p.sem["s"], p.add("s", ptv, waits=w))
                    t_ptv_par[par] = t
                    return t

                def emit_pv(it, t_ptv):
                    nonlocal t_oev
                    h, qc = its[it]
                    par = it % 2
                    for kk in range(5):
                        def pv(e, kk=kk, h=h, qc=qc, par=par):
                            return e.matmul(
                                BSA[0:64, 0:128],
                                VR[:, qc + kk, h * 64:h * 64 + 64],
                                PTB[:, par, kk * 128:kk * 128 + 128],
                                start=(kk == 0), stop=(kk == 4))
                        w = [t_ptv] if kk == 0 else []
                        if kk == 0 and t_oev is not None:
                            w.append(t_oev)
                        p.add("pe", pv, inc=(kk == 4), waits=w)
                    t_pv = p.tok("pe")
                    t_pv_par[par] = t_pv
                    def oev(e, h=h, qc=qc):
                        return e.tensor_copy(
                            OH[0:64, h, qc * 128:qc * 128 + 128],
                            BSA[0:64, 0:128])
                    t_oev = (p.sem["v"], p.add("v", oev, waits=[t_pv]))

                sc_tok = {}
                ptv_tok = {}
                n_its = len(its)
                for it in range(n_its + 3):
                    if it < n_its:
                        sc_tok[it] = emit_scores(it)
                    if 1 <= it <= n_its:
                        emit_vmex(it - 1, sc_tok[it - 1])
                        emit_vnorm(it - 1)
                    if 2 <= it <= n_its + 1:
                        emit_tp(it - 2)
                        ptv_tok[it - 2] = emit_ptv(it - 2)
                    if 3 <= it <= n_its + 2:
                        emit_pv(it - 3, ptv_tok[it - 3])
                t_oh = p.tok("v")

                for half in range(2):
                    for oc4 in range(4):
                        oc = half * 4 + oc4
                        for kc in range(CH):
                            def mo(e, oc=oc, oc4=oc4, kc=kc):
                                return e.matmul(
                                    PA[oc4][:],
                                    WOF[0:64, kc, oc * 128:oc * 128 + 128],
                                    OH[0:64, kc, :], start=(kc == 0),
                                    stop=(kc == CH - 1))
                            w = []
                            if oc4 == 0 and kc == 0:
                                w = [t_oh] if half == 0 else [p.tok("v")]
                            p.add("pe", mo, inc=(oc4 == 3 and kc == CH - 1),
                                  waits=w)
                    t_wo = p.tok("pe")
                    for oc4 in range(4):
                        oc = half * 4 + oc4
                        def yup(e, oc=oc, oc4=oc4, YBF=YBF):
                            return e.tensor_tensor(YBF[:, oc, :],
                                                   YBF[:, oc, :],
                                                   PA[oc4][:], ALU.add)
                        p.add("v", yup, waits=([t_wo] if oc4 == 0 else ()))
                t_yb = p.tok("v")
                t_store_par[ypar] = p.dma("y", f"yw{l}a{ypar}",
                                          ysp_h[:, :, col:col + FT], YBF[:],
                                          waits=[t_yb])
                t_pe_rb = p.tok("pe")
                p.dma("y", f"rg{l}", KR[:, :, 0:512], KR[:, :, 512:1024],
                      waits=[t_pe_rb])
                t_ringcp = p.dma("y", f"rg{l}", VR[:, 0:4, :], VR[:, 4:8, :])
            t_pa_done = p.tok("pe")
            t_prev_writes = [(p.dsem[f"yw{l}a0"], p.dma_cnt[f"yw{l}a0"]),
                             (p.dsem[f"yw{l}a1"], p.dma_cnt[f"yw{l}a1"])]
            if dbg_dump:
                t_prev_writes2 = p.dma("y", "dby", dby_h[f"a{l}"], ysp_h,
                                       waits=list(t_prev_writes))
                t_prev_writes = [t_prev_writes2]
            pa.close()

            pb = ExitStack()
            bsb = lambda nm, shp, dty: pb.enter_context(
                nc.sbuf_tensor(nm + f"_l{l}", shp, dty))
            W1I = bsb("W1I", [128, CH, NSLOT * 128], BF16)
            W2P = bsb("W2P", [128, NJ, 1024], BF16)
            HACT = bsb("HACT", [128, NJ, FT], BF16)
            HB = bsb("HB", [128, CH, FT], BF16)
            SQ2 = bsb("SQ2", [128, CH, FT], BF16)
            TMP1 = bsb("TMP1", [128, FT], F32)
            PB_ = [pb.enter_context(
                nc.psum_tensor(f"pb{i}_l{l}", [128, 512], F32))
                for i in range(4)]
            PBA = pb.enter_context(
                nc.psum_tensor(f"pba_l{l}", [128, 512], F32))
            PBG = pb.enter_context(
                nc.psum_tensor(f"pbg_l{l}", [128, 512], F32))
            PR0 = pb.enter_context(
                nc.psum_tensor(f"pr0_l{l}", [128, 512], F32))
            PR1 = pb.enter_context(
                nc.psum_tensor(f"pr1_l{l}", [128, 512], F32))

            w1r = w1i_h[l].rearrange("(c p) s -> p c s", p=128)
            gateb = [t_pa_done, p.tok("v"), p.tok("s"),
                     p.tok("g")] + p.all_dma_toks()
            W1GRP = 4
            wfb_grp = []
            for jj in range(0, NJ, W1GRP):
                je = min(jj + W1GRP, NJ)
                tk = p.dma("g", f"wfb{l}g{jj // W1GRP}",
                           W1I[:, :, 2 * jj * 128:2 * je * 128],
                           w1r[:, :, 2 * jj * 128:2 * je * 128],
                           waits=(gateb if jj == 0 else ()))
                wfb_grp.append(tk)
            w2r = w2p_h[l].rearrange("(j p) o -> p j o", p=128)
            for jj in range(0, NJ, 4):
                je = min(jj + 4, NJ)
                t_wfb = p.dma("g", f"wfb{l}", W2P[:, jj:je, :],
                              w2r[:, jj:je, :])
            t_wfb_w2p = t_wfb

            def zYL2(e):
                return e.tensor_scalar(col2d2(YLC, 1), smw("w"), 0.0, 0.0,
                                       ALU.mult, ALU.add)
            p.add("v", zYL2)

            t_store_par = [None, None]
            for fb in range(NF):
                col = fb * FT
                ypar = fb % 2
                YBF = YB2[:, ypar]
                w_l = (list(t_prev_writes)
                       if isinstance(t_prev_writes, list)
                       else [t_prev_writes])
                if t_store_par[ypar] is not None:
                    w_l.append(t_store_par[ypar])
                t_yl = p.dma("y", f"yb{l}", YBF[:],
                             ysp_h[:, :, col:col + FT], waits=w_l)
                def c1(e, YBF=YBF):
                    return e.tensor_copy(HB[:, 0:4, :], YBF[:, 0:4, :])
                p.add("v", c1, waits=[t_yl])
                def c2(e, YBF=YBF):
                    return e.tensor_copy(HB[:, 4:8, 1:FT],
                                         YBF[:, 4:8, 0:FT - 1])
                p.add("v", c2)
                def c3(e):
                    return e.tensor_copy(HB[:, 4:8, 0:1], YLC[:, 4:8, 1:2])
                p.add("v", c3)
                t_cb = p.tok("v")
                def sq1(e, YBF=YBF):
                    return e.activation(SQ2[:, 0:4, :], YBF[:, 0:4, :],
                                        AF.Square)
                p.add("s", sq1, waits=[t_yl, (p.sem["v"], p.cnt["v"])])
                def sq2_(e, YBF=YBF):
                    return e.activation(SQ2[:, 4:8, 1:FT],
                                        YBF[:, 4:8, 0:FT - 1], AF.Square)
                p.add("s", sq2_)
                def sq3(e):
                    return e.activation(SQ2[:, 4:8, 0:1], YLC[:, 4:8, 1:2],
                                        AF.Square)
                p.add("s", sq3)
                t_sq = p.tok("s")
                for c in range(CH):
                    def m0(e, c=c):
                        return e.matmul(PR0[0:1, 0:FT], cbf("ones1"),
                                        HB[:, c, :], start=(c == 0),
                                        stop=(c == CH - 1))
                    p.add("pe", m0, waits=([t_cb] if c == 0 else ()))
                for c in range(CH):
                    def m1(e, c=c):
                        return e.matmul(PR1[0:1, 0:FT], cbf("ones1"),
                                        SQ2[:, c, :], start=(c == 0),
                                        stop=(c == CH - 1))
                    p.add("pe", m1, inc=(c == CH - 1),
                          waits=([t_sq] if c == 0 else ()))
                t_st = p.tok("pe")
                mrow = RRF[0:1, 1024:1024 + FT]
                srow = RRF[0:1, 0:FT]
                xrow = RRF[0:1, 512:512 + FT]
                def q1(e):
                    return e.tensor_scalar(mrow, PR0[0:1, 0:FT], 1.0 / D,
                                           None, ALU.mult)
                p.add("v", q1, waits=[t_st])
                def q2(e):
                    return e.tensor_tensor(xrow, mrow, mrow, ALU.mult)
                p.add("v", q2)
                def q3(e):
                    return e.scalar_tensor_tensor(srow, PR1[0:1, 0:FT],
                                                  1.0 / D, xrow, ALU.mult,
                                                  ALU.subtract)
                p.add("v", q3)
                t_v2 = p.tok("v")
                def q4(e):
                    return e.activation(srow, srow, AF.Ln,
                                        bias=smw("eps")[0:1, :])
                p.add("s", q4, waits=[t_v2])
                def q6(e):
                    return e.activation(srow, srow, AF.Exp, scale=-0.5)
                p.add("s", q6)
                t_rw = p.tok("s")
                def bb1(e):
                    return e.matmul(PR0[:, 0:FT], c32("ones", 0, 1), mrow,
                                    start=True, stop=True)
                p.add("pe", bb1, waits=[t_rw])
                def bb2(e):
                    return e.matmul(PR1[:, 0:FT], c32("ones", 0, 1), srow,
                                    start=True, stop=True)
                p.add("pe", bb2, inc=True)
                t_bc = p.tok("pe")
                for c in range(CH):
                    if c < 4:
                        def ap1(e, c=c, YBF=YBF):
                            return e.tensor_tensor(TMP1[:], YBF[:, c, :],
                                                   PR0[:, 0:FT],
                                                   ALU.subtract)
                        p.add("v", ap1, waits=([t_bc] if c == 0 else ()))
                    else:
                        def ap1(e, c=c, YBF=YBF):
                            return e.tensor_tensor(TMP1[:, 1:FT],
                                                   YBF[:, c, 0:FT - 1],
                                                   PR0[:, 1:FT],
                                                   ALU.subtract)
                        p.add("v", ap1, waits=([t_bc] if c == 0 else ()))
                        def ap1b(e, c=c):
                            return e.tensor_tensor(TMP1[:, 0:1],
                                                   YLC[:, c, 1:2],
                                                   PR0[:, 0:1], ALU.subtract)
                        p.add("v", ap1b)
                    def ap2(e, c=c):
                        return e.scalar_tensor_tensor(
                            TMP1[:], TMP1[:], smw(f"lng{l}")[:, c:c + 1],
                            PR1[:, 0:FT], ALU.mult, ALU.mult)
                    p.add("v", ap2)
                    def ap3(e, c=c):
                        return e.tensor_scalar(HB[:, c, :], TMP1[:],
                                               smw(f"lnb{l}")[:, c:c + 1],
                                               None, ALU.add)
                    p.add("v", ap3)
                t_h1 = p.tok("v")
                if dbg_dump and l == 0 and fb == 0:
                    t_dbhb = p.dma("g", "dbg2", dbhb_h, HB[:], waits=[t_h1])
                    t_dw1 = p.dma("g", "dbg2", dbw1_h, W1I[:],
                                  waits=[t_h1, t_wfb])
                    p.add("v", lambda e: e.engine_nop(), waits=[t_dw1])
                def ylup2(e, YBF=YBF):
                    return e.tensor_copy(col2d2(YLC, 1),
                                         YBF[:, :, FT - 1:FT]
                                         .rearrange("p c n -> p (c n)"))
                p.add("v", ylup2, waits=[t_sq])
                t_evj = [None, None]
                for j in range(NJ):
                    par = j % 2
                    bA = PBA if par == 0 else PB_[0]
                    bG = PBG if par == 0 else PB_[1]
                    cA, cG = (1, 0) if par == 0 else (3, 2)
                    for kc in range(CH):
                        def ma(e, j=j, kc=kc, bA=bA):
                            return e.matmul(
                                bA[:], W1I[:, kc,
                                           (2 * j) * 128:(2 * j) * 128 + 128],
                                HB[:, kc, :], start=(kc == 0),
                                stop=(kc == CH - 1))
                        w = []
                        if kc == 0:
                            w = [t_h1] if j == 0 else (
                                [t_evj[par]] if t_evj[par] else [])
                            if fb == 0 and j % W1GRP == 0:
                                w = list(w) + [wfb_grp[j // W1GRP]]
                        p.add("pe", ma, waits=w)
                    for kc in range(CH):
                        def mg(e, j=j, kc=kc, bG=bG):
                            return e.matmul(
                                bG[:], W1I[:, kc,
                                           (2 * j + 1) * 128:
                                           (2 * j + 1) * 128 + 128],
                                HB[:, kc, :], start=(kc == 0),
                                stop=(kc == CH - 1))
                        p.add("pe", mg, inc=(kc == CH - 1))
                    t_j = p.tok("pe")
                    def gg(e, j=j, bG=bG, cG=cG):
                        return e.activation(SQ2[:, cG, 0:FT], bG[:, 0:FT],
                                            AF.Gelu,
                                            bias=smw(f"b1i{l}")
                                            [:, 2 * j + 1:2 * j + 2])
                    p.add("s", gg, waits=[t_j])
                    def aa(e, j=j, bA=bA, cA=cA):
                        return e.tensor_scalar(SQ2[:, cA, 0:FT], bA[:, 0:FT],
                                               smw(f"b1i{l}")
                                               [:, 2 * j:2 * j + 1],
                                               None, ALU.add)
                    p.add("v", aa, waits=[t_j])
                    t_ag = p.tok("s")
                    def hmul(e, j=j, cA=cA, cG=cG):
                        return e.tensor_tensor(HACT[:, j, :], SQ2[:, cA, 0:FT],
                                               SQ2[:, cG, 0:FT], ALU.mult)
                    t_evj[par] = (p.sem["v"], p.add("v", hmul, waits=[t_ag]))
                t_hact = p.tok("v")
                if dbg_dump and l == 0 and fb == 0:
                    t_dbha = p.dma("g", "dbg2", dbha_h, HACT[:],
                                   waits=[t_hact])
                    p.add("v", lambda e: e.engine_nop(),
                          waits=[t_dbha])
                for half in range(2):
                    for oc4 in range(4):
                        oc = half * 4 + oc4
                        for j in range(NJ):
                            def m2(e, oc=oc, oc4=oc4, j=j):
                                return e.matmul(
                                    PB_[oc4][:],
                                    W2P[:, j, oc * 128:oc * 128 + 128],
                                    HACT[:, j, :], start=(j == 0),
                                    stop=(j == NJ - 1))
                            w = []
                            if oc4 == 0 and j == 0:
                                w = [t_hact] if half == 0 else [p.tok("v")]
                                if half == 0 and fb == 0:
                                    w.append(t_wfb_w2p)
                            p.add("pe", m2,
                                  inc=(oc4 == 3 and j == NJ - 1), waits=w)
                    t_w2 = p.tok("pe")
                    for oc4 in range(4):
                        oc = half * 4 + oc4
                        def yup(e, oc=oc, oc4=oc4, YBF=YBF):
                            return e.scalar_tensor_tensor(
                                YBF[:, oc, :], PB_[oc4][:],
                                smw(f"b2_{l}")[:, oc:oc + 1], YBF[:, oc, :],
                                ALU.add, ALU.add)
                        p.add("v", yup, waits=([t_w2] if oc4 == 0 else ()))
                t_yb2 = p.tok("v")
                t_store_par[ypar] = p.dma("y", f"yw{l}b{ypar}",
                                          ysp_h[:, :, col:col + FT], YBF[:],
                                          waits=[t_yb2])
            t_pb_done = p.tok("pe")
            t_prev_writes = [(p.dsem[f"yw{l}b0"], p.dma_cnt[f"yw{l}b0"]),
                             (p.dsem[f"yw{l}b1"], p.dma_cnt[f"yw{l}b1"])]
            if dbg_dump:
                t_prev_writes = [p.dma("y", "dby", dby_h[f"b{l}"], ysp_h,
                                       waits=list(t_prev_writes))]
            t_gate_prev = [t_pb_done, p.tok("v"), p.tok("s"), p.tok("g")]
            pb.close()
            return t_prev_writes, t_gate_prev

        t_gate_prev = None
        for l in range(nlayers):
            t_prev_writes, t_gate_prev = emit_layer(
                l, t_prev_writes, t_front, t_gate_prev)

        with nc.sbuf_tensor("XO", [128, CH, FT], BF16) as XO, \
             nc.sbuf_tensor("XOF", [128, CH, FT], F32) as XOF, \
             nc.sbuf_tensor("YF4", [128, CH, FT], BF16) as YF4, \
             nc.sbuf_tensor("CB4", [128, CH, FT], BF16) as CB4, \
             nc.sbuf_tensor("SQ4", [128, CH, FT], BF16) as SQ4, \
             nc.sbuf_tensor("TMP4", [128, FT], F32) as TMP4, \
             nc.sbuf_tensor("RW4", [1, 1536], F32) as RW4, \
             nc.sbuf_tensor("OT", [128, 4, D], F32) as OT, \
             nc.psum_tensor("PF0", [128, 512], F32) as PF0, \
             nc.psum_tensor("PF1", [128, 512], F32) as PF1, \
             nc.psum_tensor("FR0", [128, 512], F32) as FR0, \
             nc.psum_tensor("FR1", [128, 512], F32) as FR1:
            pt_evtok = {0: None, 1: None}
            out_dma_tok = {tc: None for tc in range(4)}
            last_pe_tok = None
            if stage < 2:
                t_front = [t_xsp, p.tok("v"), p.tok("s"), p.tok("pe")]
                t_prev_writes = None
            for fb in range(NF):
                col = fb * FT
                w_lx = [t_xsp] + (list(t_front[1:2]) if fb == 0 else [])
                if fb > 0:
                    w_lx.append(prev_vgate)
                if fb == 0:
                    w_lx.append((p.sem["v"], p.cnt["v"]))
                    w_lx.append((p.sem["s"], p.cnt["s"]))
                    w_lx.append(p.tok("pe"))
                    w_lx.extend(p.all_dma_toks())
                t_lx = p.dma("y", "xo", XO[:], xsp_h[:, :, col:col + FT],
                             waits=w_lx)
                t_vgate = None
                if stage >= 2:
                    wly = (list(t_prev_writes)
                           if isinstance(t_prev_writes, list)
                           else [t_prev_writes])
                    t_ly = p.dma("y", "yf4", YF4[:],
                                 ysp_h[:, :, col:col + FT],
                                 waits=wly + [t_lx])
                    def c4a(e):
                        return e.tensor_copy(CB4[:], YF4[:])
                    p.add("v", c4a, waits=[t_ly])
                    t_c4 = p.tok("v")
                    def s4a(e):
                        return e.activation(SQ4[:], YF4[:], AF.Square)
                    p.add("s", s4a, waits=[t_ly, (p.sem["v"], p.cnt["v"])])
                    t_s4 = p.tok("s")
                    for c in range(CH):
                        def m0(e, c=c):
                            return e.matmul(FR0[0:1, 0:FT], cbf("ones1"),
                                            CB4[:, c, :], start=(c == 0),
                                            stop=(c == CH - 1))
                        p.add("pe", m0, waits=([t_c4] if c == 0 else ()))
                    for c in range(CH):
                        def m1(e, c=c):
                            return e.matmul(FR1[0:1, 0:FT], cbf("ones1"),
                                            SQ4[:, c, :], start=(c == 0),
                                            stop=(c == CH - 1))
                        p.add("pe", m1, inc=(c == CH - 1),
                              waits=([t_s4] if c == 0 else ()))
                    t_st4 = p.tok("pe")
                    mrow = RW4[0:1, 0:FT]
                    srow = RW4[0:1, 512:512 + FT]
                    xrow = RW4[0:1, 1024:1024 + FT]
                    def q1(e):
                        return e.tensor_scalar(mrow, FR0[0:1, 0:FT], 1.0 / D,
                                               None, ALU.mult)
                    p.add("v", q1, waits=[t_st4])
                    def q2(e):
                        return e.tensor_tensor(xrow, mrow, mrow, ALU.mult)
                    p.add("v", q2)
                    def q3(e):
                        return e.scalar_tensor_tensor(srow, FR1[0:1, 0:FT],
                                                      1.0 / D, xrow,
                                                      ALU.mult, ALU.subtract)
                    p.add("v", q3)
                    t_v4 = p.tok("v")
                    def q4(e):
                        return e.activation(srow, srow, AF.Ln,
                                            bias=smw("eps")[0:1, :])
                    p.add("s", q4, waits=[t_v4])
                    def q6(e):
                        return e.activation(srow, srow, AF.Exp, scale=-0.5)
                    p.add("s", q6)
                    t_r4 = p.tok("s")
                    def bb1(e):
                        return e.matmul(FR0[:, 0:FT], c32("ones", 0, 1),
                                        mrow, start=True, stop=True)
                    p.add("pe", bb1, waits=[t_r4])
                    def bb2(e):
                        return e.matmul(FR1[:, 0:FT], c32("ones", 0, 1),
                                        srow, start=True, stop=True)
                    p.add("pe", bb2, inc=True)
                    t_bc4 = p.tok("pe")
                    for c in range(CH):
                        def f1(e, c=c):
                            return e.tensor_tensor(TMP4[:], YF4[:, c, :],
                                                   FR0[:, 0:FT],
                                                   ALU.subtract)
                        w = [t_bc4, t_lx] if c == 0 else []
                        if c == 0 and last_pe_tok is not None:
                            w.append(last_pe_tok)
                        p.add("v", f1, waits=w)
                        def f2(e, c=c):
                            return e.tensor_tensor(TMP4[:], TMP4[:],
                                                   FR1[:, 0:FT], ALU.mult)
                        p.add("v", f2)
                        def f3(e, c=c):
                            return e.tensor_tensor(XOF[:, c, :], XO[:, c, :],
                                                   TMP4[:], ALU.add)
                        p.add("v", f3)
                    t_vgate = p.tok("v")
                else:
                    def cpx(e):
                        return e.tensor_copy(XOF[:], XO[:])
                    wc = [t_lx]
                    if last_pe_tok is not None:
                        wc.append(last_pe_tok)
                    p.add("v", cpx, inc=True, waits=wc)
                    t_vgate = p.tok("v")
                prev_vgate = t_vgate
                for tc in range(4):
                    for half, PT in ((0, PF0), (1, PF1)):
                        for cc in range(4):
                            c = half * 4 + cc
                            def tr(e, tc=tc, c=c, cc=cc, PT=PT):
                                return e.transpose(
                                    PT[:, cc * 128:cc * 128 + 128],
                                    XOF[:, c, tc * 128:tc * 128 + 128],
                                    c32("ident"))
                            w = []
                            if cc == 0:
                                w.append(t_lx)
                                if t_vgate is not None:
                                    w.append(t_vgate)
                                if pt_evtok[half] is not None:
                                    w.append(pt_evtok[half])
                            p.add("pe", tr, inc=(cc == 3), waits=w)
                        tpe = p.tok("pe")
                        last_pe_tok = tpe
                        def ev(e, tc=tc, half=half, PT=PT):
                            return e.tensor_copy(
                                OT[:, tc, half * 512:half * 512 + 512], PT[:])
                        w = [tpe]
                        if half == 0 and out_dma_tok[tc] is not None:
                            w.append(out_dma_tok[tc])
                        ve = p.add("v", ev, waits=w)
                        pt_evtok[half] = (p.sem["v"], ve)
                    t_evt = p.tok("v")
                    out_dma_tok[tc] = p.dma(
                        "y", f"out{tc}",
                        out_h[col + tc * 128: col + tc * 128 + 128, :],
                        OT[:, tc, :], waits=[t_evt])
            def fin(e):
                return e.nop()
            p.add("y", fin,
                  waits=[(p.dsem[f"out{tc}"], p.dma_cnt[f"out{tc}"])
                         for tc in range(4)])

        with nc.Block() as block:
            p.emit(block)
    return nc


SMALL_SEGS = [("eps", 1), ("e9", 1),
              ("w", 8), ("u", 8), ("tmk", 8), ("tmv", 8), ("tmr", 8),
              ("lng0", 8), ("lnb0", 8), ("lng1", 8), ("lnb1", 8),
              ("b2_0", 8), ("b2_1", 8), ("qs0", 1), ("ks0", 1),
              ("qs1", 1), ("ks1", 1), ("qsw0", 1), ("ksw0", 1),
              ("qsw1", 1), ("ksw1", 1), ("b1i0", NSLOT), ("b1i1", NSLOT)]
SMALL_OFF = {}
_c = 0
for _nm, _w in SMALL_SEGS:
    SMALL_OFF[_nm] = (_c, _w)
    _c += _w
SMALL_COLS = _c


def _feat(v):
    return np.ascontiguousarray(v.reshape(8, 128).T, np.float32)


def _small_params(inp):
    sm = np.zeros((128, SMALL_COLS), np.float32)

    def put(nm, arr):
        o, w = SMALL_OFF[nm]
        sm[:, o:o + w] = arr.reshape(128, w)

    put("eps", np.full((128, 1), 1e-5, np.float32))
    put("e9", np.full((128, 1), 1e-9, np.float32))
    put("w", _feat(inp["L0_attn_time_decay"]))
    put("u", _feat(inp["L0_attn_time_first"]))
    put("tmk", _feat(inp["L0_attn_time_mix_k"].reshape(-1)))
    put("tmv", _feat(inp["L0_attn_time_mix_v"].reshape(-1)))
    put("tmr", _feat(inp["L0_attn_time_mix_r"].reshape(-1)))
    for l in range(2):
        put(f"lng{l}", _feat(inp["L0_ffn_ln_g"][l]))
        put(f"lnb{l}", _feat(inp["L0_ffn_ln_b"][l]))
        put(f"b2_{l}", _feat(inp["L0_ffn_b2"][l]))
        qs = inp["L0_ffn_q_scale"][l]
        ks = inp["L0_ffn_k_scale"][l]
        du = np.arange(128) % 64
        put(f"qs{l}", qs[du].reshape(128, 1))
        put(f"ks{l}", ks[du].reshape(128, 1))
        put(f"qsw{l}", qs[(du + 32) % 64].reshape(128, 1))
        put(f"ksw{l}", ks[(du + 32) % 64].reshape(128, 1))
        b1 = inp["L0_ffn_b1"][l]
        b1i = np.zeros((NSLOT, 128), np.float32)
        for j in range(NJ):
            a = b1[128 * j:128 * (j + 1)]
            g = b1[HID + 128 * j:HID + 128 * (j + 1)]
            b1i[2 * j, :len(a)] = a
            b1i[2 * j + 1, :len(g)] = g
        put(f"b1i{l}", b1i.T)
    return sm


def _prep_weights(inp):
    d = {}
    d["wkT"] = np.ascontiguousarray(inp["L0_attn_W_k"].T, np.float32)
    d["wvT"] = np.ascontiguousarray(inp["L0_attn_W_v"].T, np.float32)
    d["wrT"] = np.ascontiguousarray(inp["L0_attn_W_r"].T, np.float32)
    d["woT"] = np.ascontiguousarray(inp["L0_attn_W_o"].T, np.float32)
    d["wqkv"] = np.ascontiguousarray(inp["L0_ffn_wqkv"], np.float32)
    perm = np.arange(1024)
    for o in range(1024):
        h, du = o // 64, o % 64
        perm[o] = h * 64 + (du + 32) % 64
    d["wqsw"] = np.ascontiguousarray(inp["L0_ffn_wqkv"][:, :, perm],
                                     np.float32)
    d["wof"] = np.ascontiguousarray(inp["L0_ffn_wo"], np.float32)
    w1 = inp["L0_ffn_w1"]
    w1i = np.zeros((2, D, NSLOT, 128), np.float32)
    for j in range(NJ):
        n = min(128, HID - 128 * j)
        w1i[:, :, 2 * j, :n] = w1[:, :, 128 * j:128 * j + n]
        w1i[:, :, 2 * j + 1, :n] = w1[:, :, HID + 128 * j:HID + 128 * j + n]
    d["w1i"] = w1i.reshape(2, D, NSLOT * 128)
    w2 = inp["L0_ffn_w2"]
    w2p = np.zeros((2, NJ * 128, D), np.float32)
    w2p[:, :HID, :] = w2
    d["w2p"] = w2p
    return d


def kernel(**inputs):
    x = np.ascontiguousarray(np.asarray(inputs["L0_attn_x"], np.float32))
    c32_np, segs32, cbf_np, segsbf, rot_np, cbf_res = _host_consts()
    c32_buf, _ = _pack(c32_np, segs32)
    cbf_buf, _ = _pack(cbf_np, segsbf)
    sm = _small_params(inputs)
    wts = _prep_weights(inputs)

    import os
    ncores = int(os.environ.get("K_NCORES", NCORES))
    nc = build(bass.Bass("TRN2", target_bir_lowering=False), BUILD_STAGE)
    in_maps = []
    for b in range(ncores):
        m = {"x": np.ascontiguousarray(x[b]),
             "cst32": c32_buf, "cstbf": cbf_buf, "rot": rot_np,
             "smallw": sm}
        m.update(wts)
        in_maps.append(m)
    res = run_bass_kernel_spmd(nc, in_maps, core_ids=list(range(ncores)))
    if os.environ.get("K_DUMP"):
        kernel.dbg = res.results
    out = np.stack([r["out"] for r in res.results], 0)
    if ncores < NCORES:
        out = np.concatenate([out] + [out[:1]] * (NCORES - ncores), 0)
    return out


# version 20
# speedup vs baseline: 1.0288x; 1.0288x over previous
import math
import numpy as np
from contextlib import ExitStack

import concourse.bass as bass
import concourse.mybir as mybir
from concourse.bass_utils import run_bass_kernel_spmd

F32 = mybir.dt.float32
BF16 = mybir.dt.bfloat16
AF = mybir.ActivationFunctionType
ALU = mybir.AluOpType
AX = mybir.AxisListType

T = 2048
D = 1024
CH = 8
NCORES = 8
HEADS = 8
DH = 64
INNER = 512
HID = 2730
NJ = 22
NSLOT = 44
WIN = 512

TB = 256
NB = T // TB
TS = 128
NS = TB // TS

FT = 512
NF = T // FT

NEG = -1.0e30

BUILD_STAGE = 3
DBG_STOP = None
DBG_NB = None


def _host_consts():
    c32 = {}
    segs32 = []

    def add32(name, arr):
        arr = np.asarray(arr, np.float32)
        assert arr.shape[0] == 128 or arr.ndim == 2
        c32[name] = arr
        segs32.append(name)

    add32("ones", np.ones((128, 128), np.float32))
    add32("ident", np.eye(128, dtype=np.float32))
    add32("iota", np.broadcast_to(np.arange(128, dtype=np.float32)[None, :],
                                  (128, 128)).copy())
    add32("iota1", np.broadcast_to(
        np.arange(1, 129, dtype=np.float32)[None, :], (128, 128)).copy())
    hbc = np.zeros((128, 4, 128), np.float32)
    for c in range(4):
        hbc[2 * c, c, 0:64] = 1.0
        hbc[2 * c + 1, c, 64:128] = 1.0
    add32("hbc", hbc.reshape(128, 512))

    i = np.arange(32, dtype=np.float64)
    inv_freq = 1.0 / (10000.0 ** (2.0 * i / 64.0))
    base = (2.0 * i + 0.4 * 64.0) / (1.4 * 64.0)
    t = np.arange(T, dtype=np.float64)
    ang = t[:, None] * inv_freq[None, :]
    power = (t - T // 2) / float(WIN // 2)
    scl = base[None, :] ** power[:, None]
    cos = np.cos(ang)
    sin = np.sin(ang)
    def mk(tabc, tabs):
        cs = np.concatenate([tabc, tabc], 1)
        sn = np.concatenate([-tabs, tabs], 1)
        return cs, sn
    csq, snq = mk(cos * scl * 8.0, sin * scl * 8.0)
    csk, snk = mk(cos / scl, sin / scl)
    rot = np.stack([csq, snq, csk, snk], 0)
    rot = np.broadcast_to(rot.transpose(0, 2, 1)[:, None, :, :],
                          (4, 2, 64, T)).reshape(4, 128, T)
    rot = np.ascontiguousarray(rot, np.float32)

    cbf = {}
    segsbf = []

    def addbf(name, arr):
        cbf[name] = np.asarray(arr, np.float32)
        segsbf.append(name)

    addbf("ident", np.eye(128, dtype=np.float32))
    addbf("ones1", np.ones((128, 1), np.float32))
    hsum = np.zeros((128, 4, 8), np.float32)
    for c in range(4):
        hsum[0:64, c, 2 * c] = 1.0
        hsum[64:128, c, 2 * c + 1] = 1.0
    addbf("hsum", hsum.reshape(128, 32))
    addbf("hbc", hbc.reshape(128, 512))
    a = np.arange(128)[:, None]
    j = np.arange(512)[None, :]
    maskA = np.where(j >= a, 0.0, NEG)
    addbf("maskA", maskA)
    jb = np.arange(128)[None, :]
    maskB = np.where(jb <= a, 0.0, NEG)
    addbf("maskB", maskB)
    mA0 = np.empty((128, 4, 512), np.float64)
    for qc in range(4):
        lim = 512 - 128 * qc
        mA0[:, qc, :] = np.where((j >= a) & (j >= lim), 0.0, NEG)
    addbf("maskA0", mA0.reshape(128, 2048))
    resident = ["ident", "ones1", "hsum", "hbc"]

    return c32, segs32, cbf, segsbf, rot, resident


def _pack(cdict, order):
    offs = {}
    cols = 0
    for nm in order:
        a = cdict[nm].reshape(128, -1)
        offs[nm] = (cols, a.shape[1])
        cols += a.shape[1]
    buf = np.zeros((128, cols), np.float32)
    for nm in order:
        a = cdict[nm].reshape(128, -1)
        o, w = offs[nm]
        buf[:, o:o + w] = a
    return buf, offs


class PB:
    ENGS = ("pe", "v", "s", "g", "y")

    def __init__(self, nc):
        self.nc = nc
        self.items = {e: [] for e in self.ENGS}
        self.cnt = {e: 0 for e in self.ENGS}
        self.dma_cnt = {}
        self.sem = {}
        self.dsem = {}
        self._es = None

    def init_sems(self, es):
        self._es = es
        for e in self.ENGS:
            self.sem[e] = es.enter_context(self.nc.semaphore(f"sem_{e}"))

    CHAINED = ("v", "s", "g")

    @staticmethod
    def _ckw(waits):
        for w in waits:
            assert (isinstance(w, tuple) and len(w) == 2
                    and isinstance(w[1], int)), f"bad wait token: {w!r}"

    def add(self, eng, fn, inc=False, waits=()):
        self._ckw(waits)
        if eng in self.CHAINED:
            prev = self.cnt[eng]
            w = list(waits) + [(self.sem[eng], prev)]
            self.items[eng].append((fn, True, w))
            self.cnt[eng] += 1
        else:
            self.items[eng].append((fn, bool(inc), list(waits)))
            if inc:
                self.cnt[eng] += 1
        return self.cnt[eng]

    def dma(self, q, name, out, in_, waits=()):
        if name not in self.dsem:
            self.dsem[name] = self._es.enter_context(
                self.nc.semaphore(f"dma_{name}"))
            self.dma_cnt[name] = 0

        self._ckw(waits)

        def fn(e, out=out, in_=in_):
            return e.dma_start(out=out, in_=in_)
        self.items[q].append((fn, ("dma", name), list(waits)))
        self.dma_cnt[name] += 16
        return (self.dsem[name], self.dma_cnt[name])

    def tok(self, eng):
        return (self.sem[eng], self.cnt[eng])

    def all_dma_toks(self):
        return [(self.dsem[n], self.dma_cnt[n]) for n in self.dsem]

    def emit(self, block):
        def mk_body(eng_name):
            def body(e):
                for fn, inc, waits in self.items[eng_name]:
                    for sem, val in waits:
                        e.wait_ge(sem, val)
                    instr = fn(e)
                    if inc is True:
                        instr.then_inc(self.sem[eng_name], 1)
                    elif isinstance(inc, tuple) and inc[0] == "dma":
                        instr.then_inc(self.dsem[inc[1]], 16)
            return body
        block.tensor(mk_body("pe"))
        block.vector(mk_body("v"))
        block.scalar(mk_body("s"))
        block.gpsimd(mk_body("g"))
        block.sync(mk_body("y"))


def build(nc, stage):
    dt = lambda nm, shp, k=None: (nc.dram_tensor(nm, shp, F32, kind=k)
                                  if k else nc.dram_tensor(nm, shp, F32)).ap()
    x_h = dt("x", [T, D], "ExternalInput")
    out_h = dt("out", [T, D], "ExternalOutput")
    c32_np, segs32, cbf_np, segsbf, rot_np, cbf_res = _host_consts()
    c32_buf, c32_off = _pack(c32_np, segs32)
    cbf_buf, cbf_off = _pack(cbf_np, segsbf)
    smallw = dt("smallw", [128, SMALL_COLS], "ExternalInput")
    cst32_h = dt("cst32", list(c32_buf.shape), "ExternalInput")
    cstbf_h = dt("cstbf", list(cbf_buf.shape), "ExternalInput")
    rot_h = dt("rot", [4, 128, T], "ExternalInput")
    wkT_h = dt("wkT", [D, D], "ExternalInput")
    wvT_h = dt("wvT", [D, D], "ExternalInput")
    wrT_h = dt("wrT", [D, D], "ExternalInput")
    woT_h = dt("woT", [D, D], "ExternalInput")
    wqkv_h = dt("wqkv", [2, D, 1536], "ExternalInput")
    wqsw_h = dt("wqsw", [2, D, 1024], "ExternalInput")
    wof_h = dt("wof", [2, INNER, D], "ExternalInput")
    w1i_h = dt("w1i", [2, D, NSLOT * 128], "ExternalInput")
    w2p_h = dt("w2p", [2, NJ * 128, D], "ExternalInput")
    xsp_h = nc.dram_tensor("xsp", [128, CH, T], BF16).ap()
    ysp_h = nc.dram_tensor("ysp", [128, CH, T], BF16).ap()
    import os
    dbg_dump = bool(os.environ.get("K_DUMP"))
    if dbg_dump:
        dby_h = {}
        for tag in ("0", "a0", "b0", "a1", "b1"):
            dby_h[tag] = dt(f"dby{tag}", [128, CH, T], "ExternalOutput")
        dbhb_h = dt("dbhb", [128, CH, FT], "ExternalOutput")
        dbha_h = dt("dbha", [128, NJ, FT], "ExternalOutput")
        dbw1_h = dt("dbw1", [128, CH, NSLOT * 128], "ExternalOutput")

    p = PB(nc)
    es = ExitStack()
    with es:
        p.init_sems(es)
        sb = lambda nm, shp, dty: es.enter_context(nc.sbuf_tensor(nm, shp, dty))
        front_es = ExitStack()
        fsb = lambda nm, shp, dty: front_es.enter_context(
            nc.sbuf_tensor(nm, shp, dty))
        C32 = sb("C32", list(c32_buf.shape), F32)
        ncres = sum(cbf_off[nm][1] for nm in cbf_res)
        CBF = sb("CBF", [128, ncres], BF16)
        SMW = sb("SMW", [128, SMALL_COLS], F32)
        RRF = sb("RRF", [128, 2048], F32)
        XF = fsb("XF", [128, CH, T], BF16)

        def c32(nm, r0=0, r1=128):
            o, w = c32_off[nm]
            return C32[r0:r1, o:o + w]

        def cbf(nm, r0=0, r1=128):
            o, w = cbf_off[nm]
            assert nm in cbf_res and o + w <= ncres
            return CBF[r0:r1, o:o + w]

        def cbf_h(nm):
            o, w = cbf_off[nm]
            return cstbf_h[:, o:o + w]

        def smw(nm):
            o, w = SMALL_OFF[nm]
            return SMW[:, o:o + w]

        t_c32 = p.dma("y", "cst", C32[:], cst32_h)
        t_smw = p.dma("y", "smw", SMW[:], smallw)
        t_cbf = p.dma("g", "cbf", CBF[:], cstbf_h[:, 0:ncres])

        s0_ev = [None] * 16
        with nc.sbuf_tensor("XTOK", [128, 16, D], BF16) as XTOK, \
             nc.psum_tensor("PT0", [128, 512], BF16) as PT0, \
             nc.psum_tensor("PT1", [128, 512], BF16) as PT1:
            t_x = p.dma("g", "x", XTOK[:],
                        x_h.rearrange("(tc p) c -> p tc c", p=128))
            wx = [t_x, t_cbf]
            pev = 0
            for tc in range(16):
                for half, PT in ((0, PT0), (1, PT1)):
                    for cc in range(4):
                        c = half * 4 + cc
                        def tr(e, tc=tc, c=c, cc=cc, PT=PT):
                            return e.transpose(
                                PT[:, cc * 128:(cc + 1) * 128],
                                XTOK[:, tc, c * 128:(c + 1) * 128],
                                cbf("ident"))
                        w = list(wx) if (tc == 0 and half == 0 and cc == 0) else []
                        if tc > 0 and cc == 0:
                            w.append((p.sem["v"], pev - 1))
                        p.add("pe", tr, inc=(cc == 3), waits=w)
                    tpe = p.tok("pe")
                    def ev(e, tc=tc, half=half, PT=PT):
                        return e.tensor_copy(
                            XF[:, half * 4:half * 4 + 4, tc * 128:tc * 128 + 128],
                            PT[:].rearrange("p (c n) -> p c n", c=4))
                    pev = p.add("v", ev, inc=True, waits=[tpe])
                s0_ev[tc] = p.tok("v")

        t_s0pe = p.tok("pe")
        WK = fsb("WK", [128, CH, D], BF16)
        WV = fsb("WV", [128, CH, D], BF16)
        WR = fsb("WR", [128, CH, D], BF16)
        WO = fsb("WO", [128, CH, D], BF16)
        p.dma("g", "wr4", WK[:], wkT_h.rearrange("(c p) o -> p c o", p=128),
              waits=[t_s0pe])
        p.dma("g", "wr4", WV[:], wvT_h.rearrange("(c p) o -> p c o", p=128))
        p.dma("g", "wr4", WR[:], wrT_h.rearrange("(c p) o -> p c o", p=128))
        t_wr4 = p.dma("g", "wr4", WO[:],
                      woT_h.rearrange("(c p) o -> p c o", p=128))

        rwkv_scope = ExitStack()
        with rwkv_scope as rs:
            rsb = lambda nm, shp, dty: rs.enter_context(
                nc.sbuf_tensor(nm, shp, dty))
            H0 = rsb("H0", [128, CH, TB + 1], BF16)
            XKk = rsb("XKk", [128, CH, TB], BF16)
            XKv = rsb("XKv", [128, CH, TB], BF16)
            WKVR = rsb("WKVR", [128, 2, CH, TB], BF16)
            RG2 = rsb("RG2", [128, 2, CH, TB], BF16)
            KF = rsb("KF", [128, CH, TB], F32)
            XC = rsb("XC", [128, CH, TB], F32)
            VFB = rsb("VFB", [128, CH, TB], BF16)
            M2 = rsb("M2", [128, 2, CH, TS + 1], F32)
            A2 = rsb("A2", [128, 2, CH, TS + 1], BF16)
            B2 = rsb("B2", [128, 2, CH, TS + 1], BF16)
            S1B = rsb("S1B", [128, CH, TS], F32)
            SAB = rsb("SAB", [128, 2, CH, TS], F32)
            E12 = rsb("E12", [128, 2, CH, TS], BF16)
            N1B = rsb("N1B", [128, CH, TS], BF16)
            DEN = rsb("DEN", [128, CH, TS], BF16)
            RCB = rsb("RCB", [128, CH, TS], BF16)
            TWU = rsb("TWU", [128, CH, TS], F32)
            TWZ = rsb("TWZ", [128, CH, TS], F32)
            SMX = rsb("SMX", [128, CH], F32)
            PBS = [rs.enter_context(nc.psum_tensor(f"pb{i}", [128, 512], F32))
                   for i in range(8)]

            murow = RRF[0:1, 0:TB]
            rstdrow = RRF[0:1, 256:256 + TB]
            scrap = RRF[0:1, 512:512 + TB]
            sdrow = RRF[0:1, 768:768 + TB]

            wsm = [t_smw, t_c32]
            def f0(e):
                return e.tensor_tensor(SMX[:], smw("w"), smw("u"), ALU.add)
            p.add("v", f0, waits=wsm)
            for c in range(CH):
                def f1(e, c=c):
                    return e.tensor_scalar(TWU[:, c, :], c32("iota"),
                                           smw("w")[:, c:c + 1], None, ALU.mult)
                p.add("v", f1)
            for c in range(CH):
                def f2(e, c=c):
                    return e.tensor_scalar(TWU[:, c, :], TWU[:, c, :],
                                           smw("u")[:, c:c + 1], None,
                                           ALU.subtract)
                p.add("v", f2)
            for c in range(CH):
                def f3z(e, c=c):
                    return e.tensor_scalar(TWZ[:, c, :], TWU[:, c, :],
                                           SMX[:, c:c + 1], None, ALU.add)
                p.add("v", f3z)

            def col2d4(buf, par, i):
                return buf[:, par, :, i:i + 1].rearrange("p c n -> p (c n)")

            def fill(dst, val):
                def f(e, dst=dst, val=val):
                    return e.tensor_scalar(dst, smw("w"), 0.0, float(val),
                                           ALU.mult, ALU.add)
                return f
            p.add("v", fill(col2d4(M2, 0, 0), -1.0e38))
            p.add("v", fill(col2d4(A2, 0, 0), 0.0))
            p.add("v", fill(col2d4(B2, 0, 0), 0.0))
            p.add("v", fill(H0[:, :, 0:1].rearrange("p c n -> p (c n)"), 0.0),
                  inc=True)

            ST = {"pend": None, "sg": 0, "t_exps": None, "t_vnum": None,
                  "t_gb": None, "t_vnum2": None,
                  "t_vwk": None, "t_vwk_s1": None, "t_wkvr_done": None,
                  "t_ln2v": p.tok("v"), "t_ln2pe": None,
                  "t_kw": None, "t_vw": None, "t_rw": None,
                  "t_kev": None, "t_vev": None, "t_revs": None,
                  "t_aev": None, "t_aevs": None}

            def emit_wkv(pd):
                def wkv(e):
                    return e.tensor_tensor(N1B[:], N1B[:], RCB[:], ALU.mult)
                w = [pd["t_sdn"]]
                if ST["t_ln2pe"] is not None:
                    w.append(ST["t_ln2pe"])
                p.add("v", wkv, waits=w)
                bp = pd["b"] & 1
                def wkr(e, bp=bp, wsl=pd["wsl"]):
                    return e.tensor_tensor(WKVR[:, bp, :, wsl], N1B[:],
                                           RG2[:, bp, :, wsl], ALU.mult)
                p.add("v", wkr, inc=True)
                ST["t_vwk"] = p.tok("v")
                if pd["sblk"] == 1:
                    ST["t_vwk_s1"] = ST["t_vwk"]
                    ST["t_wkvr_done"] = ST["t_vwk"]

            def emit_slot(b, sblk):
                sg = ST["sg"]
                par = sg & 1
                op = 1 - par
                wsl = slice(sblk * TS, sblk * TS + TS)
                pend = ST["pend"]
                if sg > 0:
                    def mcar(e, par=par, op=op):
                        return e.scalar_tensor_tensor(
                            col2d4(M2, par, 0), smw("w"), float(TS),
                            col2d4(M2, op, TS), ALU.mult, ALU.add)
                    p.add("v", mcar)
                    def acar(e, par=par, op=op):
                        return e.tensor_copy(col2d4(A2, par, 0),
                                             col2d4(A2, op, TS))
                    p.add("v", acar)
                    def bcar(e, par=par, op=op):
                        return e.tensor_copy(col2d4(B2, par, 0),
                                             col2d4(B2, op, TS))
                    p.add("v", bcar)
                wz = [pend["t_sdn"]] if pend is not None else []
                if sblk == 0:
                    wz.append(ST["t_kev"])
                def zf(e, wsl=wsl):
                    return e.tensor_tensor(S1B[:], KF[:, :, wsl], TWZ[:],
                                           ALU.subtract)
                p.add("v", zf, waits=wz)
                for c in range(CH):
                    def cm(e, c=c, par=par):
                        return e.tensor_tensor_scan(
                            M2[:, par, c, 1:TS + 1], S1B[:, c, :],
                            S1B[:, c, :], M2[:, par, c, 0:1],
                            ALU.max, ALU.max)
                    p.add("v", cm)
                wa = []
                if ST["t_exps"] is not None:
                    wa.append(ST["t_exps"])
                if ST["t_gb"] is not None:
                    wa.append(ST["t_gb"])
                def asub(e, par=par):
                    return e.tensor_tensor(SAB[:, 0], M2[:, par, :, 0:TS],
                                           M2[:, par, :, 1:TS + 1],
                                           ALU.subtract)
                p.add("v", asub, waits=wa)
                def bsub(e, par=par):
                    return e.tensor_tensor(SAB[:, 1], S1B[:],
                                           M2[:, par, :, 1:TS + 1],
                                           ALU.subtract)
                p.add("v", bsub, inc=True)
                t_subs = p.tok("v")
                def dd1(e, wsl=wsl, par=par):
                    return e.tensor_tensor(S1B[:], KF[:, :, wsl],
                                           M2[:, par, :, 0:TS], ALU.subtract)
                p.add("v", dd1)
                def dd2(e):
                    return e.tensor_tensor(S1B[:], S1B[:], TWU[:],
                                           ALU.subtract)
                p.add("v", dd2, inc=True)
                t_vD = p.tok("v")
                if pend is not None:
                    emit_wkv(pend)
                def exps(e):
                    return e.activation(SAB[:], SAB[:], AF.Exp)
                p.add("s", exps, inc=True, waits=[t_subs])
                ST["t_exps"] = p.tok("s")
                wr1 = [t_vD]
                if ST["t_vnum"] is not None:
                    wr1.append(ST["t_vnum"])
                def rl1(e):
                    return e.activation(E12[:, 0], S1B[:], AF.Relu)
                p.add("s", rl1, waits=wr1)
                def rl2(e):
                    return e.activation(E12[:, 1], S1B[:], AF.Relu, scale=-1.0)
                p.add("s", rl2)
                def eexp(e):
                    return e.activation(E12[:], E12[:], AF.Exp, scale=-1.0)
                p.add("s", eexp, inc=True)
                t_sE = p.tok("s")
                for c in range(CH):
                    def sb_(e, c=c, par=par):
                        return e.tensor_tensor_scan(
                            B2[:, par, c, 1:TS + 1], SAB[:, 0, c, :],
                            SAB[:, 1, c, :], B2[:, par, c, 0:1],
                            ALU.mult, ALU.add)
                    p.add("v", sb_,
                          waits=([ST["t_exps"]] if c == 0 else ()))
                def bv(e, wsl=wsl):
                    return e.tensor_tensor(SAB[:, 1], SAB[:, 1],
                                           VFB[:, :, wsl], ALU.mult)
                p.add("v", bv)
                for c in range(CH):
                    def sa_(e, c=c, par=par):
                        return e.tensor_tensor_scan(
                            A2[:, par, c, 1:TS + 1], SAB[:, 0, c, :],
                            SAB[:, 1, c, :], A2[:, par, c, 0:1],
                            ALU.mult, ALU.add)
                    p.add("v", sa_)
                def d1(e, par=par):
                    return e.tensor_tensor(DEN[:], E12[:, 0],
                                           B2[:, par, :, 0:TS], ALU.mult)
                p.add("v", d1, waits=[t_sE])
                def d2(e):
                    return e.tensor_tensor(DEN[:], DEN[:], E12[:, 1], ALU.add)
                p.add("v", d2, inc=True)
                t_vden = p.tok("v")
                def t2(e, par=par):
                    return e.tensor_tensor(N1B[:], E12[:, 0],
                                           A2[:, par, :, 0:TS], ALU.mult)
                p.add("v", t2)
                def t3(e, wsl=wsl):
                    return e.tensor_tensor(E12[:, 1], E12[:, 1],
                                           VFB[:, :, wsl], ALU.mult)
                p.add("v", t3)
                def t4(e):
                    return e.tensor_tensor(N1B[:], N1B[:], E12[:, 1], ALU.add)
                p.add("v", t4, inc=True)
                ST["t_vnum"] = p.tok("v")
                ST["t_vnum2"] = ST["t_vnum"]
                wln = [t_vden]
                if ST["t_vwk"] is not None:
                    wln.append(ST["t_vwk"])
                def dln(e):
                    return e.activation(S1B[:], DEN[:], AF.Ln,
                                        bias=smw("e9")[:, 0:1])
                p.add("s", dln, waits=wln)
                def dex(e):
                    return e.activation(RCB[:], S1B[:], AF.Exp, scale=-1.0)
                p.add("s", dex, inc=True)
                t_sdn = p.tok("s")
                ST["pend"] = {"b": b, "sblk": sblk, "wsl": wsl, "par": par,
                              "t_sE": t_sE, "t_sdn": t_sdn}
                ST["sg"] = sg + 1

            def emit_ln1_mixk_kwave(b):
                col = b * TB
                xfb = XF[:, :, col:col + TB]
                if b > 0:
                    def hcar(e):
                        return e.tensor_copy(H0[:, :, 0:1], H0[:, :, TB:TB + 1])
                    p.add("v", hcar)
                ws = [s0_ev[2 * b + 1], ST["t_ln2v"]]
                def sqr(e, xfb=xfb):
                    return e.activation(XKk[:], xfb, AF.Square)
                p.add("s", sqr, inc=True, waits=ws)
                t_sqr = p.tok("s")
                for c in range(CH):
                    def m0(e, c=c, xfb=xfb):
                        return e.matmul(PBS[6][0:1, 0:TB], cbf("ones1"),
                                        xfb[:, c, :], start=(c == 0),
                                        stop=(c == CH - 1))
                    w = []
                    if c == 0:
                        w = [s0_ev[2 * b + 1], ST["t_ln2v"], t_cbf]
                        if ST["t_revs"] is not None:
                            w.append(ST["t_revs"])
                        if ST["t_aevs"] is not None:
                            w.append(ST["t_aevs"])
                    p.add("pe", m0, waits=w)
                for c in range(CH):
                    def m1(e, c=c):
                        return e.matmul(PBS[7][0:1, 0:TB], cbf("ones1"),
                                        XKk[:, c, :], start=(c == 0),
                                        stop=(c == CH - 1))
                    p.add("pe", m1, inc=(c == CH - 1),
                          waits=([t_sqr] if c == 0 else ()))
                t_stats = p.tok("pe")
                def r1(e):
                    return e.tensor_scalar(murow, PBS[6][0:1, 0:TB],
                                           1.0 / D, None, ALU.mult)
                p.add("v", r1, waits=[t_stats])
                def r2(e):
                    return e.tensor_tensor(scrap, murow, murow, ALU.mult)
                p.add("v", r2)
                def r3(e):
                    return e.scalar_tensor_tensor(sdrow, PBS[7][0:1, 0:TB],
                                                  1.0 / D, scrap, ALU.mult,
                                                  ALU.subtract)
                p.add("v", r3, inc=True)
                t_var = p.tok("v")
                def r4(e):
                    return e.activation(sdrow, sdrow, AF.Ln,
                                        bias=smw("eps")[0:1, :])
                p.add("s", r4, waits=[t_var])
                def r6(e):
                    return e.activation(rstdrow, sdrow, AF.Exp, scale=-0.5)
                p.add("s", r6, inc=True)
                t_rows = p.tok("s")
                def b1(e):
                    return e.matmul(PBS[6][:, 0:TB], c32("ones", 0, 1), murow,
                                    start=True, stop=True)
                p.add("pe", b1, waits=[t_rows])
                def b2(e):
                    return e.matmul(PBS[7][:, 0:TB], c32("ones", 0, 1),
                                    rstdrow, start=True, stop=True)
                p.add("pe", b2, inc=True)
                t_bc = p.tok("pe")
                for c in range(CH):
                    def a1(e, c=c, xfb=xfb):
                        return e.tensor_tensor(XC[:, c, :], xfb[:, c, :],
                                               PBS[6][:, 0:TB], ALU.subtract)
                    p.add("v", a1, waits=([t_bc] if c == 0 else ()))
                for c in range(CH):
                    def a2(e, c=c):
                        return e.tensor_tensor(H0[:, c, 1:TB + 1], XC[:, c, :],
                                               PBS[7][:, 0:TB], ALU.mult)
                    p.add("v", a2)
                def dxf(e):
                    return e.tensor_tensor(XC[:], H0[:, :, 1:TB + 1],
                                           H0[:, :, 0:TB], ALU.subtract)
                p.add("v", dxf)
                def pslot(bank0, oc):
                    return PBS[bank0 + oc // 2][:, (oc % 2) * TB:
                                                (oc % 2) * TB + TB]

                def mixwave(W, mixnm, dst, bank0, t_gate, mx_wait=()):
                    for c in range(CH):
                        def mx(e, c=c, dst=dst, mixnm=mixnm):
                            return e.scalar_tensor_tensor(
                                dst[:, c, :], XC[:, c, :],
                                smw(mixnm)[:, c:c + 1], H0[:, c, 0:TB],
                                ALU.mult, ALU.add)
                        p.add("v", mx, inc=(c == CH - 1),
                              waits=(mx_wait if c == 0 else ()))
                    t_mix = p.tok("v")
                    for oc in range(CH):
                        for kc in range(CH):
                            def mm(e, oc=oc, kc=kc, W=W, dst=dst, bank0=bank0):
                                return e.matmul(
                                    pslot(bank0, oc),
                                    W[:, kc, oc * 128:oc * 128 + 128],
                                    dst[:, kc, :], start=(kc == 0),
                                    stop=(kc == CH - 1))
                            w = []
                            if oc == 0 and kc == 0:
                                w = [t_mix, t_wr4]
                                if t_gate is not None:
                                    w.append(t_gate)
                            p.add("pe", mm,
                                  inc=(oc == CH - 1 and kc == CH - 1),
                                  waits=w)
                    return p.tok("pe")
                ST["mixwave"] = mixwave
                ST["pslot"] = pslot
                ST["t_kw"] = mixwave(WK, "tmk", XKk, 0, None)
                ST["t_vw"] = mixwave(WV, "tmv", XKv, 4, None)

            def emit_evkv(b):
                pslot = ST["pslot"]
                for oc in range(CH):
                    def evv(e, oc=oc):
                        return e.activation(VFB[:, oc, :], pslot(4, oc),
                                            AF.Copy, bias=0.0)
                    w = []
                    if oc == 0:
                        w = [ST["t_vw"]]
                        if ST["t_vnum"] is not None:
                            w.append(ST["t_vnum"])
                    p.add("s", evv, inc=(oc == CH - 1), waits=w)
                ST["t_vev"] = p.tok("s")
                for oc in range(CH):
                    def evk(e, oc=oc):
                        return e.activation(KF[:, oc, :], pslot(0, oc),
                                            AF.Copy, bias=0.0)
                    p.add("s", evk, inc=(oc == CH - 1),
                          waits=([ST["t_kw"]] if oc == 0 else ()))
                ST["t_kev"] = p.tok("s")

            def emit_rwave(b):
                pslot = ST["pslot"]
                ST["t_rw"] = ST["mixwave"](WR, "tmr", XKk, 0, ST["t_kev"],
                                           mx_wait=[ST["t_kw"]])
                bp = b & 1
                for oc in range(CH):
                    def evr(e, oc=oc, bp=bp):
                        return e.activation(RG2[:, bp, oc, :], pslot(0, oc),
                                            AF.Tanh, scale=0.5)
                    w = []
                    if oc == 0:
                        w = [ST["t_rw"]]
                        if ST["t_vwk_s1"] is not None:
                            w.append(ST["t_vwk_s1"])
                    p.add("s", evr, inc=(oc == CH - 1), waits=w)
                ST["t_revs"] = p.tok("s")
                def rresc(e, bp=bp):
                    return e.tensor_scalar(RG2[:, bp], RG2[:, bp], 0.5, 0.5,
                                           ALU.mult, ALU.add)
                p.add("v", rresc, inc=True, waits=[ST["t_revs"]])

            def emit_wo_ln2(bb):
                bp = bb & 1
                col = bb * TB
                xfb = XF[:, :, col:col + TB]
                pslot = ST["pslot"]
                for oc in range(CH):
                    for kc in range(CH):
                        def mo(e, oc=oc, kc=kc, bp=bp):
                            return e.matmul(
                                pslot(4, oc),
                                WO[:, kc, oc * 128:oc * 128 + 128],
                                WKVR[:, bp, kc, :], start=(kc == 0),
                                stop=(kc == CH - 1))
                        w = []
                        if oc == 0 and kc == 0:
                            w = [ST["t_wkvr_done"], ST["t_vev"]]
                            if ST["t_aev"] is not None:
                                w += [ST["t_aev"], ST["t_aevs"]]
                        p.add("pe", mo, inc=(oc == CH - 1 and kc == CH - 1),
                              waits=w)
                t_ow = p.tok("pe")
                for bk in range(4):
                    tv_bank = None
                    for half in range(2):
                        oc = bk * 2 + half
                        def ea1(e, oc=oc):
                            return e.tensor_copy(XC[:, oc, :], pslot(4, oc))
                        p.add("v", ea1,
                              waits=([t_ow] if oc == 0 else ()))
                        def ea2(e, oc=oc, bp=bp):
                            return e.tensor_copy(WKVR[:, bp, oc, :],
                                                 pslot(4, oc))
                        tv_bank = p.add("v", ea2, inc=(oc == CH - 1))
                    for half in range(2):
                        oc = bk * 2 + half
                        def ea3(e, oc=oc):
                            return e.activation(XKk[:, oc, :], pslot(4, oc),
                                                AF.Square)
                        p.add("s", ea3, inc=(oc == CH - 1),
                              waits=([(p.sem["v"], tv_bank)]
                                     if half == 0 else ()))
                ST["t_aev"] = p.tok("v")
                ST["t_aevs"] = p.tok("s")
                for c in range(CH):
                    def m0(e, c=c, bp=bp):
                        return e.matmul(PBS[6][0:1, 0:TB], cbf("ones1"),
                                        WKVR[:, bp, c, :], start=(c == 0),
                                        stop=(c == CH - 1))
                    p.add("pe", m0,
                          waits=([ST["t_aev"], ST["t_aevs"]] if c == 0 else ()))
                for c in range(CH):
                    def m1(e, c=c):
                        return e.matmul(PBS[7][0:1, 0:TB], cbf("ones1"),
                                        XKk[:, c, :], start=(c == 0),
                                        stop=(c == CH - 1))
                    p.add("pe", m1, inc=(c == CH - 1))
                t_st2 = p.tok("pe")
                def q1(e):
                    return e.tensor_scalar(murow, PBS[6][0:1, 0:TB], 1.0 / D,
                                           None, ALU.mult)
                p.add("v", q1, waits=[t_st2])
                def q2(e):
                    return e.tensor_tensor(scrap, murow, murow, ALU.mult)
                p.add("v", q2)
                def q3(e):
                    return e.scalar_tensor_tensor(sdrow, PBS[7][0:1, 0:TB],
                                                  1.0 / D, scrap, ALU.mult,
                                                  ALU.subtract)
                p.add("v", q3, inc=True)
                t_v2 = p.tok("v")
                def q4(e):
                    return e.activation(sdrow, sdrow, AF.Ln,
                                        bias=smw("eps")[0:1, :])
                p.add("s", q4, waits=[t_v2])
                def q6(e):
                    return e.activation(rstdrow, sdrow, AF.Exp, scale=-0.5)
                p.add("s", q6, inc=True)
                t_rws = p.tok("s")
                def bb1(e):
                    return e.matmul(PBS[6][:, 0:TB], c32("ones", 0, 1), murow,
                                    start=True, stop=True)
                p.add("pe", bb1, waits=[t_rws])
                def bb2(e):
                    return e.matmul(PBS[7][:, 0:TB], c32("ones", 0, 1),
                                    rstdrow, start=True, stop=True)
                p.add("pe", bb2, inc=True)
                ST["t_ln2pe"] = p.tok("pe")
                for c in range(CH):
                    def g1(e, c=c):
                        return e.tensor_tensor(XC[:, c, :], XC[:, c, :],
                                               PBS[6][:, 0:TB], ALU.subtract)
                    p.add("v", g1,
                          waits=([ST["t_ln2pe"]] if c == 0 else ()))
                for c in range(CH):
                    def g2(e, c=c):
                        return e.tensor_tensor(XKk[:, c, :], XC[:, c, :],
                                               PBS[7][:, 0:TB], ALU.mult)
                    p.add("v", g2)
                for c in range(CH):
                    def g3(e, c=c, xfb=xfb):
                        return e.tensor_tensor(xfb[:, c, :], xfb[:, c, :],
                                               XKk[:, c, :], ALU.add)
                    p.add("v", g3, inc=(c == CH - 1))
                ST["t_ln2v"] = p.tok("v")

            nb_run = DBG_NB if DBG_NB is not None else NB
            for it in range(nb_run + 2):
                if it < nb_run:
                    emit_ln1_mixk_kwave(it)
                if 1 <= it <= nb_run:
                    emit_slot(it - 1, 0)
                    emit_slot(it - 1, 1)
                if it == nb_run + 1 and ST["pend"] is not None:
                    emit_wkv(ST["pend"])
                    ST["pend"] = None
                if it < nb_run:
                    emit_evkv(it)
                    emit_rwave(it)
                if it >= 2 and it - 2 < nb_run:
                    emit_wo_ln2(it - 2)
            t_rwkv_done = ST["t_ln2v"]

            t_xsp = p.dma("y", "xsp", xsp_h, XF[:], waits=[t_rwkv_done])
            t_y0 = None
            v_apply = ST["t_ln2v"]
            t_st3 = None
            t_ap3 = None
            if stage >= 2:
                for b in range(NB):
                    col = b * TB
                    xfb = XF[:, :, col:col + TB]
                    ws3 = [v_apply] if b == 0 else [t_st3]
                    def sqr3(e, xfb=xfb):
                        return e.activation(XKk[:], xfb, AF.Square)
                    p.add("s", sqr3, inc=True, waits=ws3)
                    t_x2 = p.tok("s")
                    for c in range(CH):
                        def m0(e, c=c, xfb=xfb):
                            return e.matmul(PBS[6][0:1, 0:TB], cbf("ones1"),
                                            xfb[:, c, :], start=(c == 0),
                                            stop=(c == CH - 1))
                        w = []
                        if b == 0 and c == 0:
                            w = [v_apply, ST["t_aevs"]]
                        elif c == 0:
                            w = [t_ap3]
                        p.add("pe", m0, waits=w)
                    for c in range(CH):
                        def m1(e, c=c):
                            return e.matmul(PBS[7][0:1, 0:TB], cbf("ones1"),
                                            XKk[:, c, :], start=(c == 0),
                                            stop=(c == CH - 1))
                        p.add("pe", m1, inc=(c == CH - 1),
                              waits=([t_x2] if c == 0 else ()))
                    t_st3 = p.tok("pe")
                    def r1(e):
                        return e.tensor_scalar(murow, PBS[6][0:1, 0:TB],
                                               1.0 / D, None, ALU.mult)
                    p.add("v", r1, waits=[t_st3])
                    def r2(e):
                        return e.tensor_tensor(scrap, murow, murow, ALU.mult)
                    p.add("v", r2)
                    def r3(e):
                        return e.scalar_tensor_tensor(sdrow, PBS[7][0:1, 0:TB],
                                                      1.0 / D, scrap, ALU.mult,
                                                      ALU.subtract)
                    p.add("v", r3, inc=True)
                    t_var = p.tok("v")
                    def r4(e):
                        return e.activation(sdrow, sdrow, AF.Ln,
                                            bias=smw("eps")[0:1, :])
                    p.add("s", r4, waits=[t_var])
                    def r6(e):
                        return e.activation(rstdrow, sdrow, AF.Exp, scale=-0.5)
                    p.add("s", r6, inc=True)
                    t_rows3 = p.tok("s")
                    def b1(e):
                        return e.matmul(PBS[6][:, 0:TB], c32("ones", 0, 1),
                                        murow, start=True, stop=True)
                    p.add("pe", b1, waits=[t_rows3])
                    def b2(e):
                        return e.matmul(PBS[7][:, 0:TB], c32("ones", 0, 1),
                                        rstdrow, start=True, stop=True)
                    p.add("pe", b2, inc=True)
                    t_bc3 = p.tok("pe")
                    for c in range(CH):
                        def a1(e, c=c, xfb=xfb):
                            return e.tensor_tensor(XC[:, c, :], xfb[:, c, :],
                                                   PBS[6][:, 0:TB],
                                                   ALU.subtract)
                        p.add("v", a1, waits=([t_bc3] if c == 0 else ()))
                    for c in range(CH):
                        def a2(e, c=c):
                            return e.tensor_tensor(XKv[:, c, :], XC[:, c, :],
                                                   PBS[7][:, 0:TB], ALU.mult)
                        w = []
                        if c == 0 and t_y0 is not None:
                            w.append(t_y0)
                        p.add("v", a2, waits=w)
                    t_ap3 = p.tok("v")
                    t_y0 = p.dma("y", "ysp0",
                                 ysp_h[:, :, col:col + TB], XKv[:],
                                 waits=[t_ap3])
            t_front = [t_xsp, p.tok("v"), p.tok("s"), p.tok("pe")]
            if t_y0 is not None:
                t_front.append(t_y0)
            if dbg_dump and t_y0 is not None:
                t_front.append(p.dma("y", "dby", dby_h["0"], ysp_h,
                                     waits=[t_y0]))

        front_es.close()


        nlayers = 0 if stage < 2 else (1 if stage == 2 else 2)
        t_prev_writes = (p.dsem.get("ysp0"), p.dma_cnt.get("ysp0")) \
            if nlayers else None
        YB2 = sb("YB2", [128, 2, CH, FT], BF16)
        YLC = sb("YLC", [128, CH, 2], BF16)

        def col2d2(buf, i):
            return buf[:, :, i:i + 1].rearrange("p c n -> p (c n)")

        def emit_layer(l, t_prev_writes, t_front, t_gate_prev):
            pa = ExitStack()
            asb = lambda nm, shp, dty: pa.enter_context(
                nc.sbuf_tensor(nm + f"_l{l}", shp, dty))
            WQKV = asb("WQKV", [128, CH, 1536], BF16)
            WQSW = asb("WQSW", [128, CH, 1024], BF16)
            WOF = asb("WOF", [64, CH, 1024], BF16)
            MSK = asb("MSK", [128, 512 + 128 + 2048], BF16)
            HQ = asb("HQ", [128, CH, FT], BF16)
            KR = asb("KR", [128, 4, 1024], BF16)
            VR = asb("VR", [128, 8, 512], BF16)
            QH = asb("QH", [128, 4, FT], BF16)
            QA = asb("QA", [128, 4, FT], F32)
            QB2 = asb("QB2", [128, 4, FT], F32)
            RT = asb("RT", [128, 4, FT], F32)
            SQB = asb("SQB", [128, 4, FT], BF16)
            PRB = asb("PRB", [128, 2, 640], BF16)
            PTB = asb("PTB", [128, 2, 640], BF16)
            RRB = asb("RRB", [8, 1024], BF16)
            OH = asb("OH", [64, CH, FT], BF16)
            PA = [pa.enter_context(
                nc.psum_tensor(f"pa{i}_l{l}", [128, 512], F32))
                for i in range(4)]
            BROW = pa.enter_context(
                nc.psum_tensor(f"brow_l{l}", [128, 512], F32))
            BSA = pa.enter_context(
                nc.psum_tensor(f"bsa_l{l}", [128, 512], F32))
            PBF2 = [pa.enter_context(
                nc.psum_tensor(f"pbf{i}_l{l}", [128, 640], BF16))
                for i in range(2)]

            gate = (list(t_front) if l == 0 else list(t_gate_prev)) \
                + p.all_dma_toks()
            t_wfa = p.dma("g", f"wfa{l}", WQKV[:],
                          wqkv_h[l].rearrange("(c p) o -> p c o", p=128),
                          waits=gate)
            p.dma("g", f"wfa{l}", WQSW[:],
                  wqsw_h[l].rearrange("(c p) o -> p c o", p=128))
            p.dma("g", f"wfa{l}", WOF[:],
                  wof_h[l].rearrange("(kc p) o -> p kc o", p=64))
            m_w = 512 + 128 + 2048
            def mskload():
                o0, _ = cbf_off["maskA"]
                return cstbf_h[:, o0:o0 + m_w]
            t_wfa = p.dma("g", f"wfa{l}", MSK[:], mskload())
            mA_off, mB_off, mA0_off = 0, 512, 640

            def zKR(e):
                return e.memset(KR[:, :, 0:512], 0.0)
            p.add("g", zKR, waits=gate)
            def zVR(e):
                return e.memset(VR[:, 0:4, :], 0.0)
            p.add("g", zVR)
            t_ring0 = p.tok("g")
            def zYL(e):
                return e.tensor_scalar(col2d2(YLC, 0), smw("w"), 0.0, 0.0,
                                       ALU.mult, ALU.add)
            p.add("v", zYL, waits=gate)

            t_store_par = [None, None]
            t_ringcp = t_ring0
            for fb in range(NF):
                col = fb * FT
                ypar = fb % 2
                YBF = YB2[:, ypar]
                w_l = (list(t_prev_writes)
                       if isinstance(t_prev_writes, list)
                       else [t_prev_writes])
                if t_store_par[ypar] is not None:
                    w_l.append(t_store_par[ypar])
                if fb == 0:
                    w_l.extend(gate)
                t_yl = p.dma("y", f"ya{l}", YBF[:],
                             ysp_h[:, :, col:col + FT], waits=w_l)
                t_rt = p.dma("y", f"rt{l}{fb % 2}", RT[:],
                             rot_h[:, :, col:col + FT].rearrange(
                                 "r p t -> p r t"),
                             waits=[(p.sem["v"], p.cnt["v"])])
                def hq1(e, YBF=YBF):
                    return e.tensor_copy(HQ[:, 0:4, :], YBF[:, 0:4, :])
                p.add("v", hq1, waits=[t_yl])
                def hq2(e, YBF=YBF):
                    return e.tensor_copy(HQ[:, 4:8, 1:FT],
                                         YBF[:, 4:8, 0:FT - 1])
                p.add("v", hq2)
                def hq3(e):
                    return e.tensor_copy(HQ[:, 4:8, 0:1], YLC[:, 4:8, 0:1])
                p.add("v", hq3)
                def ylup(e, YBF=YBF):
                    return e.tensor_copy(col2d2(YLC, 0),
                                         YBF[:, :, FT - 1:FT]
                                         .rearrange("p c n -> p (c n)"))
                p.add("v", ylup)
                t_hq = p.tok("v")

                for tc in range(4):
                    for kc in range(CH):
                        def mv(e, tc=tc, kc=kc):
                            return e.matmul(
                                PA[tc][:], HQ[:, kc, tc * 128:tc * 128 + 128],
                                WQKV[:, kc, 1024:1536],
                                start=(kc == 0), stop=(kc == CH - 1))
                        w = []
                        if tc == 0 and kc == 0:
                            w = [t_hq, t_wfa, t_ring0]
                        p.add("pe", mv, inc=(tc == 3 and kc == CH - 1),
                              waits=w)
                t_vw = p.tok("pe")
                for tc in range(4):
                    def ev(e, tc=tc):
                        return e.tensor_copy(VR[:, 4 + tc, :], PA[tc][:])
                    w = []
                    if tc == 0:
                        w = [t_vw, t_ringcp]
                    p.add("v", ev, waits=w)
                t_vev = p.tok("v")

                def wave(dst_off, swapped, waits0):
                    W_, cols = (WQSW, (dst_off, dst_off + 512)) if swapped \
                        else (WQKV, (dst_off, dst_off + 512))
                    for oc in range(4):
                        for kc in range(CH):
                            def mm(e, oc=oc, kc=kc, W_=W_, cols=cols):
                                return e.matmul(
                                    PA[oc][:],
                                    W_[:, kc, cols[0] + oc * 128:
                                       cols[0] + oc * 128 + 128],
                                    HQ[:, kc, :],
                                    start=(kc == 0), stop=(kc == CH - 1))
                            w = list(waits0) if (oc == 0 and kc == 0) else []
                            p.add("pe", mm,
                                  inc=(oc == 3 and kc == CH - 1), waits=w)
                    return p.tok("pe")

                def qk_pipeline(dst_off, qsnm, qswnm, rows_cols, rt_cs, rt_sn,
                                out_to_kr):
                    t_w = wave(dst_off, False, [p.tok("v"), p.tok("s")])
                    for c in range(4):
                        def sq(e, c=c):
                            return e.activation(SQB[:, c, :], PA[c][:],
                                                AF.Square)
                        p.add("s", sq, waits=([t_w] if c == 0 else ()))
                    t_sq = p.tok("s")
                    for c in range(4):
                        def qa(e, c=c, rt_cs=rt_cs, qsnm=qsnm):
                            return e.scalar_tensor_tensor(
                                QA[:, c, :], PA[c][:], smw(qsnm),
                                RT[:, rt_cs, :], ALU.mult, ALU.mult)
                        p.add("v", qa, waits=([t_sq, t_rt] if c == 0 else ()))
                    t_qa = p.tok("v")
                    for c in range(4):
                        def sm(e, c=c):
                            return e.matmul(
                                BROW[0:8, 0:512],
                                cbf("hsum")[:, c * 8:c * 8 + 8],
                                SQB[:, c, :], start=(c == 0), stop=(c == 3))
                        p.add("pe", sm, inc=(c == 3),
                              waits=([t_sq] if c == 0 else ()))
                    t_sums = p.tok("pe")
                    t_w2 = wave(dst_off, True, [t_qa])
                    def rc(e, rows_cols=rows_cols):
                        return e.tensor_copy(
                            RRF[0:8, rows_cols:rows_cols + 512],
                            BROW[0:8, 0:512])
                    p.add("v", rc, waits=[t_sums])
                    def rm(e, rows_cols=rows_cols):
                        return e.tensor_scalar_max(
                            RRF[0:8, rows_cols:rows_cols + 512],
                            RRF[0:8, rows_cols:rows_cols + 512], 1e-24)
                    p.add("v", rm)
                    t_rv = p.tok("v")
                    def rl(e, rows_cols=rows_cols):
                        return e.activation(
                            RRF[0:8, rows_cols:rows_cols + 512],
                            RRF[0:8, rows_cols:rows_cols + 512], AF.Ln)
                    p.add("s", rl, waits=[t_rv])
                    def re_(e, rows_cols=rows_cols):
                        return e.activation(
                            RRB[0:8, rows_cols:rows_cols + 512],
                            RRF[0:8, rows_cols:rows_cols + 512], AF.Exp,
                            scale=-0.5)
                    p.add("s", re_)
                    t_rr = p.tok("s")
                    for c in range(4):
                        def qb(e, c=c, rt_sn=rt_sn, qswnm=qswnm):
                            return e.scalar_tensor_tensor(
                                QB2[:, c, :], PA[c][:], smw(qswnm),
                                RT[:, rt_sn, :], ALU.mult, ALU.mult)
                        p.add("v", qb, waits=([t_w2] if c == 0 else ()))
                    t_qb = p.tok("v")
                    def gadd(e):
                        return e.tensor_tensor(QA[:], QA[:], QB2[:], ALU.add)
                    p.add("v", gadd, waits=[t_qb])
                    t_g = p.tok("v")
                    for c in range(4):
                        def bc(e, c=c, rows_cols=rows_cols):
                            return e.matmul(
                                PA[c][:, 0:512],
                                cbf("hbc", 0, 8)[:, c * 128:c * 128 + 128],
                                RRB[0:8, rows_cols:rows_cols + 512],
                                start=True, stop=True)
                        w = [t_rr, t_g, t_qb] if c == 0 else []
                        p.add("pe", bc, inc=(c == 3), waits=w)
                    t_bcq = p.tok("pe")
                    for c in range(4):
                        if out_to_kr:
                            def fm(e, c=c):
                                return e.tensor_tensor(
                                    KR[:, c, 512:1024], QA[:, c, :],
                                    PA[c][:, 0:512], ALU.mult)
                        else:
                            def fm(e, c=c):
                                return e.tensor_tensor(
                                    QH[:, c, :], QA[:, c, :],
                                    PA[c][:, 0:512], ALU.mult)
                        p.add("v", fm, waits=([t_bcq] if c == 0 else ()))
                    return p.tok("v")

                t_qh = qk_pipeline(0, f"qs{l}", f"qsw{l}", 0, 0, 1, False)
                t_kh = qk_pipeline(512, f"ks{l}", f"ksw{l}", 512, 2, 3, True)

                its = [(h, qc) for h in range(HEADS) for qc in range(4)]
                t_ex_par = [None, None]
                t_pv_par = [None, None]
                t_ptv_par = [None, None]
                t_oev = None
                st_pn = {}
                st_tp = {}

                def emit_scores(it):
                    h, qc = its[it]
                    par = it % 2
                    w0 = 128 * qc
                    pq = (h % 2) * 64
                    tcp = h // 2
                    sA = PA[par]
                    sB = PA[2 + par]
                    def s1m(e, pq=pq, tcp=tcp, w0=w0, sA=sA):
                        return e.matmul(
                            sA[:, 0:512],
                            QH[pq:pq + 64, tcp, w0:w0 + 128],
                            KR[pq:pq + 64, tcp, w0:w0 + 512],
                            start=True, stop=False)
                    w = [t_qh, t_kh] if it == 0 else []
                    if t_ex_par[par] is not None:
                        w = w + [t_ex_par[par]]
                    p.add("pe", s1m, waits=w)
                    if fb == 0:
                        mA = MSK[:, mA0_off + 512 * qc:mA0_off + 512 * qc + 512]
                    else:
                        mA = MSK[:, mA_off:mA_off + 512]
                    def s1k(e, mA=mA, sA=sA):
                        return e.matmul(sA[:, 0:512], cbf("ident"), mA,
                                        start=False, stop=True)
                    p.add("pe", s1k)
                    def s2m(e, pq=pq, tcp=tcp, w0=w0, sB=sB):
                        return e.matmul(
                            sB[:, 0:128],
                            QH[pq:pq + 64, tcp, w0:w0 + 128],
                            KR[pq:pq + 64, tcp, w0 + 512:w0 + 640],
                            start=True, stop=False)
                    p.add("pe", s2m)
                    def s2k(e, sB=sB):
                        return e.matmul(sB[:, 0:128], cbf("ident"),
                                        MSK[:, mB_off:mB_off + 128],
                                        start=False, stop=True)
                    p.add("pe", s2k, inc=True)
                    return p.tok("pe")

                st_ex = {}

                def emit_vmex(it, t_sc):
                    par = it % 2
                    sA = PA[par]
                    sB = PA[2 + par]
                    so = 1536 + 6 * par
                    nmA = RRF[:, so:so + 1]
                    nmB = RRF[:, so + 1:so + 2]
                    nm = RRF[:, so + 2:so + 3]
                    s1c = RRF[:, so + 3:so + 4]
                    s2c = RRF[:, so + 4:so + 5]
                    def vm1(e, sA=sA, nmA=nmA):
                        return e.reduce_max(nmA, sA[:, 0:512], axis=AX.X,
                                            negate=True)
                    p.add("v", vm1, waits=[t_sc])
                    def vm2(e, sB=sB, nmB=nmB):
                        return e.reduce_max(nmB, sB[:, 0:128], axis=AX.X,
                                            negate=True)
                    p.add("v", vm2)
                    def vm3(e, nm=nm, nmA=nmA, nmB=nmB):
                        return e.tensor_tensor(nm, nmA, nmB, ALU.min)
                    p.add("v", vm3)
                    t_nm = p.tok("v")
                    def ex1(e, sA=sA, nm=nm, s1c=s1c, par=par):
                        return e.activation(PRB[:, par, 0:512], sA[:, 0:512],
                                            AF.Exp, bias=nm, accum_out=s1c)
                    p.add("s", ex1, waits=[t_nm])
                    def ex2(e, sB=sB, nm=nm, s2c=s2c, par=par):
                        return e.activation(PRB[:, par, 512:640],
                                            sB[:, 0:128], AF.Exp,
                                            bias=nm, accum_out=s2c)
                    p.add("s", ex2, inc=True)
                    st_ex[it] = p.tok("s")
                    t_ex_par[par] = st_ex[it]

                def emit_vnorm(it):
                    par = it % 2
                    so = 1536 + 6 * par
                    s1c = RRF[:, so + 3:so + 4]
                    s2c = RRF[:, so + 4:so + 5]
                    def vn1(e, s1c=s1c, s2c=s2c):
                        return e.tensor_tensor(s1c, s1c, s2c, ALU.add)
                    p.add("v", vn1, waits=[st_ex[it]])
                    def vn2(e, s1c=s1c):
                        return e.reciprocal(s1c, s1c)
                    p.add("v", vn2)
                    def vn3(e, s1c=s1c, par=par):
                        return e.tensor_scalar(PRB[:, par, :], PRB[:, par, :],
                                               s1c, None, ALU.mult)
                    p.add("v", vn3)
                    st_pn[it] = p.tok("v")

                def emit_tp(it):
                    par = it % 2
                    for kk in range(5):
                        def tp(e, kk=kk, par=par):
                            return e.transpose(
                                PBF2[par][:, kk * 128:kk * 128 + 128],
                                PRB[:, par, kk * 128:kk * 128 + 128],
                                cbf("ident"))
                        w = [st_pn[it]] if kk == 0 else []
                        if kk == 0 and t_ptv_par[par] is not None:
                            w.append(t_ptv_par[par])
                        p.add("pe", tp, inc=(kk == 4), waits=w)
                    st_tp[it] = p.tok("pe")

                def emit_ptv(it):
                    par = it % 2
                    def ptv(e, par=par):
                        return e.activation(PTB[:, par, :], PBF2[par][:],
                                            AF.Copy)
                    w = [st_tp[it]]
                    if t_pv_par[par] is not None:
                        w.append(t_pv_par[par])
                    t = (p.sem["s"], p.add("s", ptv, waits=w))
                    t_ptv_par[par] = t
                    return t

                def emit_pv(it, t_ptv):
                    nonlocal t_oev
                    h, qc = its[it]
                    par = it % 2
                    for kk in range(5):
                        def pv(e, kk=kk, h=h, qc=qc, par=par):
                            return e.matmul(
                                BSA[0:64, 0:128],
                                VR[:, qc + kk, h * 64:h * 64 + 64],
                                PTB[:, par, kk * 128:kk * 128 + 128],
                                start=(kk == 0), stop=(kk == 4))
                        w = [t_ptv] if kk == 0 else []
                        if kk == 0 and t_oev is not None:
                            w.append(t_oev)
                        p.add("pe", pv, inc=(kk == 4), waits=w)
                    t_pv = p.tok("pe")
                    t_pv_par[par] = t_pv
                    def oev(e, h=h, qc=qc):
                        return e.tensor_copy(
                            OH[0:64, h, qc * 128:qc * 128 + 128],
                            BSA[0:64, 0:128])
                    t_oev = (p.sem["v"], p.add("v", oev, waits=[t_pv]))

                sc_tok = {}
                ptv_tok = {}
                n_its = len(its)
                for it in range(n_its + 3):
                    if it < n_its:
                        sc_tok[it] = emit_scores(it)
                    if 1 <= it <= n_its:
                        emit_vmex(it - 1, sc_tok[it - 1])
                        emit_vnorm(it - 1)
                    if 2 <= it <= n_its + 1:
                        emit_tp(it - 2)
                        ptv_tok[it - 2] = emit_ptv(it - 2)
                    if 3 <= it <= n_its + 2:
                        emit_pv(it - 3, ptv_tok[it - 3])
                t_oh = p.tok("v")

                for half in range(2):
                    for oc4 in range(4):
                        oc = half * 4 + oc4
                        for kc in range(CH):
                            def mo(e, oc=oc, oc4=oc4, kc=kc):
                                return e.matmul(
                                    PA[oc4][:],
                                    WOF[0:64, kc, oc * 128:oc * 128 + 128],
                                    OH[0:64, kc, :], start=(kc == 0),
                                    stop=(kc == CH - 1))
                            w = []
                            if oc4 == 0 and kc == 0:
                                w = [t_oh] if half == 0 else [p.tok("v")]
                            p.add("pe", mo, inc=(oc4 == 3 and kc == CH - 1),
                                  waits=w)
                    t_wo = p.tok("pe")
                    for oc4 in range(4):
                        oc = half * 4 + oc4
                        def yup(e, oc=oc, oc4=oc4, YBF=YBF):
                            return e.tensor_tensor(YBF[:, oc, :],
                                                   YBF[:, oc, :],
                                                   PA[oc4][:], ALU.add)
                        p.add("v", yup, waits=([t_wo] if oc4 == 0 else ()))
                t_yb = p.tok("v")
                t_store_par[ypar] = p.dma("y", f"yw{l}a{ypar}",
                                          ysp_h[:, :, col:col + FT], YBF[:],
                                          waits=[t_yb])
                t_pe_rb = p.tok("pe")
                p.dma("y", f"rg{l}", KR[:, :, 0:512], KR[:, :, 512:1024],
                      waits=[t_pe_rb])
                t_ringcp = p.dma("y", f"rg{l}", VR[:, 0:4, :], VR[:, 4:8, :])
            t_pa_done = p.tok("pe")
            t_prev_writes = [(p.dsem[f"yw{l}a0"], p.dma_cnt[f"yw{l}a0"]),
                             (p.dsem[f"yw{l}a1"], p.dma_cnt[f"yw{l}a1"])]
            if dbg_dump:
                t_prev_writes2 = p.dma("y", "dby", dby_h[f"a{l}"], ysp_h,
                                       waits=list(t_prev_writes))
                t_prev_writes = [t_prev_writes2]
            pa.close()

            pb = ExitStack()
            bsb = lambda nm, shp, dty: pb.enter_context(
                nc.sbuf_tensor(nm + f"_l{l}", shp, dty))
            W1I = bsb("W1I", [128, CH, NSLOT * 128], BF16)
            W2P = bsb("W2P", [128, NJ, 1024], BF16)
            HACT = bsb("HACT", [128, NJ, FT], BF16)
            HB = bsb("HB", [128, CH, FT], BF16)
            SQ2 = bsb("SQ2", [128, CH, FT], BF16)
            TMP1 = bsb("TMP1", [128, FT], F32)
            PB_ = [pb.enter_context(
                nc.psum_tensor(f"pb{i}_l{l}", [128, 512], F32))
                for i in range(4)]
            PBA = pb.enter_context(
                nc.psum_tensor(f"pba_l{l}", [128, 512], F32))
            PBG = pb.enter_context(
                nc.psum_tensor(f"pbg_l{l}", [128, 512], F32))
            PR0 = pb.enter_context(
                nc.psum_tensor(f"pr0_l{l}", [128, 512], F32))
            PR1 = pb.enter_context(
                nc.psum_tensor(f"pr1_l{l}", [128, 512], F32))

            w1r = w1i_h[l].rearrange("(c p) s -> p c s", p=128)
            gateb = [t_pa_done, p.tok("v"), p.tok("s"),
                     p.tok("g")] + p.all_dma_toks()
            W1GRP = 4
            wfb_grp = []
            for jj in range(0, NJ, W1GRP):
                je = min(jj + W1GRP, NJ)
                tk = p.dma("g", f"wfb{l}g{jj // W1GRP}",
                           W1I[:, :, 2 * jj * 128:2 * je * 128],
                           w1r[:, :, 2 * jj * 128:2 * je * 128],
                           waits=(gateb if jj == 0 else ()))
                wfb_grp.append(tk)
            w2r = w2p_h[l].rearrange("(j p) o -> p j o", p=128)
            for jj in range(0, NJ, 4):
                je = min(jj + 4, NJ)
                t_wfb = p.dma("g", f"wfb{l}", W2P[:, jj:je, :],
                              w2r[:, jj:je, :])
            t_wfb_w2p = t_wfb

            def zYL2(e):
                return e.tensor_scalar(col2d2(YLC, 1), smw("w"), 0.0, 0.0,
                                       ALU.mult, ALU.add)
            p.add("v", zYL2)

            t_store_par = [None, None]
            for fb in range(NF):
                col = fb * FT
                ypar = fb % 2
                YBF = YB2[:, ypar]
                w_l = (list(t_prev_writes)
                       if isinstance(t_prev_writes, list)
                       else [t_prev_writes])
                if t_store_par[ypar] is not None:
                    w_l.append(t_store_par[ypar])
                t_yl = p.dma("y", f"yb{l}", YBF[:],
                             ysp_h[:, :, col:col + FT], waits=w_l)
                def c1(e, YBF=YBF):
                    return e.tensor_copy(HB[:, 0:4, :], YBF[:, 0:4, :])
                p.add("v", c1, waits=[t_yl])
                def c2(e, YBF=YBF):
                    return e.tensor_copy(HB[:, 4:8, 1:FT],
                                         YBF[:, 4:8, 0:FT - 1])
                p.add("v", c2)
                def c3(e):
                    return e.tensor_copy(HB[:, 4:8, 0:1], YLC[:, 4:8, 1:2])
                p.add("v", c3)
                t_cb = p.tok("v")
                def sq1(e, YBF=YBF):
                    return e.activation(SQ2[:, 0:4, :], YBF[:, 0:4, :],
                                        AF.Square)
                p.add("s", sq1, waits=[t_yl, (p.sem["v"], p.cnt["v"])])
                def sq2_(e, YBF=YBF):
                    return e.activation(SQ2[:, 4:8, 1:FT],
                                        YBF[:, 4:8, 0:FT - 1], AF.Square)
                p.add("s", sq2_)
                def sq3(e):
                    return e.activation(SQ2[:, 4:8, 0:1], YLC[:, 4:8, 1:2],
                                        AF.Square)
                p.add("s", sq3)
                t_sq = p.tok("s")
                for c in range(CH):
                    def m0(e, c=c):
                        return e.matmul(PR0[0:1, 0:FT], cbf("ones1"),
                                        HB[:, c, :], start=(c == 0),
                                        stop=(c == CH - 1))
                    p.add("pe", m0, waits=([t_cb] if c == 0 else ()))
                for c in range(CH):
                    def m1(e, c=c):
                        return e.matmul(PR1[0:1, 0:FT], cbf("ones1"),
                                        SQ2[:, c, :], start=(c == 0),
                                        stop=(c == CH - 1))
                    p.add("pe", m1, inc=(c == CH - 1),
                          waits=([t_sq] if c == 0 else ()))
                t_st = p.tok("pe")
                mrow = RRF[0:1, 1024:1024 + FT]
                srow = RRF[0:1, 0:FT]
                xrow = RRF[0:1, 512:512 + FT]
                def q1(e):
                    return e.tensor_scalar(mrow, PR0[0:1, 0:FT], 1.0 / D,
                                           None, ALU.mult)
                p.add("v", q1, waits=[t_st])
                def q2(e):
                    return e.tensor_tensor(xrow, mrow, mrow, ALU.mult)
                p.add("v", q2)
                def q3(e):
                    return e.scalar_tensor_tensor(srow, PR1[0:1, 0:FT],
                                                  1.0 / D, xrow, ALU.mult,
                                                  ALU.subtract)
                p.add("v", q3)
                t_v2 = p.tok("v")
                def q4(e):
                    return e.activation(srow, srow, AF.Ln,
                                        bias=smw("eps")[0:1, :])
                p.add("s", q4, waits=[t_v2])
                def q6(e):
                    return e.activation(srow, srow, AF.Exp, scale=-0.5)
                p.add("s", q6)
                t_rw = p.tok("s")
                def bb1(e):
                    return e.matmul(PR0[:, 0:FT], c32("ones", 0, 1), mrow,
                                    start=True, stop=True)
                p.add("pe", bb1, waits=[t_rw])
                def bb2(e):
                    return e.matmul(PR1[:, 0:FT], c32("ones", 0, 1), srow,
                                    start=True, stop=True)
                p.add("pe", bb2, inc=True)
                t_bc = p.tok("pe")
                for c in range(CH):
                    if c < 4:
                        def ap1(e, c=c, YBF=YBF):
                            return e.tensor_tensor(TMP1[:], YBF[:, c, :],
                                                   PR0[:, 0:FT],
                                                   ALU.subtract)
                        p.add("v", ap1, waits=([t_bc] if c == 0 else ()))
                    else:
                        def ap1(e, c=c, YBF=YBF):
                            return e.tensor_tensor(TMP1[:, 1:FT],
                                                   YBF[:, c, 0:FT - 1],
                                                   PR0[:, 1:FT],
                                                   ALU.subtract)
                        p.add("v", ap1, waits=([t_bc] if c == 0 else ()))
                        def ap1b(e, c=c):
                            return e.tensor_tensor(TMP1[:, 0:1],
                                                   YLC[:, c, 1:2],
                                                   PR0[:, 0:1], ALU.subtract)
                        p.add("v", ap1b)
                    def ap2(e, c=c):
                        return e.scalar_tensor_tensor(
                            TMP1[:], TMP1[:], smw(f"lng{l}")[:, c:c + 1],
                            PR1[:, 0:FT], ALU.mult, ALU.mult)
                    p.add("v", ap2)
                    def ap3(e, c=c):
                        return e.tensor_scalar(HB[:, c, :], TMP1[:],
                                               smw(f"lnb{l}")[:, c:c + 1],
                                               None, ALU.add)
                    p.add("v", ap3)
                t_h1 = p.tok("v")
                if dbg_dump and l == 0 and fb == 0:
                    t_dbhb = p.dma("g", "dbg2", dbhb_h, HB[:], waits=[t_h1])
                    t_dw1 = p.dma("g", "dbg2", dbw1_h, W1I[:],
                                  waits=[t_h1, t_wfb])
                    p.add("v", lambda e: e.engine_nop(), waits=[t_dw1])
                def ylup2(e, YBF=YBF):
                    return e.tensor_copy(col2d2(YLC, 1),
                                         YBF[:, :, FT - 1:FT]
                                         .rearrange("p c n -> p (c n)"))
                p.add("v", ylup2, waits=[t_sq])
                t_evj = [None, None]
                for j in range(NJ):
                    par = j % 2
                    bA = PBA if par == 0 else PB_[0]
                    bG = PBG if par == 0 else PB_[1]
                    cA, cG = (1, 0) if par == 0 else (3, 2)
                    for kc in range(CH):
                        def ma(e, j=j, kc=kc, bA=bA):
                            return e.matmul(
                                bA[:], W1I[:, kc,
                                           (2 * j) * 128:(2 * j) * 128 + 128],
                                HB[:, kc, :], start=(kc == 0),
                                stop=(kc == CH - 1))
                        w = []
                        if kc == 0:
                            w = [t_h1] if j == 0 else (
                                [t_evj[par]] if t_evj[par] else [])
                            if fb == 0 and j % W1GRP == 0:
                                w = list(w) + [wfb_grp[j // W1GRP]]
                        p.add("pe", ma, waits=w)
                    for kc in range(CH):
                        def mg(e, j=j, kc=kc, bG=bG):
                            return e.matmul(
                                bG[:], W1I[:, kc,
                                           (2 * j + 1) * 128:
                                           (2 * j + 1) * 128 + 128],
                                HB[:, kc, :], start=(kc == 0),
                                stop=(kc == CH - 1))
                        p.add("pe", mg, inc=(kc == CH - 1))
                    t_j = p.tok("pe")
                    def gg(e, j=j, bG=bG, cG=cG):
                        return e.activation(SQ2[:, cG, 0:FT], bG[:, 0:FT],
                                            AF.Gelu,
                                            bias=smw(f"b1i{l}")
                                            [:, 2 * j + 1:2 * j + 2])
                    p.add("s", gg, waits=[t_j])
                    def aa(e, j=j, bA=bA, cA=cA):
                        return e.tensor_scalar(SQ2[:, cA, 0:FT], bA[:, 0:FT],
                                               smw(f"b1i{l}")
                                               [:, 2 * j:2 * j + 1],
                                               None, ALU.add)
                    p.add("v", aa, waits=[t_j])
                    t_ag = p.tok("s")
                    def hmul(e, j=j, cA=cA, cG=cG):
                        return e.tensor_tensor(HACT[:, j, :], SQ2[:, cA, 0:FT],
                                               SQ2[:, cG, 0:FT], ALU.mult)
                    t_evj[par] = (p.sem["v"], p.add("v", hmul, waits=[t_ag]))
                t_hact = p.tok("v")
                if dbg_dump and l == 0 and fb == 0:
                    t_dbha = p.dma("g", "dbg2", dbha_h, HACT[:],
                                   waits=[t_hact])
                    p.add("v", lambda e: e.engine_nop(),
                          waits=[t_dbha])
                for half in range(2):
                    for oc4 in range(4):
                        oc = half * 4 + oc4
                        for j in range(NJ):
                            def m2(e, oc=oc, oc4=oc4, j=j):
                                return e.matmul(
                                    PB_[oc4][:],
                                    W2P[:, j, oc * 128:oc * 128 + 128],
                                    HACT[:, j, :], start=(j == 0),
                                    stop=(j == NJ - 1))
                            w = []
                            if oc4 == 0 and j == 0:
                                w = [t_hact] if half == 0 else [p.tok("v")]
                                if half == 0 and fb == 0:
                                    w.append(t_wfb_w2p)
                            p.add("pe", m2,
                                  inc=(oc4 == 3 and j == NJ - 1), waits=w)
                    t_w2 = p.tok("pe")
                    for oc4 in range(4):
                        oc = half * 4 + oc4
                        def yup(e, oc=oc, oc4=oc4, YBF=YBF):
                            return e.scalar_tensor_tensor(
                                YBF[:, oc, :], PB_[oc4][:],
                                smw(f"b2_{l}")[:, oc:oc + 1], YBF[:, oc, :],
                                ALU.add, ALU.add)
                        p.add("v", yup, waits=([t_w2] if oc4 == 0 else ()))
                t_yb2 = p.tok("v")
                t_store_par[ypar] = p.dma("y", f"yw{l}b{ypar}",
                                          ysp_h[:, :, col:col + FT], YBF[:],
                                          waits=[t_yb2])
            t_pb_done = p.tok("pe")
            t_prev_writes = [(p.dsem[f"yw{l}b0"], p.dma_cnt[f"yw{l}b0"]),
                             (p.dsem[f"yw{l}b1"], p.dma_cnt[f"yw{l}b1"])]
            if dbg_dump:
                t_prev_writes = [p.dma("y", "dby", dby_h[f"b{l}"], ysp_h,
                                       waits=list(t_prev_writes))]
            t_gate_prev = [t_pb_done, p.tok("v"), p.tok("s"), p.tok("g")]
            pb.close()
            return t_prev_writes, t_gate_prev

        t_gate_prev = None
        for l in range(nlayers):
            t_prev_writes, t_gate_prev = emit_layer(
                l, t_prev_writes, t_front, t_gate_prev)

        with nc.sbuf_tensor("X2", [128, 2, CH, FT], BF16) as X2, \
             nc.sbuf_tensor("XF2", [128, 2, CH, FT], F32) as XF2, \
             nc.sbuf_tensor("Y2", [128, 2, CH, FT], BF16) as Y2, \
             nc.sbuf_tensor("CB2", [128, 2, CH, FT], BF16) as CB2, \
             nc.sbuf_tensor("SQ2f", [128, 2, CH, FT], BF16) as SQ2f, \
             nc.sbuf_tensor("TMP4", [128, FT], F32) as TMP4, \
             nc.sbuf_tensor("RW4", [1, 3072], F32) as RW4, \
             nc.sbuf_tensor("OT", [128, 4, D], F32) as OT, \
             nc.psum_tensor("PF0", [128, 512], F32) as PF0, \
             nc.psum_tensor("PF1", [128, 512], F32) as PF1, \
             nc.psum_tensor("FR0", [128, 512], F32) as FR0, \
             nc.psum_tensor("FR1", [128, 512], F32) as FR1, \
             nc.psum_tensor("FR2", [128, 512], F32) as FR2, \
             nc.psum_tensor("FR3", [128, 512], F32) as FR3:
            FRS = [FR0, FR1, FR2, FR3]
            pt_evtok = {0: None, 1: None}
            out_dma_tok = {tc: None for tc in range(4)}
            vgate_par = [None, None]
            tp_par = [None, None]
            ffb_par = [None, None]
            if stage < 2:
                t_front = [t_xsp, p.tok("v"), p.tok("s"), p.tok("pe")]
                t_prev_writes = None
            for fb in range(NF):
                col = fb * FT
                par = fb % 2
                XO = X2[:, par]
                XOF = XF2[:, par]
                YF4 = Y2[:, par]
                CB4 = CB2[:, par]
                SQ4 = SQ2f[:, par]
                FRa = FRS[2 * par]
                FRb = FRS[2 * par + 1]
                mrow = RW4[0:1, par * 1536:par * 1536 + FT]
                srow = RW4[0:1, par * 1536 + 512:par * 1536 + 512 + FT]
                xrow = RW4[0:1, par * 1536 + 1024:par * 1536 + 1024 + FT]
                w_lx = [t_xsp] + (list(t_front[1:2]) if fb == 0 else [])
                if vgate_par[par] is not None:
                    w_lx.append(vgate_par[par])
                if fb == 0:
                    w_lx.append((p.sem["v"], p.cnt["v"]))
                    w_lx.append((p.sem["s"], p.cnt["s"]))
                    w_lx.append(p.tok("pe"))
                    w_lx.extend(p.all_dma_toks())
                t_lx = p.dma("y", f"xo{par}", XO[:],
                             xsp_h[:, :, col:col + FT], waits=w_lx)
                t_vgate = None
                if stage >= 2:
                    wly = (list(t_prev_writes)
                           if isinstance(t_prev_writes, list)
                           else [t_prev_writes])
                    if vgate_par[par] is not None:
                        wly.append(vgate_par[par])
                    t_ly = p.dma("y", f"yf4{par}", YF4[:],
                                 ysp_h[:, :, col:col + FT],
                                 waits=wly + [t_lx])
                    def c4a(e, CB4=CB4, YF4=YF4):
                        return e.tensor_copy(CB4[:], YF4[:])
                    p.add("v", c4a, waits=[t_ly])
                    t_c4 = p.tok("v")
                    def s4a(e, SQ4=SQ4, YF4=YF4):
                        return e.activation(SQ4[:], YF4[:], AF.Square)
                    p.add("s", s4a, waits=[t_ly, (p.sem["v"], p.cnt["v"])])
                    t_s4 = p.tok("s")
                    for c in range(CH):
                        def m0(e, c=c, FRa=FRa, CB4=CB4):
                            return e.matmul(FRa[0:1, 0:FT], cbf("ones1"),
                                            CB4[:, c, :], start=(c == 0),
                                            stop=(c == CH - 1))
                        w = []
                        if c == 0:
                            w = [t_c4]
                            if ffb_par[par] is not None:
                                w.append(ffb_par[par])
                        p.add("pe", m0, waits=w)
                    for c in range(CH):
                        def m1(e, c=c, FRb=FRb, SQ4=SQ4):
                            return e.matmul(FRb[0:1, 0:FT], cbf("ones1"),
                                            SQ4[:, c, :], start=(c == 0),
                                            stop=(c == CH - 1))
                        p.add("pe", m1, inc=(c == CH - 1),
                              waits=([t_s4] if c == 0 else ()))
                    t_st4 = p.tok("pe")
                    def q1(e, mrow=mrow, FRa=FRa):
                        return e.tensor_scalar(mrow, FRa[0:1, 0:FT], 1.0 / D,
                                               None, ALU.mult)
                    p.add("v", q1, waits=[t_st4])
                    def q2(e, mrow=mrow, xrow=xrow):
                        return e.tensor_tensor(xrow, mrow, mrow, ALU.mult)
                    p.add("v", q2)
                    def q3(e, srow=srow, xrow=xrow, FRb=FRb):
                        return e.scalar_tensor_tensor(srow, FRb[0:1, 0:FT],
                                                      1.0 / D, xrow,
                                                      ALU.mult, ALU.subtract)
                    p.add("v", q3)
                    t_v4 = p.tok("v")
                    def q4(e, srow=srow):
                        return e.activation(srow, srow, AF.Ln,
                                            bias=smw("eps")[0:1, :])
                    p.add("s", q4, waits=[t_v4])
                    def q6(e, srow=srow):
                        return e.activation(srow, srow, AF.Exp, scale=-0.5)
                    p.add("s", q6)
                    t_r4 = p.tok("s")
                    def bb1(e, mrow=mrow, FRa=FRa):
                        return e.matmul(FRa[:, 0:FT], c32("ones", 0, 1),
                                        mrow, start=True, stop=True)
                    p.add("pe", bb1, waits=[t_r4])
                    def bb2(e, srow=srow, FRb=FRb):
                        return e.matmul(FRb[:, 0:FT], c32("ones", 0, 1),
                                        srow, start=True, stop=True)
                    p.add("pe", bb2, inc=True)
                    t_bc4 = p.tok("pe")
                    for c in range(CH):
                        def f1(e, c=c, YF4=YF4, FRa=FRa):
                            return e.tensor_tensor(TMP4[:], YF4[:, c, :],
                                                   FRa[:, 0:FT],
                                                   ALU.subtract)
                        w = [t_bc4, t_lx] if c == 0 else []
                        if c == 0 and tp_par[par] is not None:
                            w.append(tp_par[par])
                        p.add("v", f1, waits=w)
                        def f2(e, c=c, FRb=FRb):
                            return e.tensor_tensor(TMP4[:], TMP4[:],
                                                   FRb[:, 0:FT], ALU.mult)
                        p.add("v", f2)
                        def f3(e, c=c, XOF=XOF, XO=XO):
                            return e.tensor_tensor(XOF[:, c, :], XO[:, c, :],
                                                   TMP4[:], ALU.add)
                        p.add("v", f3, inc=(c == CH - 1))
                    t_vgate = p.tok("v")
                    ffb_par[par] = t_vgate
                else:
                    def cpx(e, XOF=XOF, XO=XO):
                        return e.tensor_copy(XOF[:], XO[:])
                    wc = [t_lx]
                    if tp_par[par] is not None:
                        wc.append(tp_par[par])
                    p.add("v", cpx, inc=True, waits=wc)
                    t_vgate = p.tok("v")
                vgate_par[par] = t_vgate
                for tc in range(4):
                    for half, PT in ((0, PF0), (1, PF1)):
                        for cc in range(4):
                            c = half * 4 + cc
                            def tr(e, tc=tc, c=c, cc=cc, PT=PT, XOF=XOF):
                                return e.transpose(
                                    PT[:, cc * 128:cc * 128 + 128],
                                    XOF[:, c, tc * 128:tc * 128 + 128],
                                    c32("ident"))
                            w = []
                            if cc == 0:
                                w.append(t_lx)
                                if t_vgate is not None:
                                    w.append(t_vgate)
                                if pt_evtok[half] is not None:
                                    w.append(pt_evtok[half])
                            p.add("pe", tr, inc=(cc == 3), waits=w)
                        tpe = p.tok("pe")
                        tp_par[par] = tpe
                        def ev(e, tc=tc, half=half, PT=PT):
                            return e.tensor_copy(
                                OT[:, tc, half * 512:half * 512 + 512], PT[:])
                        w = [tpe]
                        if half == 0 and out_dma_tok[tc] is not None:
                            w.append(out_dma_tok[tc])
                        ve = p.add("v", ev, waits=w)
                        pt_evtok[half] = (p.sem["v"], ve)
                    t_evt = p.tok("v")
                    out_dma_tok[tc] = p.dma(
                        "y", f"out{tc}",
                        out_h[col + tc * 128: col + tc * 128 + 128, :],
                        OT[:, tc, :], waits=[t_evt])
            def fin(e):
                return e.nop()
            p.add("y", fin,
                  waits=[(p.dsem[f"out{tc}"], p.dma_cnt[f"out{tc}"])
                         for tc in range(4)])

        with nc.Block() as block:
            p.emit(block)
    return nc


SMALL_SEGS = [("eps", 1), ("e9", 1),
              ("w", 8), ("u", 8), ("tmk", 8), ("tmv", 8), ("tmr", 8),
              ("lng0", 8), ("lnb0", 8), ("lng1", 8), ("lnb1", 8),
              ("b2_0", 8), ("b2_1", 8), ("qs0", 1), ("ks0", 1),
              ("qs1", 1), ("ks1", 1), ("qsw0", 1), ("ksw0", 1),
              ("qsw1", 1), ("ksw1", 1), ("b1i0", NSLOT), ("b1i1", NSLOT)]
SMALL_OFF = {}
_c = 0
for _nm, _w in SMALL_SEGS:
    SMALL_OFF[_nm] = (_c, _w)
    _c += _w
SMALL_COLS = _c


def _feat(v):
    return np.ascontiguousarray(v.reshape(8, 128).T, np.float32)


def _small_params(inp):
    sm = np.zeros((128, SMALL_COLS), np.float32)

    def put(nm, arr):
        o, w = SMALL_OFF[nm]
        sm[:, o:o + w] = arr.reshape(128, w)

    put("eps", np.full((128, 1), 1e-5, np.float32))
    put("e9", np.full((128, 1), 1e-9, np.float32))
    put("w", _feat(inp["L0_attn_time_decay"]))
    put("u", _feat(inp["L0_attn_time_first"]))
    put("tmk", _feat(inp["L0_attn_time_mix_k"].reshape(-1)))
    put("tmv", _feat(inp["L0_attn_time_mix_v"].reshape(-1)))
    put("tmr", _feat(inp["L0_attn_time_mix_r"].reshape(-1)))
    for l in range(2):
        put(f"lng{l}", _feat(inp["L0_ffn_ln_g"][l]))
        put(f"lnb{l}", _feat(inp["L0_ffn_ln_b"][l]))
        put(f"b2_{l}", _feat(inp["L0_ffn_b2"][l]))
        qs = inp["L0_ffn_q_scale"][l]
        ks = inp["L0_ffn_k_scale"][l]
        du = np.arange(128) % 64
        put(f"qs{l}", qs[du].reshape(128, 1))
        put(f"ks{l}", ks[du].reshape(128, 1))
        put(f"qsw{l}", qs[(du + 32) % 64].reshape(128, 1))
        put(f"ksw{l}", ks[(du + 32) % 64].reshape(128, 1))
        b1 = inp["L0_ffn_b1"][l]
        b1i = np.zeros((NSLOT, 128), np.float32)
        for j in range(NJ):
            a = b1[128 * j:128 * (j + 1)]
            g = b1[HID + 128 * j:HID + 128 * (j + 1)]
            b1i[2 * j, :len(a)] = a
            b1i[2 * j + 1, :len(g)] = g
        put(f"b1i{l}", b1i.T)
    return sm


def _prep_weights(inp):
    d = {}
    d["wkT"] = np.ascontiguousarray(inp["L0_attn_W_k"].T, np.float32)
    d["wvT"] = np.ascontiguousarray(inp["L0_attn_W_v"].T, np.float32)
    d["wrT"] = np.ascontiguousarray(inp["L0_attn_W_r"].T, np.float32)
    d["woT"] = np.ascontiguousarray(inp["L0_attn_W_o"].T, np.float32)
    d["wqkv"] = np.ascontiguousarray(inp["L0_ffn_wqkv"], np.float32)
    perm = np.arange(1024)
    for o in range(1024):
        h, du = o // 64, o % 64
        perm[o] = h * 64 + (du + 32) % 64
    d["wqsw"] = np.ascontiguousarray(inp["L0_ffn_wqkv"][:, :, perm],
                                     np.float32)
    d["wof"] = np.ascontiguousarray(inp["L0_ffn_wo"], np.float32)
    w1 = inp["L0_ffn_w1"]
    w1i = np.zeros((2, D, NSLOT, 128), np.float32)
    for j in range(NJ):
        n = min(128, HID - 128 * j)
        w1i[:, :, 2 * j, :n] = w1[:, :, 128 * j:128 * j + n]
        w1i[:, :, 2 * j + 1, :n] = w1[:, :, HID + 128 * j:HID + 128 * j + n]
    d["w1i"] = w1i.reshape(2, D, NSLOT * 128)
    w2 = inp["L0_ffn_w2"]
    w2p = np.zeros((2, NJ * 128, D), np.float32)
    w2p[:, :HID, :] = w2
    d["w2p"] = w2p
    return d


def kernel(**inputs):
    x = np.ascontiguousarray(np.asarray(inputs["L0_attn_x"], np.float32))
    c32_np, segs32, cbf_np, segsbf, rot_np, cbf_res = _host_consts()
    c32_buf, _ = _pack(c32_np, segs32)
    cbf_buf, _ = _pack(cbf_np, segsbf)
    sm = _small_params(inputs)
    wts = _prep_weights(inputs)

    import os
    ncores = int(os.environ.get("K_NCORES", NCORES))
    nc = build(bass.Bass("TRN2", target_bir_lowering=False), BUILD_STAGE)
    in_maps = []
    for b in range(ncores):
        m = {"x": np.ascontiguousarray(x[b]),
             "cst32": c32_buf, "cstbf": cbf_buf, "rot": rot_np,
             "smallw": sm}
        m.update(wts)
        in_maps.append(m)
    res = run_bass_kernel_spmd(nc, in_maps, core_ids=list(range(ncores)))
    if os.environ.get("K_DUMP"):
        kernel.dbg = res.results
    out = np.stack([r["out"] for r in res.results], 0)
    if ncores < NCORES:
        out = np.concatenate([out] + [out[:1]] * (NCORES - ncores), 0)
    return out


# version 21 (final)
# speedup vs baseline: 1.0301x; 1.0014x over previous
import math
import numpy as np
from contextlib import ExitStack

import concourse.bass as bass
import concourse.mybir as mybir
from concourse.bass_utils import run_bass_kernel_spmd

F32 = mybir.dt.float32
BF16 = mybir.dt.bfloat16
AF = mybir.ActivationFunctionType
ALU = mybir.AluOpType
AX = mybir.AxisListType

T = 2048
D = 1024
CH = 8
NCORES = 8
HEADS = 8
DH = 64
INNER = 512
HID = 2730
NJ = 22
NSLOT = 44
WIN = 512

TB = 256
NB = T // TB
TS = 128
NS = TB // TS

FT = 512
NF = T // FT

NEG = -1.0e30

BUILD_STAGE = 3
DBG_STOP = None
DBG_NB = None


def _host_consts():
    c32 = {}
    segs32 = []

    def add32(name, arr):
        arr = np.asarray(arr, np.float32)
        assert arr.shape[0] == 128 or arr.ndim == 2
        c32[name] = arr
        segs32.append(name)

    add32("ones", np.ones((128, 128), np.float32))
    add32("ident", np.eye(128, dtype=np.float32))
    add32("iota", np.broadcast_to(np.arange(128, dtype=np.float32)[None, :],
                                  (128, 128)).copy())
    add32("iota1", np.broadcast_to(
        np.arange(1, 129, dtype=np.float32)[None, :], (128, 128)).copy())
    hbc = np.zeros((128, 4, 128), np.float32)
    for c in range(4):
        hbc[2 * c, c, 0:64] = 1.0
        hbc[2 * c + 1, c, 64:128] = 1.0
    add32("hbc", hbc.reshape(128, 512))

    i = np.arange(32, dtype=np.float64)
    inv_freq = 1.0 / (10000.0 ** (2.0 * i / 64.0))
    base = (2.0 * i + 0.4 * 64.0) / (1.4 * 64.0)
    t = np.arange(T, dtype=np.float64)
    ang = t[:, None] * inv_freq[None, :]
    power = (t - T // 2) / float(WIN // 2)
    scl = base[None, :] ** power[:, None]
    cos = np.cos(ang)
    sin = np.sin(ang)
    def mk(tabc, tabs):
        cs = np.concatenate([tabc, tabc], 1)
        sn = np.concatenate([-tabs, tabs], 1)
        return cs, sn
    csq, snq = mk(cos * scl * 8.0, sin * scl * 8.0)
    csk, snk = mk(cos / scl, sin / scl)
    rot = np.stack([csq, snq, csk, snk], 0)
    rot = np.broadcast_to(rot.transpose(0, 2, 1)[:, None, :, :],
                          (4, 2, 64, T)).reshape(4, 128, T)
    rot = np.ascontiguousarray(rot, np.float32)

    cbf = {}
    segsbf = []

    def addbf(name, arr):
        cbf[name] = np.asarray(arr, np.float32)
        segsbf.append(name)

    addbf("ident", np.eye(128, dtype=np.float32))
    addbf("ones1", np.ones((128, 1), np.float32))
    hsum = np.zeros((128, 4, 8), np.float32)
    for c in range(4):
        hsum[0:64, c, 2 * c] = 1.0
        hsum[64:128, c, 2 * c + 1] = 1.0
    addbf("hsum", hsum.reshape(128, 32))
    addbf("hbc", hbc.reshape(128, 512))
    addbf("onesr", np.ones((128, 128), np.float32))
    a = np.arange(128)[:, None]
    j = np.arange(512)[None, :]
    maskA = np.where(j >= a, 0.0, NEG)
    addbf("maskA", maskA)
    jb = np.arange(128)[None, :]
    maskB = np.where(jb <= a, 0.0, NEG)
    addbf("maskB", maskB)
    mA0 = np.empty((128, 4, 512), np.float64)
    for qc in range(4):
        lim = 512 - 128 * qc
        mA0[:, qc, :] = np.where((j >= a) & (j >= lim), 0.0, NEG)
    addbf("maskA0", mA0.reshape(128, 2048))
    resident = ["ident", "ones1", "hsum", "hbc", "onesr"]

    return c32, segs32, cbf, segsbf, rot, resident


def _pack(cdict, order):
    offs = {}
    cols = 0
    for nm in order:
        a = cdict[nm].reshape(128, -1)
        offs[nm] = (cols, a.shape[1])
        cols += a.shape[1]
    buf = np.zeros((128, cols), np.float32)
    for nm in order:
        a = cdict[nm].reshape(128, -1)
        o, w = offs[nm]
        buf[:, o:o + w] = a
    return buf, offs


class PB:
    ENGS = ("pe", "v", "s", "g", "y")

    def __init__(self, nc):
        self.nc = nc
        self.items = {e: [] for e in self.ENGS}
        self.cnt = {e: 0 for e in self.ENGS}
        self.dma_cnt = {}
        self.sem = {}
        self.dsem = {}
        self._es = None

    def init_sems(self, es):
        self._es = es
        for e in self.ENGS:
            self.sem[e] = es.enter_context(self.nc.semaphore(f"sem_{e}"))

    CHAINED = ("v", "s", "g")

    @staticmethod
    def _ckw(waits):
        for w in waits:
            assert (isinstance(w, tuple) and len(w) == 2
                    and isinstance(w[1], int)), f"bad wait token: {w!r}"

    def add(self, eng, fn, inc=False, waits=()):
        self._ckw(waits)
        if eng in self.CHAINED:
            prev = self.cnt[eng]
            w = list(waits) + [(self.sem[eng], prev)]
            self.items[eng].append((fn, True, w))
            self.cnt[eng] += 1
        else:
            self.items[eng].append((fn, bool(inc), list(waits)))
            if inc:
                self.cnt[eng] += 1
        return self.cnt[eng]

    def dma(self, q, name, out, in_, waits=()):
        if name not in self.dsem:
            self.dsem[name] = self._es.enter_context(
                self.nc.semaphore(f"dma_{name}"))
            self.dma_cnt[name] = 0

        self._ckw(waits)

        def fn(e, out=out, in_=in_):
            return e.dma_start(out=out, in_=in_)
        self.items[q].append((fn, ("dma", name), list(waits)))
        self.dma_cnt[name] += 16
        return (self.dsem[name], self.dma_cnt[name])

    def tok(self, eng):
        return (self.sem[eng], self.cnt[eng])

    def all_dma_toks(self):
        return [(self.dsem[n], self.dma_cnt[n]) for n in self.dsem]

    def emit(self, block):
        def mk_body(eng_name):
            def body(e):
                for fn, inc, waits in self.items[eng_name]:
                    for sem, val in waits:
                        e.wait_ge(sem, val)
                    instr = fn(e)
                    if inc is True:
                        instr.then_inc(self.sem[eng_name], 1)
                    elif isinstance(inc, tuple) and inc[0] == "dma":
                        instr.then_inc(self.dsem[inc[1]], 16)
            return body
        block.tensor(mk_body("pe"))
        block.vector(mk_body("v"))
        block.scalar(mk_body("s"))
        block.gpsimd(mk_body("g"))
        block.sync(mk_body("y"))


def build(nc, stage):
    dt = lambda nm, shp, k=None: (nc.dram_tensor(nm, shp, F32, kind=k)
                                  if k else nc.dram_tensor(nm, shp, F32)).ap()
    x_h = dt("x", [T, D], "ExternalInput")
    out_h = dt("out", [T, D], "ExternalOutput")
    c32_np, segs32, cbf_np, segsbf, rot_np, cbf_res = _host_consts()
    c32_buf, c32_off = _pack(c32_np, segs32)
    cbf_buf, cbf_off = _pack(cbf_np, segsbf)
    smallw = dt("smallw", [128, SMALL_COLS], "ExternalInput")
    cst32_h = dt("cst32", list(c32_buf.shape), "ExternalInput")
    cstbf_h = dt("cstbf", list(cbf_buf.shape), "ExternalInput")
    rot_h = dt("rot", [4, 128, T], "ExternalInput")
    wkT_h = dt("wkT", [D, D], "ExternalInput")
    wvT_h = dt("wvT", [D, D], "ExternalInput")
    wrT_h = dt("wrT", [D, D], "ExternalInput")
    woT_h = dt("woT", [D, D], "ExternalInput")
    wqkv_h = dt("wqkv", [2, D, 1536], "ExternalInput")
    wqsw_h = dt("wqsw", [2, D, 1024], "ExternalInput")
    wof_h = dt("wof", [2, INNER, D], "ExternalInput")
    w1i_h = dt("w1i", [2, D, NSLOT * 128], "ExternalInput")
    w2p_h = dt("w2p", [2, NJ * 128, D], "ExternalInput")
    xsp_h = nc.dram_tensor("xsp", [128, CH, T], BF16).ap()
    ysp_h = nc.dram_tensor("ysp", [128, CH, T], BF16).ap()
    import os
    dbg_dump = bool(os.environ.get("K_DUMP"))
    if dbg_dump:
        dby_h = {}
        for tag in ("0", "a0", "b0", "a1", "b1"):
            dby_h[tag] = dt(f"dby{tag}", [128, CH, T], "ExternalOutput")
        dbhb_h = dt("dbhb", [128, CH, FT], "ExternalOutput")
        dbha_h = dt("dbha", [128, NJ, FT], "ExternalOutput")
        dbw1_h = dt("dbw1", [128, CH, NSLOT * 128], "ExternalOutput")

    p = PB(nc)
    es = ExitStack()
    with es:
        p.init_sems(es)
        sb = lambda nm, shp, dty: es.enter_context(nc.sbuf_tensor(nm, shp, dty))
        front_es = ExitStack()
        fsb = lambda nm, shp, dty: front_es.enter_context(
            nc.sbuf_tensor(nm, shp, dty))
        C32 = sb("C32", list(c32_buf.shape), F32)
        ncres = sum(cbf_off[nm][1] for nm in cbf_res)
        CBF = sb("CBF", [128, ncres], BF16)
        SMW = sb("SMW", [128, SMALL_COLS], F32)
        RRF = sb("RRF", [128, 2048], F32)
        XF = fsb("XF", [128, CH, T], BF16)

        def c32(nm, r0=0, r1=128):
            o, w = c32_off[nm]
            return C32[r0:r1, o:o + w]

        def cbf(nm, r0=0, r1=128):
            o, w = cbf_off[nm]
            assert nm in cbf_res and o + w <= ncres
            return CBF[r0:r1, o:o + w]

        def cbf_h(nm):
            o, w = cbf_off[nm]
            return cstbf_h[:, o:o + w]

        def smw(nm):
            o, w = SMALL_OFF[nm]
            return SMW[:, o:o + w]

        t_c32 = p.dma("y", "cst", C32[:], cst32_h)
        t_smw = p.dma("y", "smw", SMW[:], smallw)
        t_cbf = p.dma("g", "cbf", CBF[:], cstbf_h[:, 0:ncres])

        s0_ev = [None] * 16
        with nc.sbuf_tensor("XTOK", [128, 16, D], BF16) as XTOK, \
             nc.psum_tensor("PT0", [128, 512], BF16) as PT0, \
             nc.psum_tensor("PT1", [128, 512], BF16) as PT1:
            t_x = p.dma("g", "x", XTOK[:],
                        x_h.rearrange("(tc p) c -> p tc c", p=128))
            wx = [t_x, t_cbf]
            pev = 0
            for tc in range(16):
                for half, PT in ((0, PT0), (1, PT1)):
                    for cc in range(4):
                        c = half * 4 + cc
                        def tr(e, tc=tc, c=c, cc=cc, PT=PT):
                            return e.transpose(
                                PT[:, cc * 128:(cc + 1) * 128],
                                XTOK[:, tc, c * 128:(c + 1) * 128],
                                cbf("ident"))
                        w = list(wx) if (tc == 0 and half == 0 and cc == 0) else []
                        if tc > 0 and cc == 0:
                            w.append((p.sem["v"], pev - 1))
                        p.add("pe", tr, inc=(cc == 3), waits=w)
                    tpe = p.tok("pe")
                    def ev(e, tc=tc, half=half, PT=PT):
                        return e.tensor_copy(
                            XF[:, half * 4:half * 4 + 4, tc * 128:tc * 128 + 128],
                            PT[:].rearrange("p (c n) -> p c n", c=4))
                    pev = p.add("v", ev, inc=True, waits=[tpe])
                s0_ev[tc] = p.tok("v")

        t_s0pe = p.tok("pe")
        WK = fsb("WK", [128, CH, D], BF16)
        WV = fsb("WV", [128, CH, D], BF16)
        WR = fsb("WR", [128, CH, D], BF16)
        WO = fsb("WO", [128, CH, D], BF16)
        p.dma("g", "wr4", WK[:], wkT_h.rearrange("(c p) o -> p c o", p=128),
              waits=[t_s0pe])
        p.dma("g", "wr4", WV[:], wvT_h.rearrange("(c p) o -> p c o", p=128))
        p.dma("g", "wr4", WR[:], wrT_h.rearrange("(c p) o -> p c o", p=128))
        t_wr4 = p.dma("g", "wr4", WO[:],
                      woT_h.rearrange("(c p) o -> p c o", p=128))

        rwkv_scope = ExitStack()
        with rwkv_scope as rs:
            rsb = lambda nm, shp, dty: rs.enter_context(
                nc.sbuf_tensor(nm, shp, dty))
            H0 = rsb("H0", [128, CH, TB + 1], BF16)
            XKk = rsb("XKk", [128, CH, TB], BF16)
            XKv = rsb("XKv", [128, CH, TB], BF16)
            WKVR = rsb("WKVR", [128, 2, CH, TB], BF16)
            RG2 = rsb("RG2", [128, 2, CH, TB], BF16)
            KF = rsb("KF", [128, CH, TB], F32)
            XC = rsb("XC", [128, CH, TB], F32)
            VFB = rsb("VFB", [128, CH, TB], BF16)
            M2 = rsb("M2", [128, 2, CH, TS + 1], F32)
            A2 = rsb("A2", [128, 2, CH, TS + 1], BF16)
            B2 = rsb("B2", [128, 2, CH, TS + 1], BF16)
            S1B = rsb("S1B", [128, CH, TS], F32)
            SAB = rsb("SAB", [128, 2, CH, TS], F32)
            E12 = rsb("E12", [128, 2, CH, TS], BF16)
            N1B = rsb("N1B", [128, CH, TS], BF16)
            DEN = rsb("DEN", [128, CH, TS], BF16)
            RCB = rsb("RCB", [128, CH, TS], BF16)
            TWU = rsb("TWU", [128, CH, TS], F32)
            TWZ = rsb("TWZ", [128, CH, TS], F32)
            SMX = rsb("SMX", [128, CH], F32)
            PBS = [rs.enter_context(nc.psum_tensor(f"pb{i}", [128, 512], F32))
                   for i in range(8)]

            murow = RRF[0:1, 0:TB]
            rstdrow = RRF[0:1, 256:256 + TB]
            scrap = RRF[0:1, 512:512 + TB]
            sdrow = RRF[0:1, 768:768 + TB]

            wsm = [t_smw, t_c32]
            def f0(e):
                return e.tensor_tensor(SMX[:], smw("w"), smw("u"), ALU.add)
            p.add("v", f0, waits=wsm)
            for c in range(CH):
                def f1(e, c=c):
                    return e.tensor_scalar(TWU[:, c, :], c32("iota"),
                                           smw("w")[:, c:c + 1], None, ALU.mult)
                p.add("v", f1)
            for c in range(CH):
                def f2(e, c=c):
                    return e.tensor_scalar(TWU[:, c, :], TWU[:, c, :],
                                           smw("u")[:, c:c + 1], None,
                                           ALU.subtract)
                p.add("v", f2)
            for c in range(CH):
                def f3z(e, c=c):
                    return e.tensor_scalar(TWZ[:, c, :], TWU[:, c, :],
                                           SMX[:, c:c + 1], None, ALU.add)
                p.add("v", f3z)

            def col2d4(buf, par, i):
                return buf[:, par, :, i:i + 1].rearrange("p c n -> p (c n)")

            def fill(dst, val):
                def f(e, dst=dst, val=val):
                    return e.tensor_scalar(dst, smw("w"), 0.0, float(val),
                                           ALU.mult, ALU.add)
                return f
            p.add("v", fill(col2d4(M2, 0, 0), -1.0e38))
            p.add("v", fill(col2d4(A2, 0, 0), 0.0))
            p.add("v", fill(col2d4(B2, 0, 0), 0.0))
            p.add("v", fill(H0[:, :, 0:1].rearrange("p c n -> p (c n)"), 0.0),
                  inc=True)

            ST = {"pend": None, "sg": 0, "t_exps": None, "t_vnum": None,
                  "t_gb": None, "t_vnum2": None,
                  "t_vwk": None, "t_vwk_s1": None, "t_wkvr_done": None,
                  "t_ln2v": p.tok("v"), "t_ln2pe": None,
                  "t_kw": None, "t_vw": None, "t_rw": None,
                  "t_kev": None, "t_vev": None, "t_revs": None,
                  "t_aev": None, "t_aevs": None}

            def emit_wkv(pd):
                def wkv(e):
                    return e.tensor_tensor(N1B[:], N1B[:], RCB[:], ALU.mult)
                w = [pd["t_sdn"]]
                if ST["t_ln2pe"] is not None:
                    w.append(ST["t_ln2pe"])
                p.add("v", wkv, waits=w)
                bp = pd["b"] & 1
                def wkr(e, bp=bp, wsl=pd["wsl"]):
                    return e.tensor_tensor(WKVR[:, bp, :, wsl], N1B[:],
                                           RG2[:, bp, :, wsl], ALU.mult)
                p.add("v", wkr, inc=True)
                ST["t_vwk"] = p.tok("v")
                if pd["sblk"] == 1:
                    ST["t_vwk_s1"] = ST["t_vwk"]
                    ST["t_wkvr_done"] = ST["t_vwk"]

            def emit_slot(b, sblk):
                sg = ST["sg"]
                par = sg & 1
                op = 1 - par
                wsl = slice(sblk * TS, sblk * TS + TS)
                pend = ST["pend"]
                if sg > 0:
                    def mcar(e, par=par, op=op):
                        return e.scalar_tensor_tensor(
                            col2d4(M2, par, 0), smw("w"), float(TS),
                            col2d4(M2, op, TS), ALU.mult, ALU.add)
                    p.add("v", mcar)
                    def acar(e, par=par, op=op):
                        return e.tensor_copy(col2d4(A2, par, 0),
                                             col2d4(A2, op, TS))
                    p.add("v", acar)
                    def bcar(e, par=par, op=op):
                        return e.tensor_copy(col2d4(B2, par, 0),
                                             col2d4(B2, op, TS))
                    p.add("v", bcar)
                wz = [pend["t_sdn"]] if pend is not None else []
                if sblk == 0:
                    wz.append(ST["t_kev"])
                def zf(e, wsl=wsl):
                    return e.tensor_tensor(S1B[:], KF[:, :, wsl], TWZ[:],
                                           ALU.subtract)
                p.add("v", zf, waits=wz)
                for c in range(CH):
                    def cm(e, c=c, par=par):
                        return e.tensor_tensor_scan(
                            M2[:, par, c, 1:TS + 1], S1B[:, c, :],
                            S1B[:, c, :], M2[:, par, c, 0:1],
                            ALU.max, ALU.max)
                    p.add("v", cm)
                wa = []
                if ST["t_exps"] is not None:
                    wa.append(ST["t_exps"])
                if ST["t_gb"] is not None:
                    wa.append(ST["t_gb"])
                def asub(e, par=par):
                    return e.tensor_tensor(SAB[:, 0], M2[:, par, :, 0:TS],
                                           M2[:, par, :, 1:TS + 1],
                                           ALU.subtract)
                p.add("v", asub, waits=wa)
                def bsub(e, par=par):
                    return e.tensor_tensor(SAB[:, 1], S1B[:],
                                           M2[:, par, :, 1:TS + 1],
                                           ALU.subtract)
                p.add("v", bsub, inc=True)
                t_subs = p.tok("v")
                def dd1(e, wsl=wsl, par=par):
                    return e.tensor_tensor(S1B[:], KF[:, :, wsl],
                                           M2[:, par, :, 0:TS], ALU.subtract)
                p.add("v", dd1)
                def dd2(e):
                    return e.tensor_tensor(S1B[:], S1B[:], TWU[:],
                                           ALU.subtract)
                p.add("v", dd2, inc=True)
                t_vD = p.tok("v")
                if pend is not None:
                    emit_wkv(pend)
                def exps(e):
                    return e.activation(SAB[:], SAB[:], AF.Exp)
                p.add("s", exps, inc=True, waits=[t_subs])
                ST["t_exps"] = p.tok("s")
                wr1 = [t_vD]
                if ST["t_vnum"] is not None:
                    wr1.append(ST["t_vnum"])
                def rl1(e):
                    return e.activation(E12[:, 0], S1B[:], AF.Relu)
                p.add("s", rl1, waits=wr1)
                def rl2(e):
                    return e.activation(E12[:, 1], S1B[:], AF.Relu, scale=-1.0)
                p.add("s", rl2)
                def eexp(e):
                    return e.activation(E12[:], E12[:], AF.Exp, scale=-1.0)
                p.add("s", eexp, inc=True)
                t_sE = p.tok("s")
                for c in range(CH):
                    def sb_(e, c=c, par=par):
                        return e.tensor_tensor_scan(
                            B2[:, par, c, 1:TS + 1], SAB[:, 0, c, :],
                            SAB[:, 1, c, :], B2[:, par, c, 0:1],
                            ALU.mult, ALU.add)
                    p.add("v", sb_,
                          waits=([ST["t_exps"]] if c == 0 else ()))
                def bv(e, wsl=wsl):
                    return e.tensor_tensor(SAB[:, 1], SAB[:, 1],
                                           VFB[:, :, wsl], ALU.mult)
                p.add("v", bv)
                for c in range(CH):
                    def sa_(e, c=c, par=par):
                        return e.tensor_tensor_scan(
                            A2[:, par, c, 1:TS + 1], SAB[:, 0, c, :],
                            SAB[:, 1, c, :], A2[:, par, c, 0:1],
                            ALU.mult, ALU.add)
                    p.add("v", sa_)
                def d1(e, par=par):
                    return e.tensor_tensor(DEN[:], E12[:, 0],
                                           B2[:, par, :, 0:TS], ALU.mult)
                p.add("v", d1, waits=[t_sE])
                def d2(e):
                    return e.tensor_tensor(DEN[:], DEN[:], E12[:, 1], ALU.add)
                p.add("v", d2, inc=True)
                t_vden = p.tok("v")
                def t2(e, par=par):
                    return e.tensor_tensor(N1B[:], E12[:, 0],
                                           A2[:, par, :, 0:TS], ALU.mult)
                p.add("v", t2)
                def t3(e, wsl=wsl):
                    return e.tensor_tensor(E12[:, 1], E12[:, 1],
                                           VFB[:, :, wsl], ALU.mult)
                p.add("v", t3)
                def t4(e):
                    return e.tensor_tensor(N1B[:], N1B[:], E12[:, 1], ALU.add)
                p.add("v", t4, inc=True)
                ST["t_vnum"] = p.tok("v")
                ST["t_vnum2"] = ST["t_vnum"]
                wln = [t_vden]
                if ST["t_vwk"] is not None:
                    wln.append(ST["t_vwk"])
                def dln(e):
                    return e.activation(S1B[:], DEN[:], AF.Ln,
                                        bias=smw("e9")[:, 0:1])
                p.add("s", dln, waits=wln)
                def dex(e):
                    return e.activation(RCB[:], S1B[:], AF.Exp, scale=-1.0)
                p.add("s", dex, inc=True)
                t_sdn = p.tok("s")
                ST["pend"] = {"b": b, "sblk": sblk, "wsl": wsl, "par": par,
                              "t_sE": t_sE, "t_sdn": t_sdn}
                ST["sg"] = sg + 1

            def emit_ln1_mixk_kwave(b):
                col = b * TB
                xfb = XF[:, :, col:col + TB]
                if b > 0:
                    def hcar(e):
                        return e.tensor_copy(H0[:, :, 0:1], H0[:, :, TB:TB + 1])
                    p.add("v", hcar)
                ws = [s0_ev[2 * b + 1], ST["t_ln2v"]]
                def sqr(e, xfb=xfb):
                    return e.activation(XKk[:], xfb, AF.Square)
                p.add("s", sqr, inc=True, waits=ws)
                t_sqr = p.tok("s")
                for c in range(CH):
                    def m0(e, c=c, xfb=xfb):
                        return e.matmul(PBS[6][0:1, 0:TB], cbf("ones1"),
                                        xfb[:, c, :], start=(c == 0),
                                        stop=(c == CH - 1))
                    w = []
                    if c == 0:
                        w = [s0_ev[2 * b + 1], ST["t_ln2v"], t_cbf]
                        if ST["t_revs"] is not None:
                            w.append(ST["t_revs"])
                        if ST["t_aevs"] is not None:
                            w.append(ST["t_aevs"])
                    p.add("pe", m0, waits=w)
                for c in range(CH):
                    def m1(e, c=c):
                        return e.matmul(PBS[7][0:1, 0:TB], cbf("ones1"),
                                        XKk[:, c, :], start=(c == 0),
                                        stop=(c == CH - 1))
                    p.add("pe", m1, inc=(c == CH - 1),
                          waits=([t_sqr] if c == 0 else ()))
                t_stats = p.tok("pe")
                def r1(e):
                    return e.tensor_scalar(murow, PBS[6][0:1, 0:TB],
                                           1.0 / D, None, ALU.mult)
                p.add("v", r1, waits=[t_stats])
                def r2(e):
                    return e.tensor_tensor(scrap, murow, murow, ALU.mult)
                p.add("v", r2)
                def r3(e):
                    return e.scalar_tensor_tensor(sdrow, PBS[7][0:1, 0:TB],
                                                  1.0 / D, scrap, ALU.mult,
                                                  ALU.subtract)
                p.add("v", r3, inc=True)
                t_var = p.tok("v")
                def r4(e):
                    return e.activation(sdrow, sdrow, AF.Ln,
                                        bias=smw("eps")[0:1, :])
                p.add("s", r4, waits=[t_var])
                def r6(e):
                    return e.activation(rstdrow, sdrow, AF.Exp, scale=-0.5)
                p.add("s", r6, inc=True)
                t_rows = p.tok("s")
                def b1(e):
                    return e.matmul(PBS[6][:, 0:TB], c32("ones", 0, 1), murow,
                                    start=True, stop=True)
                p.add("pe", b1, waits=[t_rows])
                def b2(e):
                    return e.matmul(PBS[7][:, 0:TB], c32("ones", 0, 1),
                                    rstdrow, start=True, stop=True)
                p.add("pe", b2, inc=True)
                t_bc = p.tok("pe")
                for c in range(CH):
                    def a1(e, c=c, xfb=xfb):
                        return e.tensor_tensor(XC[:, c, :], xfb[:, c, :],
                                               PBS[6][:, 0:TB], ALU.subtract)
                    p.add("v", a1, waits=([t_bc] if c == 0 else ()))
                for c in range(CH):
                    def a2(e, c=c):
                        return e.tensor_tensor(H0[:, c, 1:TB + 1], XC[:, c, :],
                                               PBS[7][:, 0:TB], ALU.mult)
                    p.add("v", a2)
                def dxf(e):
                    return e.tensor_tensor(XC[:], H0[:, :, 1:TB + 1],
                                           H0[:, :, 0:TB], ALU.subtract)
                p.add("v", dxf)
                def pslot(bank0, oc):
                    return PBS[bank0 + oc // 2][:, (oc % 2) * TB:
                                                (oc % 2) * TB + TB]

                def mixwave(W, mixnm, dst, bank0, t_gate, mx_wait=()):
                    for c in range(CH):
                        def mx(e, c=c, dst=dst, mixnm=mixnm):
                            return e.scalar_tensor_tensor(
                                dst[:, c, :], XC[:, c, :],
                                smw(mixnm)[:, c:c + 1], H0[:, c, 0:TB],
                                ALU.mult, ALU.add)
                        p.add("v", mx, inc=(c == CH - 1),
                              waits=(mx_wait if c == 0 else ()))
                    t_mix = p.tok("v")
                    for oc in range(CH):
                        for kc in range(CH):
                            def mm(e, oc=oc, kc=kc, W=W, dst=dst, bank0=bank0):
                                return e.matmul(
                                    pslot(bank0, oc),
                                    W[:, kc, oc * 128:oc * 128 + 128],
                                    dst[:, kc, :], start=(kc == 0),
                                    stop=(kc == CH - 1))
                            w = []
                            if oc == 0 and kc == 0:
                                w = [t_mix, t_wr4]
                                if t_gate is not None:
                                    w.append(t_gate)
                            p.add("pe", mm,
                                  inc=(oc == CH - 1 and kc == CH - 1),
                                  waits=w)
                    return p.tok("pe")
                ST["mixwave"] = mixwave
                ST["pslot"] = pslot
                ST["t_kw"] = mixwave(WK, "tmk", XKk, 0, None)
                ST["t_vw"] = mixwave(WV, "tmv", XKv, 4, None)

            def emit_evkv(b):
                pslot = ST["pslot"]
                for oc in range(CH):
                    def evv(e, oc=oc):
                        return e.activation(VFB[:, oc, :], pslot(4, oc),
                                            AF.Copy, bias=0.0)
                    w = []
                    if oc == 0:
                        w = [ST["t_vw"]]
                        if ST["t_vnum"] is not None:
                            w.append(ST["t_vnum"])
                    p.add("s", evv, inc=(oc == CH - 1), waits=w)
                ST["t_vev"] = p.tok("s")
                for oc in range(CH):
                    def evk(e, oc=oc):
                        return e.activation(KF[:, oc, :], pslot(0, oc),
                                            AF.Copy, bias=0.0)
                    p.add("s", evk, inc=(oc == CH - 1),
                          waits=([ST["t_kw"]] if oc == 0 else ()))
                ST["t_kev"] = p.tok("s")

            def emit_rwave(b):
                pslot = ST["pslot"]
                ST["t_rw"] = ST["mixwave"](WR, "tmr", XKk, 0, ST["t_kev"],
                                           mx_wait=[ST["t_kw"]])
                bp = b & 1
                for oc in range(CH):
                    def evr(e, oc=oc, bp=bp):
                        return e.activation(RG2[:, bp, oc, :], pslot(0, oc),
                                            AF.Tanh, scale=0.5)
                    w = []
                    if oc == 0:
                        w = [ST["t_rw"]]
                        if ST["t_vwk_s1"] is not None:
                            w.append(ST["t_vwk_s1"])
                    p.add("s", evr, inc=(oc == CH - 1), waits=w)
                ST["t_revs"] = p.tok("s")
                def rresc(e, bp=bp):
                    return e.tensor_scalar(RG2[:, bp], RG2[:, bp], 0.5, 0.5,
                                           ALU.mult, ALU.add)
                p.add("v", rresc, inc=True, waits=[ST["t_revs"]])

            def emit_wo_ln2(bb):
                bp = bb & 1
                col = bb * TB
                xfb = XF[:, :, col:col + TB]
                pslot = ST["pslot"]
                for oc in range(CH):
                    for kc in range(CH):
                        def mo(e, oc=oc, kc=kc, bp=bp):
                            return e.matmul(
                                pslot(4, oc),
                                WO[:, kc, oc * 128:oc * 128 + 128],
                                WKVR[:, bp, kc, :], start=(kc == 0),
                                stop=(kc == CH - 1))
                        w = []
                        if oc == 0 and kc == 0:
                            w = [ST["t_wkvr_done"], ST["t_vev"]]
                            if ST["t_aev"] is not None:
                                w += [ST["t_aev"], ST["t_aevs"]]
                        p.add("pe", mo, inc=(oc == CH - 1 and kc == CH - 1),
                              waits=w)
                t_ow = p.tok("pe")
                for bk in range(4):
                    tv_bank = None
                    for half in range(2):
                        oc = bk * 2 + half
                        def ea1(e, oc=oc):
                            return e.tensor_copy(XC[:, oc, :], pslot(4, oc))
                        p.add("v", ea1,
                              waits=([t_ow] if oc == 0 else ()))
                        def ea2(e, oc=oc, bp=bp):
                            return e.tensor_copy(WKVR[:, bp, oc, :],
                                                 pslot(4, oc))
                        tv_bank = p.add("v", ea2, inc=(oc == CH - 1))
                    for half in range(2):
                        oc = bk * 2 + half
                        def ea3(e, oc=oc):
                            return e.activation(XKk[:, oc, :], pslot(4, oc),
                                                AF.Square)
                        p.add("s", ea3, inc=(oc == CH - 1),
                              waits=([(p.sem["v"], tv_bank)]
                                     if half == 0 else ()))
                ST["t_aev"] = p.tok("v")
                ST["t_aevs"] = p.tok("s")
                for c in range(CH):
                    def m0(e, c=c, bp=bp):
                        return e.matmul(PBS[6][0:1, 0:TB], cbf("ones1"),
                                        WKVR[:, bp, c, :], start=(c == 0),
                                        stop=(c == CH - 1))
                    p.add("pe", m0,
                          waits=([ST["t_aev"], ST["t_aevs"]] if c == 0 else ()))
                for c in range(CH):
                    def m1(e, c=c):
                        return e.matmul(PBS[7][0:1, 0:TB], cbf("ones1"),
                                        XKk[:, c, :], start=(c == 0),
                                        stop=(c == CH - 1))
                    p.add("pe", m1, inc=(c == CH - 1))
                t_st2 = p.tok("pe")
                def q1(e):
                    return e.tensor_scalar(murow, PBS[6][0:1, 0:TB], 1.0 / D,
                                           None, ALU.mult)
                p.add("v", q1, waits=[t_st2])
                def q2(e):
                    return e.tensor_tensor(scrap, murow, murow, ALU.mult)
                p.add("v", q2)
                def q3(e):
                    return e.scalar_tensor_tensor(sdrow, PBS[7][0:1, 0:TB],
                                                  1.0 / D, scrap, ALU.mult,
                                                  ALU.subtract)
                p.add("v", q3, inc=True)
                t_v2 = p.tok("v")
                def q4(e):
                    return e.activation(sdrow, sdrow, AF.Ln,
                                        bias=smw("eps")[0:1, :])
                p.add("s", q4, waits=[t_v2])
                def q6(e):
                    return e.activation(rstdrow, sdrow, AF.Exp, scale=-0.5)
                p.add("s", q6, inc=True)
                t_rws = p.tok("s")
                def bb1(e):
                    return e.matmul(PBS[6][:, 0:TB], c32("ones", 0, 1), murow,
                                    start=True, stop=True)
                p.add("pe", bb1, waits=[t_rws])
                def bb2(e):
                    return e.matmul(PBS[7][:, 0:TB], c32("ones", 0, 1),
                                    rstdrow, start=True, stop=True)
                p.add("pe", bb2, inc=True)
                ST["t_ln2pe"] = p.tok("pe")
                for c in range(CH):
                    def g1(e, c=c):
                        return e.tensor_tensor(XC[:, c, :], XC[:, c, :],
                                               PBS[6][:, 0:TB], ALU.subtract)
                    p.add("v", g1,
                          waits=([ST["t_ln2pe"]] if c == 0 else ()))
                for c in range(CH):
                    def g2(e, c=c):
                        return e.tensor_tensor(XKk[:, c, :], XC[:, c, :],
                                               PBS[7][:, 0:TB], ALU.mult)
                    p.add("v", g2)
                for c in range(CH):
                    def g3(e, c=c, xfb=xfb):
                        return e.tensor_tensor(xfb[:, c, :], xfb[:, c, :],
                                               XKk[:, c, :], ALU.add)
                    p.add("v", g3, inc=(c == CH - 1))
                ST["t_ln2v"] = p.tok("v")

            nb_run = DBG_NB if DBG_NB is not None else NB
            for it in range(nb_run + 2):
                if it < nb_run:
                    emit_ln1_mixk_kwave(it)
                if 1 <= it <= nb_run:
                    emit_slot(it - 1, 0)
                    emit_slot(it - 1, 1)
                if it == nb_run + 1 and ST["pend"] is not None:
                    emit_wkv(ST["pend"])
                    ST["pend"] = None
                if it < nb_run:
                    emit_evkv(it)
                    emit_rwave(it)
                if it >= 2 and it - 2 < nb_run:
                    emit_wo_ln2(it - 2)
            t_rwkv_done = ST["t_ln2v"]

            t_xsp = p.dma("y", "xsp", xsp_h, XF[:], waits=[t_rwkv_done])
            t_y0 = None
            v_apply = ST["t_ln2v"]
            t_st3 = None
            t_ap3 = None
            if stage >= 2:
                for b in range(NB):
                    col = b * TB
                    xfb = XF[:, :, col:col + TB]
                    ws3 = [v_apply] if b == 0 else [t_st3]
                    def sqr3(e, xfb=xfb):
                        return e.activation(XKk[:], xfb, AF.Square)
                    p.add("s", sqr3, inc=True, waits=ws3)
                    t_x2 = p.tok("s")
                    for c in range(CH):
                        def m0(e, c=c, xfb=xfb):
                            return e.matmul(PBS[6][0:1, 0:TB], cbf("ones1"),
                                            xfb[:, c, :], start=(c == 0),
                                            stop=(c == CH - 1))
                        w = []
                        if b == 0 and c == 0:
                            w = [v_apply, ST["t_aevs"]]
                        elif c == 0:
                            w = [t_ap3]
                        p.add("pe", m0, waits=w)
                    for c in range(CH):
                        def m1(e, c=c):
                            return e.matmul(PBS[7][0:1, 0:TB], cbf("ones1"),
                                            XKk[:, c, :], start=(c == 0),
                                            stop=(c == CH - 1))
                        p.add("pe", m1, inc=(c == CH - 1),
                              waits=([t_x2] if c == 0 else ()))
                    t_st3 = p.tok("pe")
                    def r1(e):
                        return e.tensor_scalar(murow, PBS[6][0:1, 0:TB],
                                               1.0 / D, None, ALU.mult)
                    p.add("v", r1, waits=[t_st3])
                    def r2(e):
                        return e.tensor_tensor(scrap, murow, murow, ALU.mult)
                    p.add("v", r2)
                    def r3(e):
                        return e.scalar_tensor_tensor(sdrow, PBS[7][0:1, 0:TB],
                                                      1.0 / D, scrap, ALU.mult,
                                                      ALU.subtract)
                    p.add("v", r3, inc=True)
                    t_var = p.tok("v")
                    def r4(e):
                        return e.activation(sdrow, sdrow, AF.Ln,
                                            bias=smw("eps")[0:1, :])
                    p.add("s", r4, waits=[t_var])
                    def r6(e):
                        return e.activation(rstdrow, sdrow, AF.Exp, scale=-0.5)
                    p.add("s", r6, inc=True)
                    t_rows3 = p.tok("s")
                    def b1(e):
                        return e.matmul(PBS[6][:, 0:TB], c32("ones", 0, 1),
                                        murow, start=True, stop=True)
                    p.add("pe", b1, waits=[t_rows3])
                    def b2(e):
                        return e.matmul(PBS[7][:, 0:TB], c32("ones", 0, 1),
                                        rstdrow, start=True, stop=True)
                    p.add("pe", b2, inc=True)
                    t_bc3 = p.tok("pe")
                    for c in range(CH):
                        def a1(e, c=c, xfb=xfb):
                            return e.tensor_tensor(XC[:, c, :], xfb[:, c, :],
                                                   PBS[6][:, 0:TB],
                                                   ALU.subtract)
                        p.add("v", a1, waits=([t_bc3] if c == 0 else ()))
                    for c in range(CH):
                        def a2(e, c=c):
                            return e.tensor_tensor(XKv[:, c, :], XC[:, c, :],
                                                   PBS[7][:, 0:TB], ALU.mult)
                        w = []
                        if c == 0 and t_y0 is not None:
                            w.append(t_y0)
                        p.add("v", a2, waits=w)
                    t_ap3 = p.tok("v")
                    t_y0 = p.dma("y", "ysp0",
                                 ysp_h[:, :, col:col + TB], XKv[:],
                                 waits=[t_ap3])
            t_front = [t_xsp, p.tok("v"), p.tok("s"), p.tok("pe")]
            if t_y0 is not None:
                t_front.append(t_y0)
            if dbg_dump and t_y0 is not None:
                t_front.append(p.dma("y", "dby", dby_h["0"], ysp_h,
                                     waits=[t_y0]))

        front_es.close()


        nlayers = 0 if stage < 2 else (1 if stage == 2 else 2)
        t_prev_writes = (p.dsem.get("ysp0"), p.dma_cnt.get("ysp0")) \
            if nlayers else None
        YB2 = sb("YB2", [128, 2, CH, FT], BF16)
        YLC = sb("YLC", [128, CH, 2], BF16)

        def col2d2(buf, i):
            return buf[:, :, i:i + 1].rearrange("p c n -> p (c n)")

        def emit_layer(l, t_prev_writes, t_front, t_gate_prev):
            pa = ExitStack()
            asb = lambda nm, shp, dty: pa.enter_context(
                nc.sbuf_tensor(nm + f"_l{l}", shp, dty))
            WQKV = asb("WQKV", [128, CH, 1536], BF16)
            WQSW = asb("WQSW", [128, CH, 1024], BF16)
            WOF = asb("WOF", [64, CH, 1024], BF16)
            MSK = asb("MSK", [128, 512 + 128 + 2048], BF16)
            HQ = asb("HQ", [128, CH, FT], BF16)
            KR = asb("KR", [128, 4, 1024], BF16)
            VR = asb("VR", [128, 8, 512], BF16)
            QH = asb("QH", [128, 4, FT], BF16)
            QA = asb("QA", [128, 4, FT], F32)
            QB2 = asb("QB2", [128, 4, FT], F32)
            RT = asb("RT", [128, 4, FT], F32)
            SQB = asb("SQB", [128, 4, FT], BF16)
            PRB = asb("PRB", [128, 2, 640], BF16)
            PTB = asb("PTB", [128, 2, 640], BF16)
            RRB = asb("RRB", [8, 1024], BF16)
            OH = asb("OH", [64, CH, FT], BF16)
            PA = [pa.enter_context(
                nc.psum_tensor(f"pa{i}_l{l}", [128, 512], F32))
                for i in range(4)]
            BROW = pa.enter_context(
                nc.psum_tensor(f"brow_l{l}", [128, 512], F32))
            BSA = pa.enter_context(
                nc.psum_tensor(f"bsa_l{l}", [128, 512], F32))
            PBF2 = [pa.enter_context(
                nc.psum_tensor(f"pbf{i}_l{l}", [128, 640], BF16))
                for i in range(2)]

            gate = (list(t_front) if l == 0 else list(t_gate_prev)) \
                + p.all_dma_toks()
            t_wfa = p.dma("g", f"wfa{l}", WQKV[:],
                          wqkv_h[l].rearrange("(c p) o -> p c o", p=128),
                          waits=gate)
            p.dma("g", f"wfa{l}", WQSW[:],
                  wqsw_h[l].rearrange("(c p) o -> p c o", p=128))
            p.dma("g", f"wfa{l}", WOF[:],
                  wof_h[l].rearrange("(kc p) o -> p kc o", p=64))
            m_w = 512 + 128 + 2048
            def mskload():
                o0, _ = cbf_off["maskA"]
                return cstbf_h[:, o0:o0 + m_w]
            t_wfa = p.dma("g", f"wfa{l}", MSK[:], mskload())
            mA_off, mB_off, mA0_off = 0, 512, 640

            def zKR(e):
                return e.memset(KR[:, :, 0:512], 0.0)
            p.add("g", zKR, waits=gate)
            def zVR(e):
                return e.memset(VR[:, 0:4, :], 0.0)
            p.add("g", zVR)
            t_ring0 = p.tok("g")
            def zYL(e):
                return e.tensor_scalar(col2d2(YLC, 0), smw("w"), 0.0, 0.0,
                                       ALU.mult, ALU.add)
            p.add("v", zYL, waits=gate)

            t_store_par = [None, None]
            t_ringcp = t_ring0
            for fb in range(NF):
                col = fb * FT
                ypar = fb % 2
                YBF = YB2[:, ypar]
                w_l = (list(t_prev_writes)
                       if isinstance(t_prev_writes, list)
                       else [t_prev_writes])
                if t_store_par[ypar] is not None:
                    w_l.append(t_store_par[ypar])
                if fb == 0:
                    w_l.extend(gate)
                t_yl = p.dma("y", f"ya{l}", YBF[:],
                             ysp_h[:, :, col:col + FT], waits=w_l)
                t_rt = p.dma("y", f"rt{l}{fb % 2}", RT[:],
                             rot_h[:, :, col:col + FT].rearrange(
                                 "r p t -> p r t"),
                             waits=[(p.sem["v"], p.cnt["v"])])
                def hq1(e, YBF=YBF):
                    return e.tensor_copy(HQ[:, 0:4, :], YBF[:, 0:4, :])
                p.add("v", hq1, waits=[t_yl])
                def hq2(e, YBF=YBF):
                    return e.tensor_copy(HQ[:, 4:8, 1:FT],
                                         YBF[:, 4:8, 0:FT - 1])
                p.add("v", hq2)
                def hq3(e):
                    return e.tensor_copy(HQ[:, 4:8, 0:1], YLC[:, 4:8, 0:1])
                p.add("v", hq3)
                def ylup(e, YBF=YBF):
                    return e.tensor_copy(col2d2(YLC, 0),
                                         YBF[:, :, FT - 1:FT]
                                         .rearrange("p c n -> p (c n)"))
                p.add("v", ylup)
                t_hq = p.tok("v")

                for tc in range(4):
                    for kc in range(CH):
                        def mv(e, tc=tc, kc=kc):
                            return e.matmul(
                                PA[tc][:], HQ[:, kc, tc * 128:tc * 128 + 128],
                                WQKV[:, kc, 1024:1536],
                                start=(kc == 0), stop=(kc == CH - 1))
                        w = []
                        if tc == 0 and kc == 0:
                            w = [t_hq, t_wfa, t_ring0]
                        p.add("pe", mv, inc=(tc == 3 and kc == CH - 1),
                              waits=w)
                t_vw = p.tok("pe")
                for tc in range(4):
                    def ev(e, tc=tc):
                        return e.tensor_copy(VR[:, 4 + tc, :], PA[tc][:])
                    w = []
                    if tc == 0:
                        w = [t_vw, t_ringcp]
                    p.add("v", ev, waits=w)
                t_vev = p.tok("v")

                def wave(dst_off, swapped, waits0):
                    W_, cols = (WQSW, (dst_off, dst_off + 512)) if swapped \
                        else (WQKV, (dst_off, dst_off + 512))
                    for oc in range(4):
                        for kc in range(CH):
                            def mm(e, oc=oc, kc=kc, W_=W_, cols=cols):
                                return e.matmul(
                                    PA[oc][:],
                                    W_[:, kc, cols[0] + oc * 128:
                                       cols[0] + oc * 128 + 128],
                                    HQ[:, kc, :],
                                    start=(kc == 0), stop=(kc == CH - 1))
                            w = list(waits0) if (oc == 0 and kc == 0) else []
                            p.add("pe", mm,
                                  inc=(oc == 3 and kc == CH - 1), waits=w)
                    return p.tok("pe")

                def qk_pipeline(dst_off, qsnm, qswnm, rows_cols, rt_cs, rt_sn,
                                out_to_kr):
                    t_w = wave(dst_off, False, [p.tok("v"), p.tok("s")])
                    for c in range(4):
                        def sq(e, c=c):
                            return e.activation(SQB[:, c, :], PA[c][:],
                                                AF.Square)
                        p.add("s", sq, waits=([t_w] if c == 0 else ()))
                    t_sq = p.tok("s")
                    for c in range(4):
                        def qa(e, c=c, rt_cs=rt_cs, qsnm=qsnm):
                            return e.scalar_tensor_tensor(
                                QA[:, c, :], PA[c][:], smw(qsnm),
                                RT[:, rt_cs, :], ALU.mult, ALU.mult)
                        p.add("v", qa, waits=([t_sq, t_rt] if c == 0 else ()))
                    t_qa = p.tok("v")
                    for c in range(4):
                        def sm(e, c=c):
                            return e.matmul(
                                BROW[0:8, 0:512],
                                cbf("hsum")[:, c * 8:c * 8 + 8],
                                SQB[:, c, :], start=(c == 0), stop=(c == 3))
                        p.add("pe", sm, inc=(c == 3),
                              waits=([t_sq] if c == 0 else ()))
                    t_sums = p.tok("pe")
                    t_w2 = wave(dst_off, True, [t_qa])
                    def rc(e, rows_cols=rows_cols):
                        return e.tensor_copy(
                            RRF[0:8, rows_cols:rows_cols + 512],
                            BROW[0:8, 0:512])
                    p.add("v", rc, waits=[t_sums])
                    def rm(e, rows_cols=rows_cols):
                        return e.tensor_scalar_max(
                            RRF[0:8, rows_cols:rows_cols + 512],
                            RRF[0:8, rows_cols:rows_cols + 512], 1e-24)
                    p.add("v", rm)
                    t_rv = p.tok("v")
                    def rl(e, rows_cols=rows_cols):
                        return e.activation(
                            RRF[0:8, rows_cols:rows_cols + 512],
                            RRF[0:8, rows_cols:rows_cols + 512], AF.Ln)
                    p.add("s", rl, waits=[t_rv])
                    def re_(e, rows_cols=rows_cols):
                        return e.activation(
                            RRB[0:8, rows_cols:rows_cols + 512],
                            RRF[0:8, rows_cols:rows_cols + 512], AF.Exp,
                            scale=-0.5)
                    p.add("s", re_)
                    t_rr = p.tok("s")
                    for c in range(4):
                        def qb(e, c=c, rt_sn=rt_sn, qswnm=qswnm):
                            return e.scalar_tensor_tensor(
                                QB2[:, c, :], PA[c][:], smw(qswnm),
                                RT[:, rt_sn, :], ALU.mult, ALU.mult)
                        p.add("v", qb, waits=([t_w2] if c == 0 else ()))
                    t_qb = p.tok("v")
                    def gadd(e):
                        return e.tensor_tensor(QA[:], QA[:], QB2[:], ALU.add)
                    p.add("v", gadd, waits=[t_qb])
                    t_g = p.tok("v")
                    for c in range(4):
                        def bc(e, c=c, rows_cols=rows_cols):
                            return e.matmul(
                                PA[c][:, 0:512],
                                cbf("hbc", 0, 8)[:, c * 128:c * 128 + 128],
                                RRB[0:8, rows_cols:rows_cols + 512],
                                start=True, stop=True)
                        w = [t_rr, t_g, t_qb] if c == 0 else []
                        p.add("pe", bc, inc=(c == 3), waits=w)
                    t_bcq = p.tok("pe")
                    for c in range(4):
                        if out_to_kr:
                            def fm(e, c=c):
                                return e.tensor_tensor(
                                    KR[:, c, 512:1024], QA[:, c, :],
                                    PA[c][:, 0:512], ALU.mult)
                        else:
                            def fm(e, c=c):
                                return e.tensor_tensor(
                                    QH[:, c, :], QA[:, c, :],
                                    PA[c][:, 0:512], ALU.mult)
                        p.add("v", fm, waits=([t_bcq] if c == 0 else ()))
                    return p.tok("v")

                t_qh = qk_pipeline(0, f"qs{l}", f"qsw{l}", 0, 0, 1, False)
                t_kh = qk_pipeline(512, f"ks{l}", f"ksw{l}", 512, 2, 3, True)

                its = [(h, qc) for h in range(HEADS) for qc in range(4)]
                t_ex_par = [None, None]
                t_pv_par = [None, None]
                t_ptv_par = [None, None]
                t_oev = None
                st_pn = {}
                st_tp = {}

                def emit_scores(it):
                    h, qc = its[it]
                    par = it % 2
                    w0 = 128 * qc
                    pq = (h % 2) * 64
                    tcp = h // 2
                    sA = PA[par]
                    sB = PA[2 + par]
                    def s1m(e, pq=pq, tcp=tcp, w0=w0, sA=sA):
                        return e.matmul(
                            sA[:, 0:512],
                            QH[pq:pq + 64, tcp, w0:w0 + 128],
                            KR[pq:pq + 64, tcp, w0:w0 + 512],
                            start=True, stop=False)
                    w = [t_qh, t_kh] if it == 0 else []
                    if t_ex_par[par] is not None:
                        w = w + [t_ex_par[par]]
                    p.add("pe", s1m, waits=w)
                    if fb == 0:
                        mA = MSK[:, mA0_off + 512 * qc:mA0_off + 512 * qc + 512]
                    else:
                        mA = MSK[:, mA_off:mA_off + 512]
                    def s1k(e, mA=mA, sA=sA):
                        return e.matmul(sA[:, 0:512], cbf("ident"), mA,
                                        start=False, stop=True)
                    p.add("pe", s1k)
                    def s2m(e, pq=pq, tcp=tcp, w0=w0, sB=sB):
                        return e.matmul(
                            sB[:, 0:128],
                            QH[pq:pq + 64, tcp, w0:w0 + 128],
                            KR[pq:pq + 64, tcp, w0 + 512:w0 + 640],
                            start=True, stop=False)
                    p.add("pe", s2m)
                    def s2k(e, sB=sB):
                        return e.matmul(sB[:, 0:128], cbf("ident"),
                                        MSK[:, mB_off:mB_off + 128],
                                        start=False, stop=True)
                    p.add("pe", s2k, inc=True)
                    return p.tok("pe")

                st_ex = {}

                def emit_vmex(it, t_sc):
                    par = it % 2
                    sA = PA[par]
                    sB = PA[2 + par]
                    so = 1536 + 6 * par
                    nmA = RRF[:, so:so + 1]
                    nmB = RRF[:, so + 1:so + 2]
                    nm = RRF[:, so + 2:so + 3]
                    s1c = RRF[:, so + 3:so + 4]
                    s2c = RRF[:, so + 4:so + 5]
                    def vm1(e, sA=sA, nmA=nmA):
                        return e.reduce_max(nmA, sA[:, 0:512], axis=AX.X,
                                            negate=True)
                    p.add("v", vm1, waits=[t_sc])
                    def vm2(e, sB=sB, nmB=nmB):
                        return e.reduce_max(nmB, sB[:, 0:128], axis=AX.X,
                                            negate=True)
                    p.add("v", vm2)
                    def vm3(e, nm=nm, nmA=nmA, nmB=nmB):
                        return e.tensor_tensor(nm, nmA, nmB, ALU.min)
                    p.add("v", vm3)
                    t_nm = p.tok("v")
                    def ex1(e, sA=sA, nm=nm, s1c=s1c, par=par):
                        return e.activation(PRB[:, par, 0:512], sA[:, 0:512],
                                            AF.Exp, bias=nm, accum_out=s1c)
                    p.add("s", ex1, waits=[t_nm])
                    def ex2(e, sB=sB, nm=nm, s2c=s2c, par=par):
                        return e.activation(PRB[:, par, 512:640],
                                            sB[:, 0:128], AF.Exp,
                                            bias=nm, accum_out=s2c)
                    p.add("s", ex2, inc=True)
                    st_ex[it] = p.tok("s")
                    t_ex_par[par] = st_ex[it]

                def emit_vnorm(it):
                    par = it % 2
                    so = 1536 + 6 * par
                    s1c = RRF[:, so + 3:so + 4]
                    s2c = RRF[:, so + 4:so + 5]
                    def vn1(e, s1c=s1c, s2c=s2c):
                        return e.tensor_tensor(s1c, s1c, s2c, ALU.add)
                    p.add("v", vn1, waits=[st_ex[it]])
                    def vn2(e, s1c=s1c):
                        return e.reciprocal(s1c, s1c)
                    p.add("v", vn2)
                    def vn3(e, s1c=s1c, par=par):
                        return e.tensor_scalar(PRB[:, par, :], PRB[:, par, :],
                                               s1c, None, ALU.mult)
                    p.add("v", vn3)
                    st_pn[it] = p.tok("v")

                def emit_tp(it):
                    par = it % 2
                    for kk in range(5):
                        def tp(e, kk=kk, par=par):
                            return e.transpose(
                                PBF2[par][:, kk * 128:kk * 128 + 128],
                                PRB[:, par, kk * 128:kk * 128 + 128],
                                cbf("ident"))
                        w = [st_pn[it]] if kk == 0 else []
                        if kk == 0 and t_ptv_par[par] is not None:
                            w.append(t_ptv_par[par])
                        p.add("pe", tp, inc=(kk == 4), waits=w)
                    st_tp[it] = p.tok("pe")

                def emit_ptv(it):
                    par = it % 2
                    def ptv(e, par=par):
                        return e.activation(PTB[:, par, :], PBF2[par][:],
                                            AF.Copy)
                    w = [st_tp[it]]
                    if t_pv_par[par] is not None:
                        w.append(t_pv_par[par])
                    t = (p.sem["s"], p.add("s", ptv, waits=w))
                    t_ptv_par[par] = t
                    return t

                def emit_pv(it, t_ptv):
                    nonlocal t_oev
                    h, qc = its[it]
                    par = it % 2
                    for kk in range(5):
                        def pv(e, kk=kk, h=h, qc=qc, par=par):
                            return e.matmul(
                                BSA[0:64, 0:128],
                                VR[:, qc + kk, h * 64:h * 64 + 64],
                                PTB[:, par, kk * 128:kk * 128 + 128],
                                start=(kk == 0), stop=(kk == 4))
                        w = [t_ptv] if kk == 0 else []
                        if kk == 0 and t_oev is not None:
                            w.append(t_oev)
                        p.add("pe", pv, inc=(kk == 4), waits=w)
                    t_pv = p.tok("pe")
                    t_pv_par[par] = t_pv
                    def oev(e, h=h, qc=qc):
                        return e.tensor_copy(
                            OH[0:64, h, qc * 128:qc * 128 + 128],
                            BSA[0:64, 0:128])
                    t_oev = (p.sem["v"], p.add("v", oev, waits=[t_pv]))

                sc_tok = {}
                ptv_tok = {}
                n_its = len(its)
                for it in range(n_its + 3):
                    if it < n_its:
                        sc_tok[it] = emit_scores(it)
                    if 1 <= it <= n_its:
                        emit_vmex(it - 1, sc_tok[it - 1])
                        emit_vnorm(it - 1)
                    if 2 <= it <= n_its + 1:
                        emit_tp(it - 2)
                        ptv_tok[it - 2] = emit_ptv(it - 2)
                    if 3 <= it <= n_its + 2:
                        emit_pv(it - 3, ptv_tok[it - 3])
                t_oh = p.tok("v")

                for half in range(2):
                    for oc4 in range(4):
                        oc = half * 4 + oc4
                        for kc in range(CH):
                            def mo(e, oc=oc, oc4=oc4, kc=kc):
                                return e.matmul(
                                    PA[oc4][:],
                                    WOF[0:64, kc, oc * 128:oc * 128 + 128],
                                    OH[0:64, kc, :], start=(kc == 0),
                                    stop=(kc == CH - 1))
                            w = []
                            if oc4 == 0 and kc == 0:
                                w = [t_oh] if half == 0 else [p.tok("v")]
                            p.add("pe", mo, inc=(oc4 == 3 and kc == CH - 1),
                                  waits=w)
                    t_wo = p.tok("pe")
                    for oc4 in range(4):
                        oc = half * 4 + oc4
                        def yup(e, oc=oc, oc4=oc4, YBF=YBF):
                            return e.tensor_tensor(YBF[:, oc, :],
                                                   YBF[:, oc, :],
                                                   PA[oc4][:], ALU.add)
                        p.add("v", yup, waits=([t_wo] if oc4 == 0 else ()))
                t_yb = p.tok("v")
                t_store_par[ypar] = p.dma("y", f"yw{l}a{ypar}",
                                          ysp_h[:, :, col:col + FT], YBF[:],
                                          waits=[t_yb])
                t_pe_rb = p.tok("pe")
                p.dma("y", f"rg{l}", KR[:, :, 0:512], KR[:, :, 512:1024],
                      waits=[t_pe_rb])
                t_ringcp = p.dma("y", f"rg{l}", VR[:, 0:4, :], VR[:, 4:8, :])
            t_pa_done = p.tok("pe")
            t_prev_writes = [(p.dsem[f"yw{l}a0"], p.dma_cnt[f"yw{l}a0"]),
                             (p.dsem[f"yw{l}a1"], p.dma_cnt[f"yw{l}a1"])]
            if dbg_dump:
                t_prev_writes2 = p.dma("y", "dby", dby_h[f"a{l}"], ysp_h,
                                       waits=list(t_prev_writes))
                t_prev_writes = [t_prev_writes2]
            pa.close()

            pb = ExitStack()
            bsb = lambda nm, shp, dty: pb.enter_context(
                nc.sbuf_tensor(nm + f"_l{l}", shp, dty))
            W1I = bsb("W1I", [128, CH, NSLOT * 128], BF16)
            W2P = bsb("W2P", [128, NJ, 1024], BF16)
            HACT = bsb("HACT", [128, NJ, FT], BF16)
            HB = bsb("HB", [128, CH, FT], BF16)
            SQ2 = bsb("SQ2", [128, CH, FT], BF16)
            TMP1 = bsb("TMP1", [128, FT], F32)
            RSB = bsb("RSB", [1, FT], BF16)
            PB_ = [pb.enter_context(
                nc.psum_tensor(f"pb{i}_l{l}", [128, 512], F32))
                for i in range(4)]
            PBA = pb.enter_context(
                nc.psum_tensor(f"pba_l{l}", [128, 512], F32))
            PBG = pb.enter_context(
                nc.psum_tensor(f"pbg_l{l}", [128, 512], F32))
            PR0 = pb.enter_context(
                nc.psum_tensor(f"pr0_l{l}", [128, 512], F32))
            PR1 = pb.enter_context(
                nc.psum_tensor(f"pr1_l{l}", [128, 512], F32))

            w1r = w1i_h[l].rearrange("(c p) s -> p c s", p=128)
            gateb = [t_pa_done, p.tok("v"), p.tok("s"),
                     p.tok("g")] + p.all_dma_toks()
            W1GRP = 2
            wfb_grp = []
            for jj in range(0, NJ, W1GRP):
                je = min(jj + W1GRP, NJ)
                tk = p.dma("g", f"wfb{l}g{jj // W1GRP}",
                           W1I[:, :, 2 * jj * 128:2 * je * 128],
                           w1r[:, :, 2 * jj * 128:2 * je * 128],
                           waits=(gateb if jj == 0 else ()))
                wfb_grp.append(tk)
            w2r = w2p_h[l].rearrange("(j p) o -> p j o", p=128)
            for jj in range(0, NJ, 4):
                je = min(jj + 4, NJ)
                t_wfb = p.dma("g", f"wfb{l}", W2P[:, jj:je, :],
                              w2r[:, jj:je, :])
            t_wfb_w2p = t_wfb

            def zYL2(e):
                return e.tensor_scalar(col2d2(YLC, 1), smw("w"), 0.0, 0.0,
                                       ALU.mult, ALU.add)
            p.add("v", zYL2)

            t_store_par = [None, None]
            for fb in range(NF):
                col = fb * FT
                ypar = fb % 2
                YBF = YB2[:, ypar]
                w_l = (list(t_prev_writes)
                       if isinstance(t_prev_writes, list)
                       else [t_prev_writes])
                if t_store_par[ypar] is not None:
                    w_l.append(t_store_par[ypar])
                t_yl = p.dma("y", f"yb{l}", YBF[:],
                             ysp_h[:, :, col:col + FT], waits=w_l)
                def c1(e, YBF=YBF):
                    return e.tensor_copy(HB[:, 0:4, :], YBF[:, 0:4, :])
                p.add("v", c1, waits=[t_yl])
                def c2(e, YBF=YBF):
                    return e.tensor_copy(HB[:, 4:8, 1:FT],
                                         YBF[:, 4:8, 0:FT - 1])
                p.add("v", c2)
                def c3(e):
                    return e.tensor_copy(HB[:, 4:8, 0:1], YLC[:, 4:8, 1:2])
                p.add("v", c3)
                t_cb = p.tok("v")
                def sq1(e, YBF=YBF):
                    return e.activation(SQ2[:, 0:4, :], YBF[:, 0:4, :],
                                        AF.Square)
                p.add("s", sq1, waits=[t_yl, (p.sem["v"], p.cnt["v"])])
                def sq2_(e, YBF=YBF):
                    return e.activation(SQ2[:, 4:8, 1:FT],
                                        YBF[:, 4:8, 0:FT - 1], AF.Square)
                p.add("s", sq2_)
                def sq3(e):
                    return e.activation(SQ2[:, 4:8, 0:1], YLC[:, 4:8, 1:2],
                                        AF.Square)
                p.add("s", sq3)
                t_sq = p.tok("s")
                for c in range(CH):
                    def m0(e, c=c):
                        return e.matmul(PR0[0:1, 0:FT], cbf("ones1"),
                                        HB[:, c, :], start=(c == 0),
                                        stop=(c == CH - 1))
                    p.add("pe", m0, waits=([t_cb] if c == 0 else ()))
                for c in range(CH):
                    def m1(e, c=c):
                        return e.matmul(PR1[0:1, 0:FT], cbf("ones1"),
                                        SQ2[:, c, :], start=(c == 0),
                                        stop=(c == CH - 1))
                    p.add("pe", m1, inc=(c == CH - 1),
                          waits=([t_sq] if c == 0 else ()))
                t_st = p.tok("pe")
                mrow = RRF[0:1, 1024:1024 + FT]
                srow = RRF[0:1, 0:FT]
                xrow = RRF[0:1, 512:512 + FT]
                def q1(e):
                    return e.tensor_scalar(mrow, PR0[0:1, 0:FT], 1.0 / D,
                                           None, ALU.mult)
                p.add("v", q1, waits=[t_st])
                def q2(e):
                    return e.tensor_tensor(xrow, mrow, mrow, ALU.mult)
                p.add("v", q2)
                def q3(e):
                    return e.scalar_tensor_tensor(srow, PR1[0:1, 0:FT],
                                                  1.0 / D, xrow, ALU.mult,
                                                  ALU.subtract)
                p.add("v", q3)
                t_v2 = p.tok("v")
                def q4(e):
                    return e.activation(srow, srow, AF.Ln,
                                        bias=smw("eps")[0:1, :])
                p.add("s", q4, waits=[t_v2])
                def q6(e):
                    return e.activation(RSB[0:1, 0:FT], srow, AF.Exp,
                                        scale=-0.5)
                p.add("s", q6)
                t_rw = p.tok("s")
                def bb1(e):
                    return e.matmul(PR0[:, 0:FT], c32("ones", 0, 1), mrow,
                                    start=True, stop=True)
                p.add("pe", bb1, waits=[t_rw])
                def bb2(e):
                    return e.matmul(PR1[:, 0:FT], cbf("onesr", 0, 1),
                                    RSB[0:1, 0:FT], start=True, stop=True)
                p.add("pe", bb2, inc=True)
                t_bc = p.tok("pe")
                for c in range(CH):
                    if c < 4:
                        def ap1(e, c=c, YBF=YBF):
                            return e.tensor_tensor(TMP1[:], YBF[:, c, :],
                                                   PR0[:, 0:FT],
                                                   ALU.subtract)
                        p.add("v", ap1, waits=([t_bc] if c == 0 else ()))
                    else:
                        def ap1(e, c=c, YBF=YBF):
                            return e.tensor_tensor(TMP1[:, 1:FT],
                                                   YBF[:, c, 0:FT - 1],
                                                   PR0[:, 1:FT],
                                                   ALU.subtract)
                        p.add("v", ap1, waits=([t_bc] if c == 0 else ()))
                        def ap1b(e, c=c):
                            return e.tensor_tensor(TMP1[:, 0:1],
                                                   YLC[:, c, 1:2],
                                                   PR0[:, 0:1], ALU.subtract)
                        p.add("v", ap1b)
                    def ap2(e, c=c):
                        return e.scalar_tensor_tensor(
                            TMP1[:], TMP1[:], smw(f"lng{l}")[:, c:c + 1],
                            PR1[:, 0:FT], ALU.mult, ALU.mult)
                    p.add("v", ap2)
                    def ap3(e, c=c):
                        return e.tensor_scalar(HB[:, c, :], TMP1[:],
                                               smw(f"lnb{l}")[:, c:c + 1],
                                               None, ALU.add)
                    p.add("v", ap3)
                t_h1 = p.tok("v")
                if dbg_dump and l == 0 and fb == 0:
                    t_dbhb = p.dma("g", "dbg2", dbhb_h, HB[:], waits=[t_h1])
                    t_dw1 = p.dma("g", "dbg2", dbw1_h, W1I[:],
                                  waits=[t_h1, t_wfb])
                    p.add("v", lambda e: e.engine_nop(), waits=[t_dw1])
                def ylup2(e, YBF=YBF):
                    return e.tensor_copy(col2d2(YLC, 1),
                                         YBF[:, :, FT - 1:FT]
                                         .rearrange("p c n -> p (c n)"))
                p.add("v", ylup2, waits=[t_sq])
                t_evj = [None, None]
                for j in range(NJ):
                    par = j % 2
                    bA = PBA if par == 0 else PB_[0]
                    bG = PBG if par == 0 else PB_[1]
                    cA, cG = (1, 0) if par == 0 else (3, 2)
                    for kc in range(CH):
                        def ma(e, j=j, kc=kc, bA=bA):
                            return e.matmul(
                                bA[:], W1I[:, kc,
                                           (2 * j) * 128:(2 * j) * 128 + 128],
                                HB[:, kc, :], start=(kc == 0),
                                stop=(kc == CH - 1))
                        w = []
                        if kc == 0:
                            w = [t_h1] if j == 0 else (
                                [t_evj[par]] if t_evj[par] else [])
                            if fb == 0 and j % W1GRP == 0:
                                w = list(w) + [wfb_grp[j // W1GRP]]
                        p.add("pe", ma, waits=w)
                    for kc in range(CH):
                        def mg(e, j=j, kc=kc, bG=bG):
                            return e.matmul(
                                bG[:], W1I[:, kc,
                                           (2 * j + 1) * 128:
                                           (2 * j + 1) * 128 + 128],
                                HB[:, kc, :], start=(kc == 0),
                                stop=(kc == CH - 1))
                        p.add("pe", mg, inc=(kc == CH - 1))
                    t_j = p.tok("pe")
                    def gg(e, j=j, bG=bG, cG=cG):
                        return e.activation(SQ2[:, cG, 0:FT], bG[:, 0:FT],
                                            AF.Gelu,
                                            bias=smw(f"b1i{l}")
                                            [:, 2 * j + 1:2 * j + 2])
                    p.add("s", gg, waits=[t_j])
                    def aa(e, j=j, bA=bA, cA=cA):
                        return e.tensor_scalar(SQ2[:, cA, 0:FT], bA[:, 0:FT],
                                               smw(f"b1i{l}")
                                               [:, 2 * j:2 * j + 1],
                                               None, ALU.add)
                    p.add("v", aa, waits=[t_j])
                    t_ag = p.tok("s")
                    def hmul(e, j=j, cA=cA, cG=cG):
                        return e.tensor_tensor(HACT[:, j, :], SQ2[:, cA, 0:FT],
                                               SQ2[:, cG, 0:FT], ALU.mult)
                    t_evj[par] = (p.sem["v"], p.add("v", hmul, waits=[t_ag]))
                t_hact = p.tok("v")
                if dbg_dump and l == 0 and fb == 0:
                    t_dbha = p.dma("g", "dbg2", dbha_h, HACT[:],
                                   waits=[t_hact])
                    p.add("v", lambda e: e.engine_nop(),
                          waits=[t_dbha])
                for half in range(2):
                    for oc4 in range(4):
                        oc = half * 4 + oc4
                        for j in range(NJ):
                            def m2(e, oc=oc, oc4=oc4, j=j):
                                return e.matmul(
                                    PB_[oc4][:],
                                    W2P[:, j, oc * 128:oc * 128 + 128],
                                    HACT[:, j, :], start=(j == 0),
                                    stop=(j == NJ - 1))
                            w = []
                            if oc4 == 0 and j == 0:
                                w = [t_hact] if half == 0 else [p.tok("v")]
                                if half == 0 and fb == 0:
                                    w.append(t_wfb_w2p)
                            p.add("pe", m2,
                                  inc=(oc4 == 3 and j == NJ - 1), waits=w)
                    t_w2 = p.tok("pe")
                    for oc4 in range(4):
                        oc = half * 4 + oc4
                        def yup(e, oc=oc, oc4=oc4, YBF=YBF):
                            return e.scalar_tensor_tensor(
                                YBF[:, oc, :], PB_[oc4][:],
                                smw(f"b2_{l}")[:, oc:oc + 1], YBF[:, oc, :],
                                ALU.add, ALU.add)
                        p.add("v", yup, waits=([t_w2] if oc4 == 0 else ()))
                t_yb2 = p.tok("v")
                t_store_par[ypar] = p.dma("y", f"yw{l}b{ypar}",
                                          ysp_h[:, :, col:col + FT], YBF[:],
                                          waits=[t_yb2])
            t_pb_done = p.tok("pe")
            t_prev_writes = [(p.dsem[f"yw{l}b0"], p.dma_cnt[f"yw{l}b0"]),
                             (p.dsem[f"yw{l}b1"], p.dma_cnt[f"yw{l}b1"])]
            if dbg_dump:
                t_prev_writes = [p.dma("y", "dby", dby_h[f"b{l}"], ysp_h,
                                       waits=list(t_prev_writes))]
            t_gate_prev = [t_pb_done, p.tok("v"), p.tok("s"), p.tok("g")]
            pb.close()
            return t_prev_writes, t_gate_prev

        t_gate_prev = None
        for l in range(nlayers):
            t_prev_writes, t_gate_prev = emit_layer(
                l, t_prev_writes, t_front, t_gate_prev)

        with nc.sbuf_tensor("XO", [128, CH, FT], BF16) as XO, \
             nc.sbuf_tensor("XOF", [128, CH, FT], F32) as XOF, \
             nc.sbuf_tensor("YF4", [128, CH, FT], BF16) as YF4, \
             nc.sbuf_tensor("CB4", [128, CH, FT], BF16) as CB4, \
             nc.sbuf_tensor("SQ4", [128, CH, FT], BF16) as SQ4, \
             nc.sbuf_tensor("TMP4", [128, FT], F32) as TMP4, \
             nc.sbuf_tensor("RW4", [1, 1536], F32) as RW4, \
             nc.sbuf_tensor("RSB4", [1, FT], BF16) as RSB4, \
             nc.sbuf_tensor("OT", [128, 4, D], F32) as OT, \
             nc.psum_tensor("PF0", [128, 512], F32) as PF0, \
             nc.psum_tensor("PF1", [128, 512], F32) as PF1, \
             nc.psum_tensor("FR0", [128, 512], F32) as FR0, \
             nc.psum_tensor("FR1", [128, 512], F32) as FR1:
            pt_evtok = {0: None, 1: None}
            out_dma_tok = {tc: None for tc in range(4)}
            last_pe_tok = None
            if stage < 2:
                t_front = [t_xsp, p.tok("v"), p.tok("s"), p.tok("pe")]
                t_prev_writes = None
            for fb in range(NF):
                col = fb * FT
                w_lx = [t_xsp] + (list(t_front[1:2]) if fb == 0 else [])
                if fb > 0:
                    w_lx.append(prev_vgate)
                if fb == 0:
                    w_lx.append((p.sem["v"], p.cnt["v"]))
                    w_lx.append((p.sem["s"], p.cnt["s"]))
                    w_lx.append(p.tok("pe"))
                    w_lx.extend(p.all_dma_toks())
                t_lx = p.dma("y", "xo", XO[:], xsp_h[:, :, col:col + FT],
                             waits=w_lx)
                t_vgate = None
                if stage >= 2:
                    wly = (list(t_prev_writes)
                           if isinstance(t_prev_writes, list)
                           else [t_prev_writes])
                    t_ly = p.dma("y", "yf4", YF4[:],
                                 ysp_h[:, :, col:col + FT],
                                 waits=wly + [t_lx])
                    def c4a(e):
                        return e.tensor_copy(CB4[:], YF4[:])
                    p.add("v", c4a, waits=[t_ly])
                    t_c4 = p.tok("v")
                    def s4a(e):
                        return e.activation(SQ4[:], YF4[:], AF.Square)
                    p.add("s", s4a, waits=[t_ly, (p.sem["v"], p.cnt["v"])])
                    t_s4 = p.tok("s")
                    for c in range(CH):
                        def m0(e, c=c):
                            return e.matmul(FR0[0:1, 0:FT], cbf("ones1"),
                                            CB4[:, c, :], start=(c == 0),
                                            stop=(c == CH - 1))
                        p.add("pe", m0, waits=([t_c4] if c == 0 else ()))
                    for c in range(CH):
                        def m1(e, c=c):
                            return e.matmul(FR1[0:1, 0:FT], cbf("ones1"),
                                            SQ4[:, c, :], start=(c == 0),
                                            stop=(c == CH - 1))
                        p.add("pe", m1, inc=(c == CH - 1),
                              waits=([t_s4] if c == 0 else ()))
                    t_st4 = p.tok("pe")
                    mrow = RW4[0:1, 0:FT]
                    srow = RW4[0:1, 512:512 + FT]
                    xrow = RW4[0:1, 1024:1024 + FT]
                    def q1(e):
                        return e.tensor_scalar(mrow, FR0[0:1, 0:FT], 1.0 / D,
                                               None, ALU.mult)
                    p.add("v", q1, waits=[t_st4])
                    def q2(e):
                        return e.tensor_tensor(xrow, mrow, mrow, ALU.mult)
                    p.add("v", q2)
                    def q3(e):
                        return e.scalar_tensor_tensor(srow, FR1[0:1, 0:FT],
                                                      1.0 / D, xrow,
                                                      ALU.mult, ALU.subtract)
                    p.add("v", q3)
                    t_v4 = p.tok("v")
                    def q4(e):
                        return e.activation(srow, srow, AF.Ln,
                                            bias=smw("eps")[0:1, :])
                    p.add("s", q4, waits=[t_v4])
                    def q6(e):
                        return e.activation(RSB4[0:1, 0:FT], srow, AF.Exp,
                                            scale=-0.5)
                    p.add("s", q6)
                    t_r4 = p.tok("s")
                    def bb1(e):
                        return e.matmul(FR0[:, 0:FT], c32("ones", 0, 1),
                                        mrow, start=True, stop=True)
                    p.add("pe", bb1, waits=[t_r4])
                    def bb2(e):
                        return e.matmul(FR1[:, 0:FT], cbf("onesr", 0, 1),
                                        RSB4[0:1, 0:FT], start=True,
                                        stop=True)
                    p.add("pe", bb2, inc=True)
                    t_bc4 = p.tok("pe")
                    for c in range(CH):
                        def f1(e, c=c):
                            return e.tensor_tensor(TMP4[:], YF4[:, c, :],
                                                   FR0[:, 0:FT],
                                                   ALU.subtract)
                        w = [t_bc4, t_lx] if c == 0 else []
                        if c == 0 and last_pe_tok is not None:
                            w.append(last_pe_tok)
                        p.add("v", f1, waits=w)
                        def f2(e, c=c):
                            return e.tensor_tensor(TMP4[:], TMP4[:],
                                                   FR1[:, 0:FT], ALU.mult)
                        p.add("v", f2)
                        def f3(e, c=c):
                            return e.tensor_tensor(XOF[:, c, :], XO[:, c, :],
                                                   TMP4[:], ALU.add)
                        p.add("v", f3)
                    t_vgate = p.tok("v")
                else:
                    def cpx(e):
                        return e.tensor_copy(XOF[:], XO[:])
                    wc = [t_lx]
                    if last_pe_tok is not None:
                        wc.append(last_pe_tok)
                    p.add("v", cpx, inc=True, waits=wc)
                    t_vgate = p.tok("v")
                prev_vgate = t_vgate
                for tc in range(4):
                    for half, PT in ((0, PF0), (1, PF1)):
                        for cc in range(4):
                            c = half * 4 + cc
                            def tr(e, tc=tc, c=c, cc=cc, PT=PT):
                                return e.transpose(
                                    PT[:, cc * 128:cc * 128 + 128],
                                    XOF[:, c, tc * 128:tc * 128 + 128],
                                    c32("ident"))
                            w = []
                            if cc == 0:
                                w.append(t_lx)
                                if t_vgate is not None:
                                    w.append(t_vgate)
                                if pt_evtok[half] is not None:
                                    w.append(pt_evtok[half])
                            p.add("pe", tr, inc=(cc == 3), waits=w)
                        tpe = p.tok("pe")
                        last_pe_tok = tpe
                        def ev(e, tc=tc, half=half, PT=PT):
                            return e.tensor_copy(
                                OT[:, tc, half * 512:half * 512 + 512], PT[:])
                        w = [tpe]
                        if half == 0 and out_dma_tok[tc] is not None:
                            w.append(out_dma_tok[tc])
                        ve = p.add("v", ev, waits=w)
                        pt_evtok[half] = (p.sem["v"], ve)
                    t_evt = p.tok("v")
                    out_dma_tok[tc] = p.dma(
                        "y", f"out{tc}",
                        out_h[col + tc * 128: col + tc * 128 + 128, :],
                        OT[:, tc, :], waits=[t_evt])
            def fin(e):
                return e.nop()
            p.add("y", fin,
                  waits=[(p.dsem[f"out{tc}"], p.dma_cnt[f"out{tc}"])
                         for tc in range(4)])

        with nc.Block() as block:
            p.emit(block)
    return nc


SMALL_SEGS = [("eps", 1), ("e9", 1),
              ("w", 8), ("u", 8), ("tmk", 8), ("tmv", 8), ("tmr", 8),
              ("lng0", 8), ("lnb0", 8), ("lng1", 8), ("lnb1", 8),
              ("b2_0", 8), ("b2_1", 8), ("qs0", 1), ("ks0", 1),
              ("qs1", 1), ("ks1", 1), ("qsw0", 1), ("ksw0", 1),
              ("qsw1", 1), ("ksw1", 1), ("b1i0", NSLOT), ("b1i1", NSLOT)]
SMALL_OFF = {}
_c = 0
for _nm, _w in SMALL_SEGS:
    SMALL_OFF[_nm] = (_c, _w)
    _c += _w
SMALL_COLS = _c


def _feat(v):
    return np.ascontiguousarray(v.reshape(8, 128).T, np.float32)


def _small_params(inp):
    sm = np.zeros((128, SMALL_COLS), np.float32)

    def put(nm, arr):
        o, w = SMALL_OFF[nm]
        sm[:, o:o + w] = arr.reshape(128, w)

    put("eps", np.full((128, 1), 1e-5, np.float32))
    put("e9", np.full((128, 1), 1e-9, np.float32))
    put("w", _feat(inp["L0_attn_time_decay"]))
    put("u", _feat(inp["L0_attn_time_first"]))
    put("tmk", _feat(inp["L0_attn_time_mix_k"].reshape(-1)))
    put("tmv", _feat(inp["L0_attn_time_mix_v"].reshape(-1)))
    put("tmr", _feat(inp["L0_attn_time_mix_r"].reshape(-1)))
    for l in range(2):
        put(f"lng{l}", _feat(inp["L0_ffn_ln_g"][l]))
        put(f"lnb{l}", _feat(inp["L0_ffn_ln_b"][l]))
        put(f"b2_{l}", _feat(inp["L0_ffn_b2"][l]))
        qs = inp["L0_ffn_q_scale"][l]
        ks = inp["L0_ffn_k_scale"][l]
        du = np.arange(128) % 64
        put(f"qs{l}", qs[du].reshape(128, 1))
        put(f"ks{l}", ks[du].reshape(128, 1))
        put(f"qsw{l}", qs[(du + 32) % 64].reshape(128, 1))
        put(f"ksw{l}", ks[(du + 32) % 64].reshape(128, 1))
        b1 = inp["L0_ffn_b1"][l]
        b1i = np.zeros((NSLOT, 128), np.float32)
        for j in range(NJ):
            a = b1[128 * j:128 * (j + 1)]
            g = b1[HID + 128 * j:HID + 128 * (j + 1)]
            b1i[2 * j, :len(a)] = a
            b1i[2 * j + 1, :len(g)] = g
        put(f"b1i{l}", b1i.T)
    return sm


def _prep_weights(inp):
    d = {}
    d["wkT"] = np.ascontiguousarray(inp["L0_attn_W_k"].T, np.float32)
    d["wvT"] = np.ascontiguousarray(inp["L0_attn_W_v"].T, np.float32)
    d["wrT"] = np.ascontiguousarray(inp["L0_attn_W_r"].T, np.float32)
    d["woT"] = np.ascontiguousarray(inp["L0_attn_W_o"].T, np.float32)
    d["wqkv"] = np.ascontiguousarray(inp["L0_ffn_wqkv"], np.float32)
    perm = np.arange(1024)
    for o in range(1024):
        h, du = o // 64, o % 64
        perm[o] = h * 64 + (du + 32) % 64
    d["wqsw"] = np.ascontiguousarray(inp["L0_ffn_wqkv"][:, :, perm],
                                     np.float32)
    d["wof"] = np.ascontiguousarray(inp["L0_ffn_wo"], np.float32)
    w1 = inp["L0_ffn_w1"]
    w1i = np.zeros((2, D, NSLOT, 128), np.float32)
    for j in range(NJ):
        n = min(128, HID - 128 * j)
        w1i[:, :, 2 * j, :n] = w1[:, :, 128 * j:128 * j + n]
        w1i[:, :, 2 * j + 1, :n] = w1[:, :, HID + 128 * j:HID + 128 * j + n]
    d["w1i"] = w1i.reshape(2, D, NSLOT * 128)
    w2 = inp["L0_ffn_w2"]
    w2p = np.zeros((2, NJ * 128, D), np.float32)
    w2p[:, :HID, :] = w2
    d["w2p"] = w2p
    return d


def kernel(**inputs):
    x = np.ascontiguousarray(np.asarray(inputs["L0_attn_x"], np.float32))
    c32_np, segs32, cbf_np, segsbf, rot_np, cbf_res = _host_consts()
    c32_buf, _ = _pack(c32_np, segs32)
    cbf_buf, _ = _pack(cbf_np, segsbf)
    sm = _small_params(inputs)
    wts = _prep_weights(inputs)

    import os
    ncores = int(os.environ.get("K_NCORES", NCORES))
    nc = build(bass.Bass("TRN2", target_bir_lowering=False), BUILD_STAGE)
    in_maps = []
    for b in range(ncores):
        m = {"x": np.ascontiguousarray(x[b]),
             "cst32": c32_buf, "cstbf": cbf_buf, "rot": rot_np,
             "smallw": sm}
        m.update(wts)
        in_maps.append(m)
    res = run_bass_kernel_spmd(nc, in_maps, core_ids=list(range(ncores)))
    if os.environ.get("K_DUMP"):
        kernel.dbg = res.results
    out = np.stack([r["out"] for r in res.results], 0)
    if ncores < NCORES:
        out = np.concatenate([out] + [out[:1]] * (NCORES - ncores), 0)
    return out


# version 22
# speedup vs baseline: 1.0345x; 1.0042x over previous
import math
import numpy as np
from contextlib import ExitStack

import concourse.bass as bass
import concourse.mybir as mybir
from concourse.bass_utils import run_bass_kernel_spmd

F32 = mybir.dt.float32
BF16 = mybir.dt.bfloat16
AF = mybir.ActivationFunctionType
ALU = mybir.AluOpType
AX = mybir.AxisListType

T = 2048
D = 1024
CH = 8
NCORES = 8
HEADS = 8
DH = 64
INNER = 512
HID = 2730
NJ = 22
NSLOT = 44
WIN = 512

TB = 256
NB = T // TB
TS = 128
NS = TB // TS

FT = 512
NF = T // FT

NEG = -1.0e30

BUILD_STAGE = 3
DBG_STOP = None
DBG_NB = None


def _host_consts():
    c32 = {}
    segs32 = []

    def add32(name, arr):
        arr = np.asarray(arr, np.float32)
        assert arr.shape[0] == 128 or arr.ndim == 2
        c32[name] = arr
        segs32.append(name)

    add32("ones", np.ones((128, 128), np.float32))
    add32("ident", np.eye(128, dtype=np.float32))
    add32("iota", np.broadcast_to(np.arange(128, dtype=np.float32)[None, :],
                                  (128, 128)).copy())
    add32("iota1", np.broadcast_to(
        np.arange(1, 129, dtype=np.float32)[None, :], (128, 128)).copy())
    hbc = np.zeros((128, 4, 128), np.float32)
    for c in range(4):
        hbc[2 * c, c, 0:64] = 1.0
        hbc[2 * c + 1, c, 64:128] = 1.0
    add32("hbc", hbc.reshape(128, 512))

    i = np.arange(32, dtype=np.float64)
    inv_freq = 1.0 / (10000.0 ** (2.0 * i / 64.0))
    base = (2.0 * i + 0.4 * 64.0) / (1.4 * 64.0)
    t = np.arange(T, dtype=np.float64)
    ang = t[:, None] * inv_freq[None, :]
    power = (t - T // 2) / float(WIN // 2)
    scl = base[None, :] ** power[:, None]
    cos = np.cos(ang)
    sin = np.sin(ang)
    def mk(tabc, tabs):
        cs = np.concatenate([tabc, tabc], 1)
        sn = np.concatenate([-tabs, tabs], 1)
        return cs, sn
    csq, snq = mk(cos * scl * 8.0, sin * scl * 8.0)
    csk, snk = mk(cos / scl, sin / scl)
    rot = np.stack([csq, snq, csk, snk], 0)
    rot = np.broadcast_to(rot.transpose(0, 2, 1)[:, None, :, :],
                          (4, 2, 64, T)).reshape(4, 128, T)
    rot = np.ascontiguousarray(rot, np.float32)

    cbf = {}
    segsbf = []

    def addbf(name, arr):
        cbf[name] = np.asarray(arr, np.float32)
        segsbf.append(name)

    addbf("ident", np.eye(128, dtype=np.float32))
    addbf("ones1", np.ones((128, 1), np.float32))
    hsum = np.zeros((128, 4, 8), np.float32)
    for c in range(4):
        hsum[0:64, c, 2 * c] = 1.0
        hsum[64:128, c, 2 * c + 1] = 1.0
    addbf("hsum", hsum.reshape(128, 32))
    addbf("hbc", hbc.reshape(128, 512))
    addbf("onesr", np.ones((128, 128), np.float32))
    a = np.arange(128)[:, None]
    j = np.arange(512)[None, :]
    maskA = np.where(j >= a, 0.0, NEG)
    addbf("maskA", maskA)
    jb = np.arange(128)[None, :]
    maskB = np.where(jb <= a, 0.0, NEG)
    addbf("maskB", maskB)
    mA0 = np.empty((128, 4, 512), np.float64)
    for qc in range(4):
        lim = 512 - 128 * qc
        mA0[:, qc, :] = np.where((j >= a) & (j >= lim), 0.0, NEG)
    addbf("maskA0", mA0.reshape(128, 2048))
    resident = ["ident", "ones1", "hsum", "hbc", "onesr"]

    return c32, segs32, cbf, segsbf, rot, resident


def _pack(cdict, order):
    offs = {}
    cols = 0
    for nm in order:
        a = cdict[nm].reshape(128, -1)
        offs[nm] = (cols, a.shape[1])
        cols += a.shape[1]
    buf = np.zeros((128, cols), np.float32)
    for nm in order:
        a = cdict[nm].reshape(128, -1)
        o, w = offs[nm]
        buf[:, o:o + w] = a
    return buf, offs


class PB:
    ENGS = ("pe", "v", "s", "g", "y")

    def __init__(self, nc):
        self.nc = nc
        self.items = {e: [] for e in self.ENGS}
        self.cnt = {e: 0 for e in self.ENGS}
        self.dma_cnt = {}
        self.sem = {}
        self.dsem = {}
        self._es = None

    def init_sems(self, es):
        self._es = es
        for e in self.ENGS:
            self.sem[e] = es.enter_context(self.nc.semaphore(f"sem_{e}"))

    CHAINED = ("v", "s", "g")

    @staticmethod
    def _ckw(waits):
        for w in waits:
            assert (isinstance(w, tuple) and len(w) == 2
                    and isinstance(w[1], int)), f"bad wait token: {w!r}"

    def add(self, eng, fn, inc=False, waits=()):
        self._ckw(waits)
        if eng in self.CHAINED:
            prev = self.cnt[eng]
            w = list(waits) + [(self.sem[eng], prev)]
            self.items[eng].append((fn, True, w))
            self.cnt[eng] += 1
        else:
            self.items[eng].append((fn, bool(inc), list(waits)))
            if inc:
                self.cnt[eng] += 1
        return self.cnt[eng]

    def dma(self, q, name, out, in_, waits=()):
        if name not in self.dsem:
            self.dsem[name] = self._es.enter_context(
                self.nc.semaphore(f"dma_{name}"))
            self.dma_cnt[name] = 0

        self._ckw(waits)

        def fn(e, out=out, in_=in_):
            return e.dma_start(out=out, in_=in_)
        self.items[q].append((fn, ("dma", name), list(waits)))
        self.dma_cnt[name] += 16
        return (self.dsem[name], self.dma_cnt[name])

    def tok(self, eng):
        return (self.sem[eng], self.cnt[eng])

    def all_dma_toks(self):
        return [(self.dsem[n], self.dma_cnt[n]) for n in self.dsem]

    def emit(self, block):
        def mk_body(eng_name):
            def body(e):
                for fn, inc, waits in self.items[eng_name]:
                    for sem, val in waits:
                        e.wait_ge(sem, val)
                    instr = fn(e)
                    if inc is True:
                        instr.then_inc(self.sem[eng_name], 1)
                    elif isinstance(inc, tuple) and inc[0] == "dma":
                        instr.then_inc(self.dsem[inc[1]], 16)
            return body
        block.tensor(mk_body("pe"))
        block.vector(mk_body("v"))
        block.scalar(mk_body("s"))
        block.gpsimd(mk_body("g"))
        block.sync(mk_body("y"))


def build(nc, stage):
    dt = lambda nm, shp, k=None: (nc.dram_tensor(nm, shp, F32, kind=k)
                                  if k else nc.dram_tensor(nm, shp, F32)).ap()
    x_h = dt("x", [T, D], "ExternalInput")
    out_h = dt("out", [T, D], "ExternalOutput")
    c32_np, segs32, cbf_np, segsbf, rot_np, cbf_res = _host_consts()
    c32_buf, c32_off = _pack(c32_np, segs32)
    cbf_buf, cbf_off = _pack(cbf_np, segsbf)
    smallw = dt("smallw", [128, SMALL_COLS], "ExternalInput")
    cst32_h = dt("cst32", list(c32_buf.shape), "ExternalInput")
    cstbf_h = dt("cstbf", list(cbf_buf.shape), "ExternalInput")
    rot_h = dt("rot", [4, 128, T], "ExternalInput")
    wkT_h = dt("wkT", [D, D], "ExternalInput")
    wvT_h = dt("wvT", [D, D], "ExternalInput")
    wrT_h = dt("wrT", [D, D], "ExternalInput")
    woT_h = dt("woT", [D, D], "ExternalInput")
    wqkv_h = dt("wqkv", [2, D, 1536], "ExternalInput")
    wqsw_h = dt("wqsw", [2, D, 1024], "ExternalInput")
    wof_h = dt("wof", [2, INNER, D], "ExternalInput")
    w1i_h = dt("w1i", [2, D, NSLOT * 128], "ExternalInput")
    w2p_h = dt("w2p", [2, NJ * 128, D], "ExternalInput")
    xsp_h = nc.dram_tensor("xsp", [128, CH, T], BF16).ap()
    ysp_h = nc.dram_tensor("ysp", [128, CH, T], BF16).ap()
    import os
    dbg_dump = bool(os.environ.get("K_DUMP"))
    if dbg_dump:
        dby_h = {}
        for tag in ("0", "a0", "b0", "a1", "b1"):
            dby_h[tag] = dt(f"dby{tag}", [128, CH, T], "ExternalOutput")
        dbhb_h = dt("dbhb", [128, CH, FT], "ExternalOutput")
        dbha_h = dt("dbha", [128, NJ, FT], "ExternalOutput")
        dbw1_h = dt("dbw1", [128, CH, NSLOT * 128], "ExternalOutput")

    p = PB(nc)
    es = ExitStack()
    with es:
        p.init_sems(es)
        sb = lambda nm, shp, dty: es.enter_context(nc.sbuf_tensor(nm, shp, dty))
        front_es = ExitStack()
        fsb = lambda nm, shp, dty: front_es.enter_context(
            nc.sbuf_tensor(nm, shp, dty))
        C32 = sb("C32", list(c32_buf.shape), F32)
        ncres = sum(cbf_off[nm][1] for nm in cbf_res)
        CBF = sb("CBF", [128, ncres], BF16)
        SMW = sb("SMW", [128, SMALL_COLS], F32)
        RRF = sb("RRF", [128, 2048], F32)
        XF = fsb("XF", [128, CH, T], BF16)

        def c32(nm, r0=0, r1=128):
            o, w = c32_off[nm]
            return C32[r0:r1, o:o + w]

        def cbf(nm, r0=0, r1=128):
            o, w = cbf_off[nm]
            assert nm in cbf_res and o + w <= ncres
            return CBF[r0:r1, o:o + w]

        def cbf_h(nm):
            o, w = cbf_off[nm]
            return cstbf_h[:, o:o + w]

        def smw(nm):
            o, w = SMALL_OFF[nm]
            return SMW[:, o:o + w]

        t_c32 = p.dma("y", "cst", C32[:], cst32_h)
        t_smw = p.dma("y", "smw", SMW[:], smallw)
        t_cbf = p.dma("g", "cbf", CBF[:], cstbf_h[:, 0:ncres])

        s0_ev = [None] * 16
        with nc.sbuf_tensor("XTOK", [128, 16, D], BF16) as XTOK, \
             nc.psum_tensor("PT0", [128, 512], BF16) as PT0, \
             nc.psum_tensor("PT1", [128, 512], BF16) as PT1:
            t_x = p.dma("g", "x", XTOK[:],
                        x_h.rearrange("(tc p) c -> p tc c", p=128))
            wx = [t_x, t_cbf]
            pev = 0
            for tc in range(16):
                for half, PT in ((0, PT0), (1, PT1)):
                    for cc in range(4):
                        c = half * 4 + cc
                        def tr(e, tc=tc, c=c, cc=cc, PT=PT):
                            return e.transpose(
                                PT[:, cc * 128:(cc + 1) * 128],
                                XTOK[:, tc, c * 128:(c + 1) * 128],
                                cbf("ident"))
                        w = list(wx) if (tc == 0 and half == 0 and cc == 0) else []
                        if tc > 0 and cc == 0:
                            w.append((p.sem["v"], pev - 1))
                        p.add("pe", tr, inc=(cc == 3), waits=w)
                    tpe = p.tok("pe")
                    def ev(e, tc=tc, half=half, PT=PT):
                        return e.tensor_copy(
                            XF[:, half * 4:half * 4 + 4, tc * 128:tc * 128 + 128],
                            PT[:].rearrange("p (c n) -> p c n", c=4))
                    pev = p.add("v", ev, inc=True, waits=[tpe])
                s0_ev[tc] = p.tok("v")

        t_s0pe = p.tok("pe")
        WK = fsb("WK", [128, CH, D], BF16)
        WV = fsb("WV", [128, CH, D], BF16)
        WR = fsb("WR", [128, CH, D], BF16)
        WO = fsb("WO", [128, CH, D], BF16)
        p.dma("g", "wr4", WK[:], wkT_h.rearrange("(c p) o -> p c o", p=128),
              waits=[t_s0pe])
        p.dma("g", "wr4", WV[:], wvT_h.rearrange("(c p) o -> p c o", p=128))
        p.dma("g", "wr4", WR[:], wrT_h.rearrange("(c p) o -> p c o", p=128))
        t_wr4 = p.dma("g", "wr4", WO[:],
                      woT_h.rearrange("(c p) o -> p c o", p=128))

        rwkv_scope = ExitStack()
        with rwkv_scope as rs:
            rsb = lambda nm, shp, dty: rs.enter_context(
                nc.sbuf_tensor(nm, shp, dty))
            H0 = rsb("H0", [128, CH, TB + 1], BF16)
            XKk = rsb("XKk", [128, CH, TB], BF16)
            XKv = rsb("XKv", [128, CH, TB], BF16)
            WKVR = rsb("WKVR", [128, 2, CH, TB], BF16)
            RG2 = rsb("RG2", [128, 2, CH, TB], BF16)
            KF = rsb("KF", [128, CH, TB], F32)
            XC = rsb("XC", [128, CH, TB], F32)
            VFB = rsb("VFB", [128, CH, TB], BF16)
            M2 = rsb("M2", [128, 2, CH, TS + 1], F32)
            A2 = rsb("A2", [128, 2, CH, TS + 1], BF16)
            B2 = rsb("B2", [128, 2, CH, TS + 1], BF16)
            S1B = rsb("S1B", [128, CH, TS], F32)
            SAB = rsb("SAB", [128, 2, CH, TS], F32)
            E12 = rsb("E12", [128, 2, CH, TS], BF16)
            N1B = rsb("N1B", [128, CH, TS], BF16)
            DEN = rsb("DEN", [128, CH, TS], BF16)
            RCB = rsb("RCB", [128, CH, TS], BF16)
            TWU = rsb("TWU", [128, CH, TS], F32)
            TWZ = rsb("TWZ", [128, CH, TS], F32)
            SMX = rsb("SMX", [128, CH], F32)
            RSBR = rsb("RSBR", [1, 2 * TB], BF16)
            PBS = [rs.enter_context(nc.psum_tensor(f"pb{i}", [128, 512], F32))
                   for i in range(8)]

            murow = RRF[0:1, 0:TB]
            rstdrow = RRF[0:1, 256:256 + TB]
            scrap = RRF[0:1, 512:512 + TB]
            sdrow = RRF[0:1, 768:768 + TB]

            wsm = [t_smw, t_c32]
            def f0(e):
                return e.tensor_tensor(SMX[:], smw("w"), smw("u"), ALU.add)
            p.add("v", f0, waits=wsm)
            for c in range(CH):
                def f1(e, c=c):
                    return e.tensor_scalar(TWU[:, c, :], c32("iota"),
                                           smw("w")[:, c:c + 1], None, ALU.mult)
                p.add("v", f1)
            for c in range(CH):
                def f2(e, c=c):
                    return e.tensor_scalar(TWU[:, c, :], TWU[:, c, :],
                                           smw("u")[:, c:c + 1], None,
                                           ALU.subtract)
                p.add("v", f2)
            for c in range(CH):
                def f3z(e, c=c):
                    return e.tensor_scalar(TWZ[:, c, :], TWU[:, c, :],
                                           SMX[:, c:c + 1], None, ALU.add)
                p.add("v", f3z)

            def col2d4(buf, par, i):
                return buf[:, par, :, i:i + 1].rearrange("p c n -> p (c n)")

            def fill(dst, val):
                def f(e, dst=dst, val=val):
                    return e.tensor_scalar(dst, smw("w"), 0.0, float(val),
                                           ALU.mult, ALU.add)
                return f
            p.add("v", fill(col2d4(M2, 0, 0), -1.0e38))
            p.add("v", fill(col2d4(A2, 0, 0), 0.0))
            p.add("v", fill(col2d4(B2, 0, 0), 0.0))
            p.add("v", fill(H0[:, :, 0:1].rearrange("p c n -> p (c n)"), 0.0),
                  inc=True)

            ST = {"pend": None, "sg": 0, "t_exps": None, "t_vnum": None,
                  "t_gb": None, "t_vnum2": None,
                  "t_vwk": None, "t_vwk_s1": None, "t_wkvr_done": None,
                  "t_ln2v": p.tok("v"), "t_ln2pe": None,
                  "t_kw": None, "t_vw": None, "t_rw": None,
                  "t_kev": None, "t_vev": None, "t_revs": None,
                  "t_aev": None, "t_aevs": None}

            def emit_wkv(pd):
                def wkv(e):
                    return e.tensor_tensor(N1B[:], N1B[:], RCB[:], ALU.mult)
                w = [pd["t_sdn"]]
                if ST["t_ln2pe"] is not None:
                    w.append(ST["t_ln2pe"])
                p.add("v", wkv, waits=w)
                bp = pd["b"] & 1
                def wkr(e, bp=bp, wsl=pd["wsl"]):
                    return e.tensor_tensor(WKVR[:, bp, :, wsl], N1B[:],
                                           RG2[:, bp, :, wsl], ALU.mult)
                p.add("v", wkr, inc=True)
                ST["t_vwk"] = p.tok("v")
                if pd["sblk"] == 1:
                    ST["t_vwk_s1"] = ST["t_vwk"]
                    ST["t_wkvr_done"] = ST["t_vwk"]

            def emit_slot(b, sblk):
                sg = ST["sg"]
                par = sg & 1
                op = 1 - par
                wsl = slice(sblk * TS, sblk * TS + TS)
                pend = ST["pend"]
                if sg > 0:
                    def mcar(e, par=par, op=op):
                        return e.scalar_tensor_tensor(
                            col2d4(M2, par, 0), smw("w"), float(TS),
                            col2d4(M2, op, TS), ALU.mult, ALU.add)
                    p.add("v", mcar)
                    def acar(e, par=par, op=op):
                        return e.tensor_copy(col2d4(A2, par, 0),
                                             col2d4(A2, op, TS))
                    p.add("v", acar)
                    def bcar(e, par=par, op=op):
                        return e.tensor_copy(col2d4(B2, par, 0),
                                             col2d4(B2, op, TS))
                    p.add("v", bcar)
                wz = [pend["t_sdn"]] if pend is not None else []
                if sblk == 0:
                    wz.append(ST["t_kev"])
                def zf(e, wsl=wsl):
                    return e.tensor_tensor(S1B[:], KF[:, :, wsl], TWZ[:],
                                           ALU.subtract)
                p.add("v", zf, waits=wz)
                for c in range(CH):
                    def cm(e, c=c, par=par):
                        return e.tensor_tensor_scan(
                            M2[:, par, c, 1:TS + 1], S1B[:, c, :],
                            S1B[:, c, :], M2[:, par, c, 0:1],
                            ALU.max, ALU.max)
                    p.add("v", cm)
                wa = []
                if ST["t_exps"] is not None:
                    wa.append(ST["t_exps"])
                if ST["t_gb"] is not None:
                    wa.append(ST["t_gb"])
                def asub(e, par=par):
                    return e.tensor_tensor(SAB[:, 0], M2[:, par, :, 0:TS],
                                           M2[:, par, :, 1:TS + 1],
                                           ALU.subtract)
                p.add("v", asub, waits=wa)
                def bsub(e, par=par):
                    return e.tensor_tensor(SAB[:, 1], S1B[:],
                                           M2[:, par, :, 1:TS + 1],
                                           ALU.subtract)
                p.add("v", bsub, inc=True)
                t_subs = p.tok("v")
                def dd1(e, wsl=wsl, par=par):
                    return e.tensor_tensor(S1B[:], KF[:, :, wsl],
                                           M2[:, par, :, 0:TS], ALU.subtract)
                p.add("v", dd1)
                def dd2(e):
                    return e.tensor_tensor(S1B[:], S1B[:], TWU[:],
                                           ALU.subtract)
                p.add("v", dd2, inc=True)
                t_vD = p.tok("v")
                if pend is not None:
                    emit_wkv(pend)
                def exps(e):
                    return e.activation(SAB[:], SAB[:], AF.Exp)
                p.add("s", exps, inc=True, waits=[t_subs])
                ST["t_exps"] = p.tok("s")
                wr1 = [t_vD]
                if ST["t_vnum"] is not None:
                    wr1.append(ST["t_vnum"])
                def rl1(e):
                    return e.activation(E12[:, 0], S1B[:], AF.Relu)
                p.add("s", rl1, waits=wr1)
                def rl2(e):
                    return e.activation(E12[:, 1], S1B[:], AF.Relu, scale=-1.0)
                p.add("s", rl2)
                def eexp(e):
                    return e.activation(E12[:], E12[:], AF.Exp, scale=-1.0)
                p.add("s", eexp, inc=True)
                t_sE = p.tok("s")
                for c in range(CH):
                    def sb_(e, c=c, par=par):
                        return e.tensor_tensor_scan(
                            B2[:, par, c, 1:TS + 1], SAB[:, 0, c, :],
                            SAB[:, 1, c, :], B2[:, par, c, 0:1],
                            ALU.mult, ALU.add)
                    p.add("v", sb_,
                          waits=([ST["t_exps"]] if c == 0 else ()))
                def bv(e, wsl=wsl):
                    return e.tensor_tensor(SAB[:, 1], SAB[:, 1],
                                           VFB[:, :, wsl], ALU.mult)
                p.add("v", bv)
                for c in range(CH):
                    def sa_(e, c=c, par=par):
                        return e.tensor_tensor_scan(
                            A2[:, par, c, 1:TS + 1], SAB[:, 0, c, :],
                            SAB[:, 1, c, :], A2[:, par, c, 0:1],
                            ALU.mult, ALU.add)
                    p.add("v", sa_)
                def d1(e, par=par):
                    return e.tensor_tensor(DEN[:], E12[:, 0],
                                           B2[:, par, :, 0:TS], ALU.mult)
                p.add("v", d1, waits=[t_sE])
                def d2(e):
                    return e.tensor_tensor(DEN[:], DEN[:], E12[:, 1], ALU.add)
                p.add("v", d2, inc=True)
                t_vden = p.tok("v")
                def t2(e, par=par):
                    return e.tensor_tensor(N1B[:], E12[:, 0],
                                           A2[:, par, :, 0:TS], ALU.mult)
                p.add("v", t2)
                def t3(e, wsl=wsl):
                    return e.tensor_tensor(E12[:, 1], E12[:, 1],
                                           VFB[:, :, wsl], ALU.mult)
                p.add("v", t3)
                def t4(e):
                    return e.tensor_tensor(N1B[:], N1B[:], E12[:, 1], ALU.add)
                p.add("v", t4, inc=True)
                ST["t_vnum"] = p.tok("v")
                ST["t_vnum2"] = ST["t_vnum"]
                wln = [t_vden]
                if ST["t_vwk"] is not None:
                    wln.append(ST["t_vwk"])
                def dln(e):
                    return e.activation(S1B[:], DEN[:], AF.Ln,
                                        bias=smw("e9")[:, 0:1])
                p.add("s", dln, waits=wln)
                def dex(e):
                    return e.activation(RCB[:], S1B[:], AF.Exp, scale=-1.0)
                p.add("s", dex, inc=True)
                t_sdn = p.tok("s")
                ST["pend"] = {"b": b, "sblk": sblk, "wsl": wsl, "par": par,
                              "t_sE": t_sE, "t_sdn": t_sdn}
                ST["sg"] = sg + 1

            def emit_ln1_mixk_kwave(b):
                col = b * TB
                xfb = XF[:, :, col:col + TB]
                if b > 0:
                    def hcar(e):
                        return e.tensor_copy(H0[:, :, 0:1], H0[:, :, TB:TB + 1])
                    p.add("v", hcar)
                ws = [s0_ev[2 * b + 1], ST["t_ln2v"]]
                def sqr(e, xfb=xfb):
                    return e.activation(XKk[:], xfb, AF.Square)
                p.add("s", sqr, inc=True, waits=ws)
                t_sqr = p.tok("s")
                for c in range(CH):
                    def m0(e, c=c, xfb=xfb):
                        return e.matmul(PBS[6][0:1, 0:TB], cbf("ones1"),
                                        xfb[:, c, :], start=(c == 0),
                                        stop=(c == CH - 1))
                    w = []
                    if c == 0:
                        w = [s0_ev[2 * b + 1], ST["t_ln2v"], t_cbf]
                        if ST["t_revs"] is not None:
                            w.append(ST["t_revs"])
                        if ST["t_aevs"] is not None:
                            w.append(ST["t_aevs"])
                    p.add("pe", m0, waits=w)
                for c in range(CH):
                    def m1(e, c=c):
                        return e.matmul(PBS[7][0:1, 0:TB], cbf("ones1"),
                                        XKk[:, c, :], start=(c == 0),
                                        stop=(c == CH - 1))
                    p.add("pe", m1, inc=(c == CH - 1),
                          waits=([t_sqr] if c == 0 else ()))
                t_stats = p.tok("pe")
                def r1(e):
                    return e.tensor_scalar(murow, PBS[6][0:1, 0:TB],
                                           1.0 / D, None, ALU.mult)
                p.add("v", r1, waits=[t_stats])
                def r2(e):
                    return e.tensor_tensor(scrap, murow, murow, ALU.mult)
                p.add("v", r2)
                def r3(e):
                    return e.scalar_tensor_tensor(sdrow, PBS[7][0:1, 0:TB],
                                                  1.0 / D, scrap, ALU.mult,
                                                  ALU.subtract)
                p.add("v", r3, inc=True)
                t_var = p.tok("v")
                def r4(e):
                    return e.activation(sdrow, sdrow, AF.Ln,
                                        bias=smw("eps")[0:1, :])
                p.add("s", r4, waits=[t_var])
                def r6(e):
                    return e.activation(RSBR[0:1, 0:TB], sdrow, AF.Exp,
                                        scale=-0.5)
                p.add("s", r6, inc=True)
                t_rows = p.tok("s")
                def b1(e):
                    return e.matmul(PBS[6][:, 0:TB], c32("ones", 0, 1), murow,
                                    start=True, stop=True)
                p.add("pe", b1, waits=[t_rows])
                def b2(e):
                    return e.matmul(PBS[7][:, 0:TB], cbf("onesr", 0, 1),
                                    RSBR[0:1, 0:TB], start=True, stop=True)
                p.add("pe", b2, inc=True)
                t_bc = p.tok("pe")
                for c in range(CH):
                    def a1(e, c=c, xfb=xfb):
                        return e.tensor_tensor(XC[:, c, :], xfb[:, c, :],
                                               PBS[6][:, 0:TB], ALU.subtract)
                    p.add("v", a1, waits=([t_bc] if c == 0 else ()))
                for c in range(CH):
                    def a2(e, c=c):
                        return e.tensor_tensor(H0[:, c, 1:TB + 1], XC[:, c, :],
                                               PBS[7][:, 0:TB], ALU.mult)
                    p.add("v", a2)
                def dxf(e):
                    return e.tensor_tensor(XC[:], H0[:, :, 1:TB + 1],
                                           H0[:, :, 0:TB], ALU.subtract)
                p.add("v", dxf)
                def pslot(bank0, oc):
                    return PBS[bank0 + oc // 2][:, (oc % 2) * TB:
                                                (oc % 2) * TB + TB]

                def mixwave(W, mixnm, dst, bank0, t_gate, mx_wait=()):
                    for c in range(CH):
                        def mx(e, c=c, dst=dst, mixnm=mixnm):
                            return e.scalar_tensor_tensor(
                                dst[:, c, :], XC[:, c, :],
                                smw(mixnm)[:, c:c + 1], H0[:, c, 0:TB],
                                ALU.mult, ALU.add)
                        p.add("v", mx, inc=(c == CH - 1),
                              waits=(mx_wait if c == 0 else ()))
                    t_mix = p.tok("v")
                    for oc in range(CH):
                        for kc in range(CH):
                            def mm(e, oc=oc, kc=kc, W=W, dst=dst, bank0=bank0):
                                return e.matmul(
                                    pslot(bank0, oc),
                                    W[:, kc, oc * 128:oc * 128 + 128],
                                    dst[:, kc, :], start=(kc == 0),
                                    stop=(kc == CH - 1))
                            w = []
                            if oc == 0 and kc == 0:
                                w = [t_mix, t_wr4]
                                if t_gate is not None:
                                    w.append(t_gate)
                            p.add("pe", mm,
                                  inc=(oc == CH - 1 and kc == CH - 1),
                                  waits=w)
                    return p.tok("pe")
                ST["mixwave"] = mixwave
                ST["pslot"] = pslot
                ST["t_kw"] = mixwave(WK, "tmk", XKk, 0, None)
                ST["t_vw"] = mixwave(WV, "tmv", XKv, 4, None)

            def emit_evkv(b):
                pslot = ST["pslot"]
                for oc in range(CH):
                    def evv(e, oc=oc):
                        return e.activation(VFB[:, oc, :], pslot(4, oc),
                                            AF.Copy, bias=0.0)
                    w = []
                    if oc == 0:
                        w = [ST["t_vw"]]
                        if ST["t_vnum"] is not None:
                            w.append(ST["t_vnum"])
                    p.add("s", evv, inc=(oc == CH - 1), waits=w)
                ST["t_vev"] = p.tok("s")
                for oc in range(CH):
                    def evk(e, oc=oc):
                        return e.activation(KF[:, oc, :], pslot(0, oc),
                                            AF.Copy, bias=0.0)
                    p.add("s", evk, inc=(oc == CH - 1),
                          waits=([ST["t_kw"]] if oc == 0 else ()))
                ST["t_kev"] = p.tok("s")

            def emit_rwave(b):
                pslot = ST["pslot"]
                ST["t_rw"] = ST["mixwave"](WR, "tmr", XKk, 0, ST["t_kev"],
                                           mx_wait=[ST["t_kw"]])
                bp = b & 1
                for oc in range(CH):
                    def evr(e, oc=oc, bp=bp):
                        return e.activation(RG2[:, bp, oc, :], pslot(0, oc),
                                            AF.Tanh, scale=0.5)
                    w = []
                    if oc == 0:
                        w = [ST["t_rw"]]
                        if ST["t_vwk_s1"] is not None:
                            w.append(ST["t_vwk_s1"])
                    p.add("s", evr, inc=(oc == CH - 1), waits=w)
                ST["t_revs"] = p.tok("s")
                def rresc(e, bp=bp):
                    return e.tensor_scalar(RG2[:, bp], RG2[:, bp], 0.5, 0.5,
                                           ALU.mult, ALU.add)
                p.add("v", rresc, inc=True, waits=[ST["t_revs"]])

            def emit_wo_ln2(bb):
                bp = bb & 1
                col = bb * TB
                xfb = XF[:, :, col:col + TB]
                pslot = ST["pslot"]
                for oc in range(CH):
                    for kc in range(CH):
                        def mo(e, oc=oc, kc=kc, bp=bp):
                            return e.matmul(
                                pslot(4, oc),
                                WO[:, kc, oc * 128:oc * 128 + 128],
                                WKVR[:, bp, kc, :], start=(kc == 0),
                                stop=(kc == CH - 1))
                        w = []
                        if oc == 0 and kc == 0:
                            w = [ST["t_wkvr_done"], ST["t_vev"]]
                            if ST["t_aev"] is not None:
                                w += [ST["t_aev"], ST["t_aevs"]]
                        p.add("pe", mo, inc=(oc == CH - 1 and kc == CH - 1),
                              waits=w)
                t_ow = p.tok("pe")
                for bk in range(4):
                    tv_bank = None
                    for half in range(2):
                        oc = bk * 2 + half
                        def ea1(e, oc=oc):
                            return e.tensor_copy(XC[:, oc, :], pslot(4, oc))
                        p.add("v", ea1,
                              waits=([t_ow] if oc == 0 else ()))
                        def ea2(e, oc=oc, bp=bp):
                            return e.tensor_copy(WKVR[:, bp, oc, :],
                                                 pslot(4, oc))
                        tv_bank = p.add("v", ea2, inc=(oc == CH - 1))
                    for half in range(2):
                        oc = bk * 2 + half
                        def ea3(e, oc=oc):
                            return e.activation(XKk[:, oc, :], pslot(4, oc),
                                                AF.Square)
                        p.add("s", ea3, inc=(oc == CH - 1),
                              waits=([(p.sem["v"], tv_bank)]
                                     if half == 0 else ()))
                ST["t_aev"] = p.tok("v")
                ST["t_aevs"] = p.tok("s")
                for c in range(CH):
                    def m0(e, c=c, bp=bp):
                        return e.matmul(PBS[6][0:1, 0:TB], cbf("ones1"),
                                        WKVR[:, bp, c, :], start=(c == 0),
                                        stop=(c == CH - 1))
                    p.add("pe", m0,
                          waits=([ST["t_aev"], ST["t_aevs"]] if c == 0 else ()))
                for c in range(CH):
                    def m1(e, c=c):
                        return e.matmul(PBS[7][0:1, 0:TB], cbf("ones1"),
                                        XKk[:, c, :], start=(c == 0),
                                        stop=(c == CH - 1))
                    p.add("pe", m1, inc=(c == CH - 1))
                t_st2 = p.tok("pe")
                def q1(e):
                    return e.tensor_scalar(murow, PBS[6][0:1, 0:TB], 1.0 / D,
                                           None, ALU.mult)
                p.add("v", q1, waits=[t_st2])
                def q2(e):
                    return e.tensor_tensor(scrap, murow, murow, ALU.mult)
                p.add("v", q2)
                def q3(e):
                    return e.scalar_tensor_tensor(sdrow, PBS[7][0:1, 0:TB],
                                                  1.0 / D, scrap, ALU.mult,
                                                  ALU.subtract)
                p.add("v", q3, inc=True)
                t_v2 = p.tok("v")
                def q4(e):
                    return e.activation(sdrow, sdrow, AF.Ln,
                                        bias=smw("eps")[0:1, :])
                p.add("s", q4, waits=[t_v2])
                def q6(e):
                    return e.activation(RSBR[0:1, TB:2 * TB], sdrow, AF.Exp,
                                        scale=-0.5)
                p.add("s", q6, inc=True)
                t_rws = p.tok("s")
                def bb1(e):
                    return e.matmul(PBS[6][:, 0:TB], c32("ones", 0, 1), murow,
                                    start=True, stop=True)
                p.add("pe", bb1, waits=[t_rws])
                def bb2(e):
                    return e.matmul(PBS[7][:, 0:TB], cbf("onesr", 0, 1),
                                    RSBR[0:1, TB:2 * TB], start=True,
                                    stop=True)
                p.add("pe", bb2, inc=True)
                ST["t_ln2pe"] = p.tok("pe")
                for c in range(CH):
                    def g1(e, c=c):
                        return e.tensor_tensor(XC[:, c, :], XC[:, c, :],
                                               PBS[6][:, 0:TB], ALU.subtract)
                    p.add("v", g1,
                          waits=([ST["t_ln2pe"]] if c == 0 else ()))
                for c in range(CH):
                    def g2(e, c=c):
                        return e.tensor_tensor(XKk[:, c, :], XC[:, c, :],
                                               PBS[7][:, 0:TB], ALU.mult)
                    p.add("v", g2)
                for c in range(CH):
                    def g3(e, c=c, xfb=xfb):
                        return e.tensor_tensor(xfb[:, c, :], xfb[:, c, :],
                                               XKk[:, c, :], ALU.add)
                    p.add("v", g3, inc=(c == CH - 1))
                ST["t_ln2v"] = p.tok("v")

            nb_run = DBG_NB if DBG_NB is not None else NB
            for it in range(nb_run + 2):
                if it < nb_run:
                    emit_ln1_mixk_kwave(it)
                if 1 <= it <= nb_run:
                    emit_slot(it - 1, 0)
                    emit_slot(it - 1, 1)
                if it == nb_run + 1 and ST["pend"] is not None:
                    emit_wkv(ST["pend"])
                    ST["pend"] = None
                if it < nb_run:
                    emit_evkv(it)
                    emit_rwave(it)
                if it >= 2 and it - 2 < nb_run:
                    emit_wo_ln2(it - 2)
            t_rwkv_done = ST["t_ln2v"]

            t_xsp = p.dma("y", "xsp", xsp_h, XF[:], waits=[t_rwkv_done])
            t_y0 = None
            v_apply = ST["t_ln2v"]
            t_st3 = None
            t_ap3 = None
            if stage >= 2:
                for b in range(NB):
                    col = b * TB
                    xfb = XF[:, :, col:col + TB]
                    ws3 = [v_apply] if b == 0 else [t_st3]
                    def sqr3(e, xfb=xfb):
                        return e.activation(XKk[:], xfb, AF.Square)
                    p.add("s", sqr3, inc=True, waits=ws3)
                    t_x2 = p.tok("s")
                    for c in range(CH):
                        def m0(e, c=c, xfb=xfb):
                            return e.matmul(PBS[6][0:1, 0:TB], cbf("ones1"),
                                            xfb[:, c, :], start=(c == 0),
                                            stop=(c == CH - 1))
                        w = []
                        if b == 0 and c == 0:
                            w = [v_apply, ST["t_aevs"]]
                        elif c == 0:
                            w = [t_ap3]
                        p.add("pe", m0, waits=w)
                    for c in range(CH):
                        def m1(e, c=c):
                            return e.matmul(PBS[7][0:1, 0:TB], cbf("ones1"),
                                            XKk[:, c, :], start=(c == 0),
                                            stop=(c == CH - 1))
                        p.add("pe", m1, inc=(c == CH - 1),
                              waits=([t_x2] if c == 0 else ()))
                    t_st3 = p.tok("pe")
                    def r1(e):
                        return e.tensor_scalar(murow, PBS[6][0:1, 0:TB],
                                               1.0 / D, None, ALU.mult)
                    p.add("v", r1, waits=[t_st3])
                    def r2(e):
                        return e.tensor_tensor(scrap, murow, murow, ALU.mult)
                    p.add("v", r2)
                    def r3(e):
                        return e.scalar_tensor_tensor(sdrow, PBS[7][0:1, 0:TB],
                                                      1.0 / D, scrap, ALU.mult,
                                                      ALU.subtract)
                    p.add("v", r3, inc=True)
                    t_var = p.tok("v")
                    def r4(e):
                        return e.activation(sdrow, sdrow, AF.Ln,
                                            bias=smw("eps")[0:1, :])
                    p.add("s", r4, waits=[t_var])
                    def r6(e):
                        return e.activation(rstdrow, sdrow, AF.Exp, scale=-0.5)
                    p.add("s", r6, inc=True)
                    t_rows3 = p.tok("s")
                    def b1(e):
                        return e.matmul(PBS[6][:, 0:TB], c32("ones", 0, 1),
                                        murow, start=True, stop=True)
                    p.add("pe", b1, waits=[t_rows3])
                    def b2(e):
                        return e.matmul(PBS[7][:, 0:TB], c32("ones", 0, 1),
                                        rstdrow, start=True, stop=True)
                    p.add("pe", b2, inc=True)
                    t_bc3 = p.tok("pe")
                    for c in range(CH):
                        def a1(e, c=c, xfb=xfb):
                            return e.tensor_tensor(XC[:, c, :], xfb[:, c, :],
                                                   PBS[6][:, 0:TB],
                                                   ALU.subtract)
                        p.add("v", a1, waits=([t_bc3] if c == 0 else ()))
                    for c in range(CH):
                        def a2(e, c=c):
                            return e.tensor_tensor(XKv[:, c, :], XC[:, c, :],
                                                   PBS[7][:, 0:TB], ALU.mult)
                        w = []
                        if c == 0 and t_y0 is not None:
                            w.append(t_y0)
                        p.add("v", a2, waits=w)
                    t_ap3 = p.tok("v")
                    t_y0 = p.dma("y", "ysp0",
                                 ysp_h[:, :, col:col + TB], XKv[:],
                                 waits=[t_ap3])
            t_front = [t_xsp, p.tok("v"), p.tok("s"), p.tok("pe")]
            if t_y0 is not None:
                t_front.append(t_y0)
            if dbg_dump and t_y0 is not None:
                t_front.append(p.dma("y", "dby", dby_h["0"], ysp_h,
                                     waits=[t_y0]))

        front_es.close()


        nlayers = 0 if stage < 2 else (1 if stage == 2 else 2)
        t_prev_writes = (p.dsem.get("ysp0"), p.dma_cnt.get("ysp0")) \
            if nlayers else None
        YB2 = sb("YB2", [128, 2, CH, FT], BF16)
        YLC = sb("YLC", [128, CH, 2], BF16)

        def col2d2(buf, i):
            return buf[:, :, i:i + 1].rearrange("p c n -> p (c n)")

        def emit_layer(l, t_prev_writes, t_front, t_gate_prev):
            pa = ExitStack()
            asb = lambda nm, shp, dty: pa.enter_context(
                nc.sbuf_tensor(nm + f"_l{l}", shp, dty))
            WQKV = asb("WQKV", [128, CH, 1536], BF16)
            WQSW = asb("WQSW", [128, CH, 1024], BF16)
            WOF = asb("WOF", [64, CH, 1024], BF16)
            MSK = asb("MSK", [128, 512 + 128 + 2048], BF16)
            HQ = asb("HQ", [128, CH, FT], BF16)
            KR = asb("KR", [128, 4, 1024], BF16)
            VR = asb("VR", [128, 8, 512], BF16)
            QH = asb("QH", [128, 4, FT], BF16)
            QA = asb("QA", [128, 4, FT], F32)
            QB2 = asb("QB2", [128, 4, FT], F32)
            RT = asb("RT", [128, 4, FT], F32)
            SQB = asb("SQB", [128, 4, FT], BF16)
            PRB = asb("PRB", [128, 2, 640], BF16)
            PTB = asb("PTB", [128, 2, 640], BF16)
            RRB = asb("RRB", [8, 1024], BF16)
            OH = asb("OH", [64, CH, FT], BF16)
            PA = [pa.enter_context(
                nc.psum_tensor(f"pa{i}_l{l}", [128, 512], F32))
                for i in range(4)]
            BROW = pa.enter_context(
                nc.psum_tensor(f"brow_l{l}", [128, 512], F32))
            BSA = pa.enter_context(
                nc.psum_tensor(f"bsa_l{l}", [128, 512], F32))
            PBF2 = [pa.enter_context(
                nc.psum_tensor(f"pbf{i}_l{l}", [128, 640], BF16))
                for i in range(2)]

            gate = (list(t_front) if l == 0 else list(t_gate_prev)) \
                + p.all_dma_toks()
            t_wfa = p.dma("g", f"wfa{l}", WQKV[:],
                          wqkv_h[l].rearrange("(c p) o -> p c o", p=128),
                          waits=gate)
            p.dma("g", f"wfa{l}", WQSW[:],
                  wqsw_h[l].rearrange("(c p) o -> p c o", p=128))
            p.dma("g", f"wfa{l}", WOF[:],
                  wof_h[l].rearrange("(kc p) o -> p kc o", p=64))
            m_w = 512 + 128 + 2048
            def mskload():
                o0, _ = cbf_off["maskA"]
                return cstbf_h[:, o0:o0 + m_w]
            t_wfa = p.dma("g", f"wfa{l}", MSK[:], mskload())
            mA_off, mB_off, mA0_off = 0, 512, 640

            def zKR(e):
                return e.memset(KR[:, :, 0:512], 0.0)
            p.add("g", zKR, waits=gate)
            def zVR(e):
                return e.memset(VR[:, 0:4, :], 0.0)
            p.add("g", zVR)
            t_ring0 = p.tok("g")
            def zYL(e):
                return e.tensor_scalar(col2d2(YLC, 0), smw("w"), 0.0, 0.0,
                                       ALU.mult, ALU.add)
            p.add("v", zYL, waits=gate)

            t_store_par = [None, None]
            t_ringcp = t_ring0
            for fb in range(NF):
                col = fb * FT
                ypar = fb % 2
                YBF = YB2[:, ypar]
                w_l = (list(t_prev_writes)
                       if isinstance(t_prev_writes, list)
                       else [t_prev_writes])
                if t_store_par[ypar] is not None:
                    w_l.append(t_store_par[ypar])
                if fb == 0:
                    w_l.extend(gate)
                t_yl = p.dma("y", f"ya{l}", YBF[:],
                             ysp_h[:, :, col:col + FT], waits=w_l)
                t_rt = p.dma("y", f"rt{l}{fb % 2}", RT[:],
                             rot_h[:, :, col:col + FT].rearrange(
                                 "r p t -> p r t"),
                             waits=[(p.sem["v"], p.cnt["v"])])
                def hq1(e, YBF=YBF):
                    return e.tensor_copy(HQ[:, 0:4, :], YBF[:, 0:4, :])
                p.add("v", hq1, waits=[t_yl])
                def hq2(e, YBF=YBF):
                    return e.tensor_copy(HQ[:, 4:8, 1:FT],
                                         YBF[:, 4:8, 0:FT - 1])
                p.add("v", hq2)
                def hq3(e):
                    return e.tensor_copy(HQ[:, 4:8, 0:1], YLC[:, 4:8, 0:1])
                p.add("v", hq3)
                def ylup(e, YBF=YBF):
                    return e.tensor_copy(col2d2(YLC, 0),
                                         YBF[:, :, FT - 1:FT]
                                         .rearrange("p c n -> p (c n)"))
                p.add("v", ylup)
                t_hq = p.tok("v")

                for tc in range(4):
                    for kc in range(CH):
                        def mv(e, tc=tc, kc=kc):
                            return e.matmul(
                                PA[tc][:], HQ[:, kc, tc * 128:tc * 128 + 128],
                                WQKV[:, kc, 1024:1536],
                                start=(kc == 0), stop=(kc == CH - 1))
                        w = []
                        if tc == 0 and kc == 0:
                            w = [t_hq, t_wfa, t_ring0]
                        p.add("pe", mv, inc=(tc == 3 and kc == CH - 1),
                              waits=w)
                t_vw = p.tok("pe")
                for tc in range(4):
                    def ev(e, tc=tc):
                        return e.tensor_copy(VR[:, 4 + tc, :], PA[tc][:])
                    w = []
                    if tc == 0:
                        w = [t_vw, t_ringcp]
                    p.add("v", ev, waits=w)
                t_vev = p.tok("v")

                def wave(dst_off, swapped, waits0):
                    W_, cols = (WQSW, (dst_off, dst_off + 512)) if swapped \
                        else (WQKV, (dst_off, dst_off + 512))
                    for oc in range(4):
                        for kc in range(CH):
                            def mm(e, oc=oc, kc=kc, W_=W_, cols=cols):
                                return e.matmul(
                                    PA[oc][:],
                                    W_[:, kc, cols[0] + oc * 128:
                                       cols[0] + oc * 128 + 128],
                                    HQ[:, kc, :],
                                    start=(kc == 0), stop=(kc == CH - 1))
                            w = list(waits0) if (oc == 0 and kc == 0) else []
                            p.add("pe", mm,
                                  inc=(oc == 3 and kc == CH - 1), waits=w)
                    return p.tok("pe")

                def qk_pipeline(dst_off, qsnm, qswnm, rows_cols, rt_cs, rt_sn,
                                out_to_kr):
                    t_w = wave(dst_off, False, [p.tok("v"), p.tok("s")])
                    for c in range(4):
                        def sq(e, c=c):
                            return e.activation(SQB[:, c, :], PA[c][:],
                                                AF.Square)
                        p.add("s", sq, waits=([t_w] if c == 0 else ()))
                    t_sq = p.tok("s")
                    for c in range(4):
                        def qa(e, c=c, rt_cs=rt_cs, qsnm=qsnm):
                            return e.scalar_tensor_tensor(
                                QA[:, c, :], PA[c][:], smw(qsnm),
                                RT[:, rt_cs, :], ALU.mult, ALU.mult)
                        p.add("v", qa, waits=([t_sq, t_rt] if c == 0 else ()))
                    t_qa = p.tok("v")
                    for c in range(4):
                        def sm(e, c=c):
                            return e.matmul(
                                BROW[0:8, 0:512],
                                cbf("hsum")[:, c * 8:c * 8 + 8],
                                SQB[:, c, :], start=(c == 0), stop=(c == 3))
                        p.add("pe", sm, inc=(c == 3),
                              waits=([t_sq] if c == 0 else ()))
                    t_sums = p.tok("pe")
                    t_w2 = wave(dst_off, True, [t_qa])
                    def rc(e, rows_cols=rows_cols):
                        return e.tensor_copy(
                            RRF[0:8, rows_cols:rows_cols + 512],
                            BROW[0:8, 0:512])
                    p.add("v", rc, waits=[t_sums])
                    def rm(e, rows_cols=rows_cols):
                        return e.tensor_scalar_max(
                            RRF[0:8, rows_cols:rows_cols + 512],
                            RRF[0:8, rows_cols:rows_cols + 512], 1e-24)
                    p.add("v", rm)
                    t_rv = p.tok("v")
                    def rl(e, rows_cols=rows_cols):
                        return e.activation(
                            RRF[0:8, rows_cols:rows_cols + 512],
                            RRF[0:8, rows_cols:rows_cols + 512], AF.Ln)
                    p.add("s", rl, waits=[t_rv])
                    def re_(e, rows_cols=rows_cols):
                        return e.activation(
                            RRB[0:8, rows_cols:rows_cols + 512],
                            RRF[0:8, rows_cols:rows_cols + 512], AF.Exp,
                            scale=-0.5)
                    p.add("s", re_)
                    t_rr = p.tok("s")
                    for c in range(4):
                        def qb(e, c=c, rt_sn=rt_sn, qswnm=qswnm):
                            return e.scalar_tensor_tensor(
                                QB2[:, c, :], PA[c][:], smw(qswnm),
                                RT[:, rt_sn, :], ALU.mult, ALU.mult)
                        p.add("v", qb, waits=([t_w2] if c == 0 else ()))
                    t_qb = p.tok("v")
                    def gadd(e):
                        return e.tensor_tensor(QA[:], QA[:], QB2[:], ALU.add)
                    p.add("v", gadd, waits=[t_qb])
                    t_g = p.tok("v")
                    for c in range(4):
                        def bc(e, c=c, rows_cols=rows_cols):
                            return e.matmul(
                                PA[c][:, 0:512],
                                cbf("hbc", 0, 8)[:, c * 128:c * 128 + 128],
                                RRB[0:8, rows_cols:rows_cols + 512],
                                start=True, stop=True)
                        w = [t_rr, t_g, t_qb] if c == 0 else []
                        p.add("pe", bc, inc=(c == 3), waits=w)
                    t_bcq = p.tok("pe")
                    for c in range(4):
                        if out_to_kr:
                            def fm(e, c=c):
                                return e.tensor_tensor(
                                    KR[:, c, 512:1024], QA[:, c, :],
                                    PA[c][:, 0:512], ALU.mult)
                        else:
                            def fm(e, c=c):
                                return e.tensor_tensor(
                                    QH[:, c, :], QA[:, c, :],
                                    PA[c][:, 0:512], ALU.mult)
                        p.add("v", fm, waits=([t_bcq] if c == 0 else ()))
                    return p.tok("v")

                t_qh = qk_pipeline(0, f"qs{l}", f"qsw{l}", 0, 0, 1, False)
                t_kh = qk_pipeline(512, f"ks{l}", f"ksw{l}", 512, 2, 3, True)

                its = [(h, qc) for h in range(HEADS) for qc in range(4)]
                t_ex_par = [None, None]
                t_pv_par = [None, None]
                t_ptv_par = [None, None]
                t_oev = None
                st_pn = {}
                st_tp = {}

                def emit_scores(it):
                    h, qc = its[it]
                    par = it % 2
                    w0 = 128 * qc
                    pq = (h % 2) * 64
                    tcp = h // 2
                    sA = PA[par]
                    sB = PA[2 + par]
                    def s1m(e, pq=pq, tcp=tcp, w0=w0, sA=sA):
                        return e.matmul(
                            sA[:, 0:512],
                            QH[pq:pq + 64, tcp, w0:w0 + 128],
                            KR[pq:pq + 64, tcp, w0:w0 + 512],
                            start=True, stop=False)
                    w = [t_qh, t_kh] if it == 0 else []
                    if t_ex_par[par] is not None:
                        w = w + [t_ex_par[par]]
                    p.add("pe", s1m, waits=w)
                    if fb == 0:
                        mA = MSK[:, mA0_off + 512 * qc:mA0_off + 512 * qc + 512]
                    else:
                        mA = MSK[:, mA_off:mA_off + 512]
                    def s1k(e, mA=mA, sA=sA):
                        return e.matmul(sA[:, 0:512], cbf("ident"), mA,
                                        start=False, stop=True)
                    p.add("pe", s1k)
                    def s2m(e, pq=pq, tcp=tcp, w0=w0, sB=sB):
                        return e.matmul(
                            sB[:, 0:128],
                            QH[pq:pq + 64, tcp, w0:w0 + 128],
                            KR[pq:pq + 64, tcp, w0 + 512:w0 + 640],
                            start=True, stop=False)
                    p.add("pe", s2m)
                    def s2k(e, sB=sB):
                        return e.matmul(sB[:, 0:128], cbf("ident"),
                                        MSK[:, mB_off:mB_off + 128],
                                        start=False, stop=True)
                    p.add("pe", s2k, inc=True)
                    return p.tok("pe")

                st_ex = {}

                def emit_vmex(it, t_sc):
                    par = it % 2
                    sA = PA[par]
                    sB = PA[2 + par]
                    so = 1536 + 6 * par
                    nmA = RRF[:, so:so + 1]
                    nmB = RRF[:, so + 1:so + 2]
                    nm = RRF[:, so + 2:so + 3]
                    s1c = RRF[:, so + 3:so + 4]
                    s2c = RRF[:, so + 4:so + 5]
                    def vm1(e, sA=sA, nmA=nmA):
                        return e.reduce_max(nmA, sA[:, 0:512], axis=AX.X,
                                            negate=True)
                    p.add("v", vm1, waits=[t_sc])
                    def vm2(e, sB=sB, nmB=nmB):
                        return e.reduce_max(nmB, sB[:, 0:128], axis=AX.X,
                                            negate=True)
                    p.add("v", vm2)
                    def vm3(e, nm=nm, nmA=nmA, nmB=nmB):
                        return e.tensor_tensor(nm, nmA, nmB, ALU.min)
                    p.add("v", vm3)
                    t_nm = p.tok("v")
                    def ex1(e, sA=sA, nm=nm, s1c=s1c, par=par):
                        return e.activation(PRB[:, par, 0:512], sA[:, 0:512],
                                            AF.Exp, bias=nm, accum_out=s1c)
                    p.add("s", ex1, waits=[t_nm])
                    def ex2(e, sB=sB, nm=nm, s2c=s2c, par=par):
                        return e.activation(PRB[:, par, 512:640],
                                            sB[:, 0:128], AF.Exp,
                                            bias=nm, accum_out=s2c)
                    p.add("s", ex2, inc=True)
                    st_ex[it] = p.tok("s")
                    t_ex_par[par] = st_ex[it]

                def emit_vnorm(it):
                    par = it % 2
                    so = 1536 + 6 * par
                    s1c = RRF[:, so + 3:so + 4]
                    s2c = RRF[:, so + 4:so + 5]
                    def vn1(e, s1c=s1c, s2c=s2c):
                        return e.tensor_tensor(s1c, s1c, s2c, ALU.add)
                    p.add("v", vn1, waits=[st_ex[it]])
                    def vn2(e, s1c=s1c):
                        return e.reciprocal(s1c, s1c)
                    p.add("v", vn2)
                    def vn3(e, s1c=s1c, par=par):
                        return e.tensor_scalar(PRB[:, par, :], PRB[:, par, :],
                                               s1c, None, ALU.mult)
                    p.add("v", vn3)
                    st_pn[it] = p.tok("v")

                def emit_tp(it):
                    par = it % 2
                    for kk in range(5):
                        def tp(e, kk=kk, par=par):
                            return e.transpose(
                                PBF2[par][:, kk * 128:kk * 128 + 128],
                                PRB[:, par, kk * 128:kk * 128 + 128],
                                cbf("ident"))
                        w = [st_pn[it]] if kk == 0 else []
                        if kk == 0 and t_ptv_par[par] is not None:
                            w.append(t_ptv_par[par])
                        p.add("pe", tp, inc=(kk == 4), waits=w)
                    st_tp[it] = p.tok("pe")

                def emit_ptv(it):
                    par = it % 2
                    def ptv(e, par=par):
                        return e.activation(PTB[:, par, :], PBF2[par][:],
                                            AF.Copy)
                    w = [st_tp[it]]
                    if t_pv_par[par] is not None:
                        w.append(t_pv_par[par])
                    t = (p.sem["s"], p.add("s", ptv, waits=w))
                    t_ptv_par[par] = t
                    return t

                def emit_pv(it, t_ptv):
                    nonlocal t_oev
                    h, qc = its[it]
                    par = it % 2
                    for kk in range(5):
                        def pv(e, kk=kk, h=h, qc=qc, par=par):
                            return e.matmul(
                                BSA[0:64, 0:128],
                                VR[:, qc + kk, h * 64:h * 64 + 64],
                                PTB[:, par, kk * 128:kk * 128 + 128],
                                start=(kk == 0), stop=(kk == 4))
                        w = [t_ptv] if kk == 0 else []
                        if kk == 0 and t_oev is not None:
                            w.append(t_oev)
                        p.add("pe", pv, inc=(kk == 4), waits=w)
                    t_pv = p.tok("pe")
                    t_pv_par[par] = t_pv
                    def oev(e, h=h, qc=qc):
                        return e.tensor_copy(
                            OH[0:64, h, qc * 128:qc * 128 + 128],
                            BSA[0:64, 0:128])
                    t_oev = (p.sem["v"], p.add("v", oev, waits=[t_pv]))

                sc_tok = {}
                ptv_tok = {}
                n_its = len(its)
                for it in range(n_its + 3):
                    if it < n_its:
                        sc_tok[it] = emit_scores(it)
                    if 1 <= it <= n_its:
                        emit_vmex(it - 1, sc_tok[it - 1])
                        emit_vnorm(it - 1)
                    if 2 <= it <= n_its + 1:
                        emit_tp(it - 2)
                        ptv_tok[it - 2] = emit_ptv(it - 2)
                    if 3 <= it <= n_its + 2:
                        emit_pv(it - 3, ptv_tok[it - 3])
                t_oh = p.tok("v")

                for half in range(2):
                    for oc4 in range(4):
                        oc = half * 4 + oc4
                        for kc in range(CH):
                            def mo(e, oc=oc, oc4=oc4, kc=kc):
                                return e.matmul(
                                    PA[oc4][:],
                                    WOF[0:64, kc, oc * 128:oc * 128 + 128],
                                    OH[0:64, kc, :], start=(kc == 0),
                                    stop=(kc == CH - 1))
                            w = []
                            if oc4 == 0 and kc == 0:
                                w = [t_oh] if half == 0 else [p.tok("v")]
                            p.add("pe", mo, inc=(oc4 == 3 and kc == CH - 1),
                                  waits=w)
                    t_wo = p.tok("pe")
                    for oc4 in range(4):
                        oc = half * 4 + oc4
                        def yup(e, oc=oc, oc4=oc4, YBF=YBF):
                            return e.tensor_tensor(YBF[:, oc, :],
                                                   YBF[:, oc, :],
                                                   PA[oc4][:], ALU.add)
                        p.add("v", yup, waits=([t_wo] if oc4 == 0 else ()))
                t_yb = p.tok("v")
                t_store_par[ypar] = p.dma("y", f"yw{l}a{ypar}",
                                          ysp_h[:, :, col:col + FT], YBF[:],
                                          waits=[t_yb])
                t_pe_rb = p.tok("pe")
                p.dma("y", f"rg{l}", KR[:, :, 0:512], KR[:, :, 512:1024],
                      waits=[t_pe_rb])
                t_ringcp = p.dma("y", f"rg{l}", VR[:, 0:4, :], VR[:, 4:8, :])
            t_pa_done = p.tok("pe")
            t_prev_writes = [(p.dsem[f"yw{l}a0"], p.dma_cnt[f"yw{l}a0"]),
                             (p.dsem[f"yw{l}a1"], p.dma_cnt[f"yw{l}a1"])]
            if dbg_dump:
                t_prev_writes2 = p.dma("y", "dby", dby_h[f"a{l}"], ysp_h,
                                       waits=list(t_prev_writes))
                t_prev_writes = [t_prev_writes2]
            pa.close()

            pb = ExitStack()
            bsb = lambda nm, shp, dty: pb.enter_context(
                nc.sbuf_tensor(nm + f"_l{l}", shp, dty))
            W1I = bsb("W1I", [128, CH, NSLOT * 128], BF16)
            W2P = bsb("W2P", [128, NJ, 1024], BF16)
            HACT = bsb("HACT", [128, NJ, FT], BF16)
            HB = bsb("HB", [128, CH, FT], BF16)
            SQ2 = bsb("SQ2", [128, CH, FT], BF16)
            TMP1 = bsb("TMP1", [128, FT], F32)
            RSB = bsb("RSB", [1, FT], BF16)
            PB_ = [pb.enter_context(
                nc.psum_tensor(f"pb{i}_l{l}", [128, 512], F32))
                for i in range(4)]
            PBA = pb.enter_context(
                nc.psum_tensor(f"pba_l{l}", [128, 512], F32))
            PBG = pb.enter_context(
                nc.psum_tensor(f"pbg_l{l}", [128, 512], F32))
            PR0 = pb.enter_context(
                nc.psum_tensor(f"pr0_l{l}", [128, 512], F32))
            PR1 = pb.enter_context(
                nc.psum_tensor(f"pr1_l{l}", [128, 512], F32))

            w1r = w1i_h[l].rearrange("(c p) s -> p c s", p=128)
            gateb = [t_pa_done, p.tok("v"), p.tok("s"),
                     p.tok("g")] + p.all_dma_toks()
            W1GRP = 2
            wfb_grp = []
            for jj in range(0, NJ, W1GRP):
                je = min(jj + W1GRP, NJ)
                tk = p.dma("g", f"wfb{l}g{jj // W1GRP}",
                           W1I[:, :, 2 * jj * 128:2 * je * 128],
                           w1r[:, :, 2 * jj * 128:2 * je * 128],
                           waits=(gateb if jj == 0 else ()))
                wfb_grp.append(tk)
            w2r = w2p_h[l].rearrange("(j p) o -> p j o", p=128)
            for jj in range(0, NJ, 4):
                je = min(jj + 4, NJ)
                t_wfb = p.dma("g", f"wfb{l}", W2P[:, jj:je, :],
                              w2r[:, jj:je, :])
            t_wfb_w2p = t_wfb

            def zYL2(e):
                return e.tensor_scalar(col2d2(YLC, 1), smw("w"), 0.0, 0.0,
                                       ALU.mult, ALU.add)
            p.add("v", zYL2)

            t_store_par = [None, None]
            for fb in range(NF):
                col = fb * FT
                ypar = fb % 2
                YBF = YB2[:, ypar]
                w_l = (list(t_prev_writes)
                       if isinstance(t_prev_writes, list)
                       else [t_prev_writes])
                if t_store_par[ypar] is not None:
                    w_l.append(t_store_par[ypar])
                t_yl = p.dma("y", f"yb{l}", YBF[:],
                             ysp_h[:, :, col:col + FT], waits=w_l)
                def c1(e, YBF=YBF):
                    return e.tensor_copy(HB[:, 0:4, :], YBF[:, 0:4, :])
                p.add("v", c1, waits=[t_yl])
                def c2(e, YBF=YBF):
                    return e.tensor_copy(HB[:, 4:8, 1:FT],
                                         YBF[:, 4:8, 0:FT - 1])
                p.add("v", c2)
                def c3(e):
                    return e.tensor_copy(HB[:, 4:8, 0:1], YLC[:, 4:8, 1:2])
                p.add("v", c3)
                t_cb = p.tok("v")
                def sq1(e, YBF=YBF):
                    return e.activation(SQ2[:, 0:4, :], YBF[:, 0:4, :],
                                        AF.Square)
                p.add("s", sq1, waits=[t_yl, (p.sem["v"], p.cnt["v"])])
                def sq2_(e, YBF=YBF):
                    return e.activation(SQ2[:, 4:8, 1:FT],
                                        YBF[:, 4:8, 0:FT - 1], AF.Square)
                p.add("s", sq2_)
                def sq3(e):
                    return e.activation(SQ2[:, 4:8, 0:1], YLC[:, 4:8, 1:2],
                                        AF.Square)
                p.add("s", sq3)
                t_sq = p.tok("s")
                for c in range(CH):
                    def m0(e, c=c):
                        return e.matmul(PR0[0:1, 0:FT], cbf("ones1"),
                                        HB[:, c, :], start=(c == 0),
                                        stop=(c == CH - 1))
                    p.add("pe", m0, waits=([t_cb] if c == 0 else ()))
                for c in range(CH):
                    def m1(e, c=c):
                        return e.matmul(PR1[0:1, 0:FT], cbf("ones1"),
                                        SQ2[:, c, :], start=(c == 0),
                                        stop=(c == CH - 1))
                    p.add("pe", m1, inc=(c == CH - 1),
                          waits=([t_sq] if c == 0 else ()))
                t_st = p.tok("pe")
                mrow = RRF[0:1, 1024:1024 + FT]
                srow = RRF[0:1, 0:FT]
                xrow = RRF[0:1, 512:512 + FT]
                def q1(e):
                    return e.tensor_scalar(mrow, PR0[0:1, 0:FT], 1.0 / D,
                                           None, ALU.mult)
                p.add("v", q1, waits=[t_st])
                def q2(e):
                    return e.tensor_tensor(xrow, mrow, mrow, ALU.mult)
                p.add("v", q2)
                def q3(e):
                    return e.scalar_tensor_tensor(srow, PR1[0:1, 0:FT],
                                                  1.0 / D, xrow, ALU.mult,
                                                  ALU.subtract)
                p.add("v", q3)
                t_v2 = p.tok("v")
                def q4(e):
                    return e.activation(srow, srow, AF.Ln,
                                        bias=smw("eps")[0:1, :])
                p.add("s", q4, waits=[t_v2])
                def q6(e):
                    return e.activation(RSB[0:1, 0:FT], srow, AF.Exp,
                                        scale=-0.5)
                p.add("s", q6)
                t_rw = p.tok("s")
                def bb1(e):
                    return e.matmul(PR0[:, 0:FT], c32("ones", 0, 1), mrow,
                                    start=True, stop=True)
                p.add("pe", bb1, waits=[t_rw])
                def bb2(e):
                    return e.matmul(PR1[:, 0:FT], cbf("onesr", 0, 1),
                                    RSB[0:1, 0:FT], start=True, stop=True)
                p.add("pe", bb2, inc=True)
                t_bc = p.tok("pe")
                for c in range(CH):
                    if c < 4:
                        def ap1(e, c=c, YBF=YBF):
                            return e.tensor_tensor(TMP1[:], YBF[:, c, :],
                                                   PR0[:, 0:FT],
                                                   ALU.subtract)
                        p.add("v", ap1, waits=([t_bc] if c == 0 else ()))
                    else:
                        def ap1(e, c=c, YBF=YBF):
                            return e.tensor_tensor(TMP1[:, 1:FT],
                                                   YBF[:, c, 0:FT - 1],
                                                   PR0[:, 1:FT],
                                                   ALU.subtract)
                        p.add("v", ap1, waits=([t_bc] if c == 0 else ()))
                        def ap1b(e, c=c):
                            return e.tensor_tensor(TMP1[:, 0:1],
                                                   YLC[:, c, 1:2],
                                                   PR0[:, 0:1], ALU.subtract)
                        p.add("v", ap1b)
                    def ap2(e, c=c):
                        return e.scalar_tensor_tensor(
                            TMP1[:], TMP1[:], smw(f"lng{l}")[:, c:c + 1],
                            PR1[:, 0:FT], ALU.mult, ALU.mult)
                    p.add("v", ap2)
                    def ap3(e, c=c):
                        return e.tensor_scalar(HB[:, c, :], TMP1[:],
                                               smw(f"lnb{l}")[:, c:c + 1],
                                               None, ALU.add)
                    p.add("v", ap3)
                t_h1 = p.tok("v")
                if dbg_dump and l == 0 and fb == 0:
                    t_dbhb = p.dma("g", "dbg2", dbhb_h, HB[:], waits=[t_h1])
                    t_dw1 = p.dma("g", "dbg2", dbw1_h, W1I[:],
                                  waits=[t_h1, t_wfb])
                    p.add("v", lambda e: e.engine_nop(), waits=[t_dw1])
                def ylup2(e, YBF=YBF):
                    return e.tensor_copy(col2d2(YLC, 1),
                                         YBF[:, :, FT - 1:FT]
                                         .rearrange("p c n -> p (c n)"))
                p.add("v", ylup2, waits=[t_sq])
                t_evj = [None, None]
                for j in range(NJ):
                    par = j % 2
                    bA = PBA if par == 0 else PB_[0]
                    bG = PBG if par == 0 else PB_[1]
                    cA, cG = (1, 0) if par == 0 else (3, 2)
                    for kc in range(CH):
                        def ma(e, j=j, kc=kc, bA=bA):
                            return e.matmul(
                                bA[:], W1I[:, kc,
                                           (2 * j) * 128:(2 * j) * 128 + 128],
                                HB[:, kc, :], start=(kc == 0),
                                stop=(kc == CH - 1))
                        w = []
                        if kc == 0:
                            w = [t_h1] if j == 0 else (
                                [t_evj[par]] if t_evj[par] else [])
                            if fb == 0 and j % W1GRP == 0:
                                w = list(w) + [wfb_grp[j // W1GRP]]
                        p.add("pe", ma, waits=w)
                    for kc in range(CH):
                        def mg(e, j=j, kc=kc, bG=bG):
                            return e.matmul(
                                bG[:], W1I[:, kc,
                                           (2 * j + 1) * 128:
                                           (2 * j + 1) * 128 + 128],
                                HB[:, kc, :], start=(kc == 0),
                                stop=(kc == CH - 1))
                        p.add("pe", mg, inc=(kc == CH - 1))
                    t_j = p.tok("pe")
                    def gg(e, j=j, bG=bG, cG=cG):
                        return e.activation(SQ2[:, cG, 0:FT], bG[:, 0:FT],
                                            AF.Gelu,
                                            bias=smw(f"b1i{l}")
                                            [:, 2 * j + 1:2 * j + 2])
                    p.add("s", gg, waits=[t_j])
                    def aa(e, j=j, bA=bA, cA=cA):
                        return e.tensor_scalar(SQ2[:, cA, 0:FT], bA[:, 0:FT],
                                               smw(f"b1i{l}")
                                               [:, 2 * j:2 * j + 1],
                                               None, ALU.add)
                    p.add("v", aa, waits=[t_j])
                    t_ag = p.tok("s")
                    def hmul(e, j=j, cA=cA, cG=cG):
                        return e.tensor_tensor(HACT[:, j, :], SQ2[:, cA, 0:FT],
                                               SQ2[:, cG, 0:FT], ALU.mult)
                    t_evj[par] = (p.sem["v"], p.add("v", hmul, waits=[t_ag]))
                t_hact = p.tok("v")
                if dbg_dump and l == 0 and fb == 0:
                    t_dbha = p.dma("g", "dbg2", dbha_h, HACT[:],
                                   waits=[t_hact])
                    p.add("v", lambda e: e.engine_nop(),
                          waits=[t_dbha])
                for half in range(2):
                    for oc4 in range(4):
                        oc = half * 4 + oc4
                        for j in range(NJ):
                            def m2(e, oc=oc, oc4=oc4, j=j):
                                return e.matmul(
                                    PB_[oc4][:],
                                    W2P[:, j, oc * 128:oc * 128 + 128],
                                    HACT[:, j, :], start=(j == 0),
                                    stop=(j == NJ - 1))
                            w = []
                            if oc4 == 0 and j == 0:
                                w = [t_hact] if half == 0 else [p.tok("v")]
                                if half == 0 and fb == 0:
                                    w.append(t_wfb_w2p)
                            p.add("pe", m2,
                                  inc=(oc4 == 3 and j == NJ - 1), waits=w)
                    t_w2 = p.tok("pe")
                    for oc4 in range(4):
                        oc = half * 4 + oc4
                        def yup(e, oc=oc, oc4=oc4, YBF=YBF):
                            return e.scalar_tensor_tensor(
                                YBF[:, oc, :], PB_[oc4][:],
                                smw(f"b2_{l}")[:, oc:oc + 1], YBF[:, oc, :],
                                ALU.add, ALU.add)
                        p.add("v", yup, waits=([t_w2] if oc4 == 0 else ()))
                t_yb2 = p.tok("v")
                t_store_par[ypar] = p.dma("y", f"yw{l}b{ypar}",
                                          ysp_h[:, :, col:col + FT], YBF[:],
                                          waits=[t_yb2])
            t_pb_done = p.tok("pe")
            t_prev_writes = [(p.dsem[f"yw{l}b0"], p.dma_cnt[f"yw{l}b0"]),
                             (p.dsem[f"yw{l}b1"], p.dma_cnt[f"yw{l}b1"])]
            if dbg_dump:
                t_prev_writes = [p.dma("y", "dby", dby_h[f"b{l}"], ysp_h,
                                       waits=list(t_prev_writes))]
            t_gate_prev = [t_pb_done, p.tok("v"), p.tok("s"), p.tok("g")]
            pb.close()
            return t_prev_writes, t_gate_prev

        t_gate_prev = None
        for l in range(nlayers):
            t_prev_writes, t_gate_prev = emit_layer(
                l, t_prev_writes, t_front, t_gate_prev)

        with nc.sbuf_tensor("XO", [128, CH, FT], BF16) as XO, \
             nc.sbuf_tensor("XOF", [128, CH, FT], F32) as XOF, \
             nc.sbuf_tensor("YF4", [128, CH, FT], BF16) as YF4, \
             nc.sbuf_tensor("CB4", [128, CH, FT], BF16) as CB4, \
             nc.sbuf_tensor("SQ4", [128, CH, FT], BF16) as SQ4, \
             nc.sbuf_tensor("TMP4", [128, FT], F32) as TMP4, \
             nc.sbuf_tensor("RW4", [1, 1536], F32) as RW4, \
             nc.sbuf_tensor("RSB4", [1, FT], BF16) as RSB4, \
             nc.sbuf_tensor("OT", [128, 4, D], F32) as OT, \
             nc.psum_tensor("PF0", [128, 512], F32) as PF0, \
             nc.psum_tensor("PF1", [128, 512], F32) as PF1, \
             nc.psum_tensor("FR0", [128, 512], F32) as FR0, \
             nc.psum_tensor("FR1", [128, 512], F32) as FR1:
            pt_evtok = {0: None, 1: None}
            out_dma_tok = {tc: None for tc in range(4)}
            last_pe_tok = None
            if stage < 2:
                t_front = [t_xsp, p.tok("v"), p.tok("s"), p.tok("pe")]
                t_prev_writes = None
            for fb in range(NF):
                col = fb * FT
                w_lx = [t_xsp] + (list(t_front[1:2]) if fb == 0 else [])
                if fb > 0:
                    w_lx.append(prev_vgate)
                if fb == 0:
                    w_lx.append((p.sem["v"], p.cnt["v"]))
                    w_lx.append((p.sem["s"], p.cnt["s"]))
                    w_lx.append(p.tok("pe"))
                    w_lx.extend(p.all_dma_toks())
                t_lx = p.dma("y", "xo", XO[:], xsp_h[:, :, col:col + FT],
                             waits=w_lx)
                t_vgate = None
                if stage >= 2:
                    wly = (list(t_prev_writes)
                           if isinstance(t_prev_writes, list)
                           else [t_prev_writes])
                    t_ly = p.dma("y", "yf4", YF4[:],
                                 ysp_h[:, :, col:col + FT],
                                 waits=wly + [t_lx])
                    def c4a(e):
                        return e.tensor_copy(CB4[:], YF4[:])
                    p.add("v", c4a, waits=[t_ly])
                    t_c4 = p.tok("v")
                    def s4a(e):
                        return e.activation(SQ4[:], YF4[:], AF.Square)
                    p.add("s", s4a, waits=[t_ly, (p.sem["v"], p.cnt["v"])])
                    t_s4 = p.tok("s")
                    for c in range(CH):
                        def m0(e, c=c):
                            return e.matmul(FR0[0:1, 0:FT], cbf("ones1"),
                                            CB4[:, c, :], start=(c == 0),
                                            stop=(c == CH - 1))
                        p.add("pe", m0, waits=([t_c4] if c == 0 else ()))
                    for c in range(CH):
                        def m1(e, c=c):
                            return e.matmul(FR1[0:1, 0:FT], cbf("ones1"),
                                            SQ4[:, c, :], start=(c == 0),
                                            stop=(c == CH - 1))
                        p.add("pe", m1, inc=(c == CH - 1),
                              waits=([t_s4] if c == 0 else ()))
                    t_st4 = p.tok("pe")
                    mrow = RW4[0:1, 0:FT]
                    srow = RW4[0:1, 512:512 + FT]
                    xrow = RW4[0:1, 1024:1024 + FT]
                    def q1(e):
                        return e.tensor_scalar(mrow, FR0[0:1, 0:FT], 1.0 / D,
                                               None, ALU.mult)
                    p.add("v", q1, waits=[t_st4])
                    def q2(e):
                        return e.tensor_tensor(xrow, mrow, mrow, ALU.mult)
                    p.add("v", q2)
                    def q3(e):
                        return e.scalar_tensor_tensor(srow, FR1[0:1, 0:FT],
                                                      1.0 / D, xrow,
                                                      ALU.mult, ALU.subtract)
                    p.add("v", q3)
                    t_v4 = p.tok("v")
                    def q4(e):
                        return e.activation(srow, srow, AF.Ln,
                                            bias=smw("eps")[0:1, :])
                    p.add("s", q4, waits=[t_v4])
                    def q6(e):
                        return e.activation(RSB4[0:1, 0:FT], srow, AF.Exp,
                                            scale=-0.5)
                    p.add("s", q6)
                    t_r4 = p.tok("s")
                    def bb1(e):
                        return e.matmul(FR0[:, 0:FT], c32("ones", 0, 1),
                                        mrow, start=True, stop=True)
                    p.add("pe", bb1, waits=[t_r4])
                    def bb2(e):
                        return e.matmul(FR1[:, 0:FT], cbf("onesr", 0, 1),
                                        RSB4[0:1, 0:FT], start=True,
                                        stop=True)
                    p.add("pe", bb2, inc=True)
                    t_bc4 = p.tok("pe")
                    for c in range(CH):
                        def f1(e, c=c):
                            return e.tensor_tensor(TMP4[:], YF4[:, c, :],
                                                   FR0[:, 0:FT],
                                                   ALU.subtract)
                        w = [t_bc4, t_lx] if c == 0 else []
                        if c == 0 and last_pe_tok is not None:
                            w.append(last_pe_tok)
                        p.add("v", f1, waits=w)
                        def f2(e, c=c):
                            return e.tensor_tensor(TMP4[:], TMP4[:],
                                                   FR1[:, 0:FT], ALU.mult)
                        p.add("v", f2)
                        def f3(e, c=c):
                            return e.tensor_tensor(XOF[:, c, :], XO[:, c, :],
                                                   TMP4[:], ALU.add)
                        p.add("v", f3)
                    t_vgate = p.tok("v")
                else:
                    def cpx(e):
                        return e.tensor_copy(XOF[:], XO[:])
                    wc = [t_lx]
                    if last_pe_tok is not None:
                        wc.append(last_pe_tok)
                    p.add("v", cpx, inc=True, waits=wc)
                    t_vgate = p.tok("v")
                prev_vgate = t_vgate
                for tc in range(4):
                    for half, PT in ((0, PF0), (1, PF1)):
                        for cc in range(4):
                            c = half * 4 + cc
                            def tr(e, tc=tc, c=c, cc=cc, PT=PT):
                                return e.transpose(
                                    PT[:, cc * 128:cc * 128 + 128],
                                    XOF[:, c, tc * 128:tc * 128 + 128],
                                    c32("ident"))
                            w = []
                            if cc == 0:
                                w.append(t_lx)
                                if t_vgate is not None:
                                    w.append(t_vgate)
                                if pt_evtok[half] is not None:
                                    w.append(pt_evtok[half])
                            p.add("pe", tr, inc=(cc == 3), waits=w)
                        tpe = p.tok("pe")
                        last_pe_tok = tpe
                        def ev(e, tc=tc, half=half, PT=PT):
                            return e.tensor_copy(
                                OT[:, tc, half * 512:half * 512 + 512], PT[:])
                        w = [tpe]
                        if half == 0 and out_dma_tok[tc] is not None:
                            w.append(out_dma_tok[tc])
                        ve = p.add("v", ev, waits=w)
                        pt_evtok[half] = (p.sem["v"], ve)
                    t_evt = p.tok("v")
                    out_dma_tok[tc] = p.dma(
                        "y", f"out{tc}",
                        out_h[col + tc * 128: col + tc * 128 + 128, :],
                        OT[:, tc, :], waits=[t_evt])
            def fin(e):
                return e.nop()
            p.add("y", fin,
                  waits=[(p.dsem[f"out{tc}"], p.dma_cnt[f"out{tc}"])
                         for tc in range(4)])

        with nc.Block() as block:
            p.emit(block)
    return nc


SMALL_SEGS = [("eps", 1), ("e9", 1),
              ("w", 8), ("u", 8), ("tmk", 8), ("tmv", 8), ("tmr", 8),
              ("lng0", 8), ("lnb0", 8), ("lng1", 8), ("lnb1", 8),
              ("b2_0", 8), ("b2_1", 8), ("qs0", 1), ("ks0", 1),
              ("qs1", 1), ("ks1", 1), ("qsw0", 1), ("ksw0", 1),
              ("qsw1", 1), ("ksw1", 1), ("b1i0", NSLOT), ("b1i1", NSLOT)]
SMALL_OFF = {}
_c = 0
for _nm, _w in SMALL_SEGS:
    SMALL_OFF[_nm] = (_c, _w)
    _c += _w
SMALL_COLS = _c


def _feat(v):
    return np.ascontiguousarray(v.reshape(8, 128).T, np.float32)


def _small_params(inp):
    sm = np.zeros((128, SMALL_COLS), np.float32)

    def put(nm, arr):
        o, w = SMALL_OFF[nm]
        sm[:, o:o + w] = arr.reshape(128, w)

    put("eps", np.full((128, 1), 1e-5, np.float32))
    put("e9", np.full((128, 1), 1e-9, np.float32))
    put("w", _feat(inp["L0_attn_time_decay"]))
    put("u", _feat(inp["L0_attn_time_first"]))
    put("tmk", _feat(inp["L0_attn_time_mix_k"].reshape(-1)))
    put("tmv", _feat(inp["L0_attn_time_mix_v"].reshape(-1)))
    put("tmr", _feat(inp["L0_attn_time_mix_r"].reshape(-1)))
    for l in range(2):
        put(f"lng{l}", _feat(inp["L0_ffn_ln_g"][l]))
        put(f"lnb{l}", _feat(inp["L0_ffn_ln_b"][l]))
        put(f"b2_{l}", _feat(inp["L0_ffn_b2"][l]))
        qs = inp["L0_ffn_q_scale"][l]
        ks = inp["L0_ffn_k_scale"][l]
        du = np.arange(128) % 64
        put(f"qs{l}", qs[du].reshape(128, 1))
        put(f"ks{l}", ks[du].reshape(128, 1))
        put(f"qsw{l}", qs[(du + 32) % 64].reshape(128, 1))
        put(f"ksw{l}", ks[(du + 32) % 64].reshape(128, 1))
        b1 = inp["L0_ffn_b1"][l]
        b1i = np.zeros((NSLOT, 128), np.float32)
        for j in range(NJ):
            a = b1[128 * j:128 * (j + 1)]
            g = b1[HID + 128 * j:HID + 128 * (j + 1)]
            b1i[2 * j, :len(a)] = a
            b1i[2 * j + 1, :len(g)] = g
        put(f"b1i{l}", b1i.T)
    return sm


def _prep_weights(inp):
    d = {}
    d["wkT"] = np.ascontiguousarray(inp["L0_attn_W_k"].T, np.float32)
    d["wvT"] = np.ascontiguousarray(inp["L0_attn_W_v"].T, np.float32)
    d["wrT"] = np.ascontiguousarray(inp["L0_attn_W_r"].T, np.float32)
    d["woT"] = np.ascontiguousarray(inp["L0_attn_W_o"].T, np.float32)
    d["wqkv"] = np.ascontiguousarray(inp["L0_ffn_wqkv"], np.float32)
    perm = np.arange(1024)
    for o in range(1024):
        h, du = o // 64, o % 64
        perm[o] = h * 64 + (du + 32) % 64
    d["wqsw"] = np.ascontiguousarray(inp["L0_ffn_wqkv"][:, :, perm],
                                     np.float32)
    d["wof"] = np.ascontiguousarray(inp["L0_ffn_wo"], np.float32)
    w1 = inp["L0_ffn_w1"]
    w1i = np.zeros((2, D, NSLOT, 128), np.float32)
    for j in range(NJ):
        n = min(128, HID - 128 * j)
        w1i[:, :, 2 * j, :n] = w1[:, :, 128 * j:128 * j + n]
        w1i[:, :, 2 * j + 1, :n] = w1[:, :, HID + 128 * j:HID + 128 * j + n]
    d["w1i"] = w1i.reshape(2, D, NSLOT * 128)
    w2 = inp["L0_ffn_w2"]
    w2p = np.zeros((2, NJ * 128, D), np.float32)
    w2p[:, :HID, :] = w2
    d["w2p"] = w2p
    return d


def kernel(**inputs):
    x = np.ascontiguousarray(np.asarray(inputs["L0_attn_x"], np.float32))
    c32_np, segs32, cbf_np, segsbf, rot_np, cbf_res = _host_consts()
    c32_buf, _ = _pack(c32_np, segs32)
    cbf_buf, _ = _pack(cbf_np, segsbf)
    sm = _small_params(inputs)
    wts = _prep_weights(inputs)

    import os
    ncores = int(os.environ.get("K_NCORES", NCORES))
    nc = build(bass.Bass("TRN2", target_bir_lowering=False), BUILD_STAGE)
    in_maps = []
    for b in range(ncores):
        m = {"x": np.ascontiguousarray(x[b]),
             "cst32": c32_buf, "cstbf": cbf_buf, "rot": rot_np,
             "smallw": sm}
        m.update(wts)
        in_maps.append(m)
    res = run_bass_kernel_spmd(nc, in_maps, core_ids=list(range(ncores)))
    if os.environ.get("K_DUMP"):
        kernel.dbg = res.results
    out = np.stack([r["out"] for r in res.results], 0)
    if ncores < NCORES:
        out = np.concatenate([out] + [out[:1]] * (NCORES - ncores), 0)
    return out
